# version 09
# baseline (speedup 1.0000x reference)
import numpy as np
from contextlib import ExitStack

import concourse.bass as bass
import concourse.mybir as mybir
from concourse.bass_isa import ReduceOp
from concourse.library_config import attn as _q7lib
from concourse.bass_utils import run_bass_kernel_spmd

F32 = mybir.dt.float32
BF16 = mybir.dt.bfloat16
I32 = mybir.dt.int32
AF = mybir.ActivationFunctionType
OP = mybir.AluOpType
AX = mybir.AxisListType

TOK, DM, NS, HID = 4096, 2048, 1024, 8192
NCD, NCS, NCH = DM // 128, NS // 128, HID // 128
T1 = 512
NT1 = TOK // T1
TH = 2048
NTH = TOK // TH
P2TC = TH // 512
TG = 512
NG = TOK // TG
EPS = 1e-5
WN = float(DM * HID)

N_CORES = 8


class _NullInst:
    def then_inc(self, *a, **k):
        return self


class _NullEng:
    def __getattr__(self, k):
        return lambda *a, **kw: _NullInst()


NULL = _NullEng()


class Prog:

    def __init__(self):
        self.emit = False
        self.cnt = {}
        self.ck = {}

    def inc(self, name, n=1):
        self.cnt[name] = self.cnt.get(name, 0) + n

    def mark(self, key, name):
        v = self.cnt.get(name, 0)
        if self.emit:
            assert self.ck.get(key) == v, ("ck mismatch", key, self.ck.get(key), v)
        else:
            self.ck[key] = v

    def selfsync(self, eng, sem, name):
        if self.emit:
            eng.wait_ge(sem, self.cnt[name])

    def wait(self, eng, sem, key):
        v = self.ck.get(key, 0) if isinstance(key, tuple) or isinstance(key, str) else key
        if self.emit and v > 0:
            eng.wait_ge(sem, v)


def build_program(phases=(0, 1, 2, 3), dbg=False):
    nc = bass.Bass("TRN2", target_bir_lowering=False,
                   detect_race_conditions=False)

    x_fm = nc.dram_tensor("x_fm", [DM, TOK], F32, kind="ExternalInput").ap()
    btf = nc.dram_tensor("btf", [DM, NS], F32, kind="ExternalInput").ap()
    ctf = nc.dram_tensor("ctf", [NS, DM], F32, kind="ExternalInput").ap()
    dvec = nc.dram_tensor("dvec", [DM], F32, kind="ExternalInput").ap()
    w1t = nc.dram_tensor("w1t", [DM, HID], F32, kind="ExternalInput").ap()
    w2t = nc.dram_tensor("w2t", [HID, DM], F32, kind="ExternalInput").ap()
    eye = nc.dram_tensor("eye", [128, 128], F32, kind="ExternalInput").ap()
    out_tm = nc.dram_tensor("out_tm", [TOK, DM], F32, kind="ExternalOutput").ap()

    if dbg is True:
        dbg = ("btq_d", "ctq_d", "w1q_d", "w2q_d", "xq_d", "acc_d", "g_d", "scl_d")
    elif not dbg:
        dbg = ()
    def sk(n):
        return "ExternalOutput" if n in dbg else "Internal"
    btq_d = nc.dram_tensor("btq_d", [NCS, 128, NCD, 128], BF16, kind=sk("btq_d")).ap()
    ctq_d = nc.dram_tensor("ctq_d", [NCD, 128, NCS, 128], BF16, kind=sk("ctq_d")).ap()
    w1q_d = nc.dram_tensor("w1q_d", [NCH, 128, NCD, 128], BF16, kind=sk("w1q_d")).ap()
    w2q_d = nc.dram_tensor("w2q_d", [HID, DM], BF16, kind=sk("w2q_d")).ap()
    xq_d = nc.dram_tensor("xq_d", [DM, TOK], BF16, kind=sk("xq_d")).ap()
    acc_d = nc.dram_tensor("acc_d", [TOK, DM], F32, kind=sk("acc_d")).ap()
    g_d = nc.dram_tensor("g_d", [NTH, NCH, 128, TH], F32, kind=sk("g_d")).ap()
    scl_d = nc.dram_tensor("scl_d", [TOK], F32, kind=sk("scl_d")).ap()

    pr = Prog()
    PB = [0]

    with ExitStack() as es:
        def sem(name):
            return es.enter_context(nc.semaphore(name))

        c1_full = es.enter_context(nc.sbuf_tensor("c1_full", [128, TOK], F32))
        s2bc = es.enter_context(nc.sbuf_tensor("s2bc", [128, TOK], F32))
        c3col = es.enter_context(nc.sbuf_tensor("c3col", [128, 32], F32))
        sD = es.enter_context(nc.sbuf_tensor("sD", [128, NCD], F32))
        seye = es.enter_context(nc.sbuf_tensor("seye", [128, 128], F32))
        prow = es.enter_context(nc.sbuf_tensor("prow", [128, 64], F32))
        mw1 = prow[:, 0:1]
        mw2 = prow[:, 1:2]
        s1w = prow[:, 2:3]
        ns1w = prow[:, 3:4]
        acc1 = prow[:, 4:5]
        acc2 = prow[:, 5:6]
        redc = prow[:, 6:7]
        s2w = prow[:, 7:8]
        ns2w = prow[:, 8:9]
        mh2 = prow[:, 9:10]
        rsum = prow[:, 10:11]
        rtmp = prow[:, 11:12]
        racc = prow[:, 16:32]
        scancol = prow[:, 32:64]

        SEM = {n: sem(n) for n in [
            "pb", "p0misc",
            "p0ld0", "p0ld1", "p0v", "p0g", "p0st0", "p0st1", "p0s",
            "ldx0", "ldx1", "ldbt0", "ldbt1", "ldct0", "ldct1", "stxq0", "stxq1",
            "stacc", "v1", "s1", "t1", "g1",
            "ldw1_0", "ldw1_1", "ldxq", "stg0", "stg1", "stscl",
            "v2", "s2", "t2", "g2",
            "ldg0", "ldg1", "ldg2", "ldg3",
            "ldw2_0", "ldw2_1", "ldw2_2", "ldw2_3", "ldacc", "ldc3", "stout",
            "v3", "t3",
        ]}

        def barrier_end(E):
            if pr.emit:
                E.nop().then_inc(SEM["pb"], 1)
            pr.inc("pb", 1)

        def barrier_wait(E):
            v = PB[0]
            if pr.emit and v > 0:
                E.wait_ge(SEM["pb"], v)

        def phase(run, n_engines):
            snap_cnt = dict(pr.cnt)
            snap_pb = PB[0]
            run(None)
            pr.cnt = dict(snap_cnt)
            PB[0] = snap_pb
            pr.emit = True
            with nc.Block() as blk:
                run(blk)
            pr.emit = False
            PB[0] += n_engines

        with ExitStack() as p0:
            stgA = p0.enter_context(nc.sbuf_tensor("stgA", [128, HID], F32))
            stgB = p0.enter_context(nc.sbuf_tensor("stgB", [128, HID], F32))
            wqA = p0.enter_context(nc.sbuf_tensor("wqA", [128, HID], BF16))
            wqB = p0.enter_context(nc.sbuf_tensor("wqB", [128, HID], BF16))
            sa0 = p0.enter_context(nc.sbuf_tensor("sa0", [128, HID // 2], F32))
            sb0 = p0.enter_context(nc.sbuf_tensor("sb0", [128, HID // 2], F32))
            sabs = p0.enter_context(nc.sbuf_tensor("sabs", [128, HID], F32))
            rseg = p0.enter_context(nc.sbuf_tensor("rseg", [128, 128], F32))
            stg0 = [stgA, stgB]
            wq0 = [wqA, wqB]

            fills = ([("bt", c) for c in range(NCD)]
                     + [("ct", c) for c in range(NCS)]
                     + [("w1s", c) for c in range(NCD)]
                     + [("w2s", c) for c in range(16)]
                     + [("w2r", c) for c in range(16)]
                     + [("w1q", c) for c in range(NCD)]
                     + [("w2q", c) for c in range(16)])
            store_kinds = ("bt", "ct", "w1q", "w2q")
            n_stores = sum(1 for k, _ in fills if k in store_kinds)

            def fill_cols(kind):
                return {"bt": NS, "ct": DM, "w1s": HID, "w2s": HID,
                        "w2r": HID, "w1q": HID, "w2q": HID}[kind]

            def g0_sync(E):
                if pr.emit:
                    E.dma_start(out=sD[:], in_=dvec.rearrange("(c p) -> p c", p=128),
                                allow_slow_non_contiguous=True).then_inc(SEM["p0misc"], 16)
                    E.dma_start(out=seye[:], in_=eye).then_inc(SEM["p0misc"], 16)
                pr.inc("p0misc", 32)
                pr.mark("p0misc_all", "p0misc")
                for fi, (kind, c) in enumerate(fills):
                    if fi >= 2 and fills[fi - 2][0] in ("w1s", "w2s"):
                        pr.wait(E, SEM["p0s"], ("p0fill", fi - 2))
                    else:
                        pr.wait(E, SEM["p0v"], ("p0fill", fi - 2))
                    ncols = fill_cols(kind)
                    if pr.emit:
                        if kind == "bt":
                            src = btf[c * 128:(c + 1) * 128, :]
                        elif kind == "ct":
                            src = ctf[c * 128:(c + 1) * 128, :]
                        elif kind in ("w1s", "w1q"):
                            src = w1t[c * 128:(c + 1) * 128, :]
                        else:
                            src = (w2t[c * 512:(c + 1) * 512, :]
                                   .rearrange("(cc p) d -> p cc d", p=128))
                        dstp = stg0[fi % 2][:, 0:ncols]
                        if kind in ("w2s", "w2r", "w2q"):
                            dstp = dstp.rearrange("p (cc d) -> p cc d", cc=4)
                        E.dma_start(out=dstp, in_=src).then_inc(SEM[f"p0ld{fi % 2}"], 16)
                    pr.inc(f"p0ld{fi % 2}", 16)
                    pr.mark(("p0ld", fi), f"p0ld{fi % 2}")
                barrier_end(E)

            def g0_vec(E):
                if pr.emit:
                    E.memset(acc1, 0.0)
                    E.memset(acc2, 0.0).then_inc(SEM["p0v"], 2)
                pr.inc("p0v", 2)
                si = 0
                for fi, (kind, c) in enumerate(fills):
                    pr.wait(E, SEM[f"p0ld{fi % 2}"], ("p0ld", fi))
                    stg = stg0[fi % 2]
                    if kind in ("bt", "ct"):
                        ncols = fill_cols(kind)
                        pr.wait(E, SEM[f"p0st{si % 2}"], ("p0st", si - 2))
                        if pr.emit:
                            E.tensor_copy(out=wq0[si % 2][:, 0:ncols],
                                          in_=stg[:, 0:ncols]).then_inc(SEM["p0v"], 1)
                        pr.inc("p0v", 1)
                        pr.mark(("p0ready", si), "p0v")
                        si += 1
                    elif kind in ("w1s", "w2s"):
                        continue
                    elif kind == "w2r":
                        if c == 0:
                            pr.wait(E, SEM["p0s"], "scans_s")
                            if pr.emit:
                                E.reduce_sum(out=acc1, in_=scancol[:, 0:16],
                                             axis=AX.X).then_inc(SEM["p0v"], 1)
                            pr.inc("p0v", 1)
                            pr.selfsync(E, SEM["p0v"], "p0v")
                            if pr.emit:
                                E.reduce_sum(out=acc2, in_=scancol[:, 16:32],
                                             axis=AX.X).then_inc(SEM["p0v"], 1)
                            pr.inc("p0v", 1)
                            pr.selfsync(E, SEM["p0v"], "p0v")
                            pr.mark("p0scans", "p0v")
                            pr.wait(E, SEM["p0g"], "p0par")
                            if pr.emit:
                                E.tensor_scalar_mul(out=mh2, in0=acc2,
                                                    scalar1=1.0 / WN)
                                E.memset(racc[:], 0.0).then_inc(SEM["p0v"], 2)
                            pr.inc("p0v", 2)
                            pr.selfsync(E, SEM["p0v"], "p0v")
                        pr.wait(E, SEM["p0s"], ("p0abs", c))
                        if pr.emit:
                            E.tensor_scalar(out=sabs[:], in0=sabs[:], scalar1=mh2,
                                            scalar2=None, op0=OP.subtract)
                            E.reduce_sum(out=rseg[:], in_=sabs[:]
                                         .rearrange("p (t s) -> p t s", s=64),
                                         axis=AX.X).then_inc(SEM["p0v"], 2)
                        pr.inc("p0v", 2)
                        pr.selfsync(E, SEM["p0v"], "p0v")
                        if pr.emit:
                            E.reduce_sum(out=redc, in_=rseg[:], axis=AX.X
                                         ).then_inc(SEM["p0v"], 1)
                        pr.inc("p0v", 1)
                        pr.selfsync(E, SEM["p0v"], "p0v")
                        if pr.emit:
                            E.tensor_add(out=racc[:, c:c + 1], in0=racc[:, c:c + 1],
                                         in1=redc).then_inc(SEM["p0v"], 1)
                        pr.inc("p0v", 1)
                        pr.selfsync(E, SEM["p0v"], "p0v")
                        pr.mark(("p0rescons", c), "p0v")
                    else:
                        if kind == "w1q" and c == 0:
                            if pr.emit:
                                E.reduce_sum(out=rsum, in_=racc[:], axis=AX.X
                                             ).then_inc(SEM["p0v"], 1)
                            pr.inc("p0v", 1)
                            pr.selfsync(E, SEM["p0v"], "p0v")
                            pr.mark("p0rsum_pre", "p0v")
                            pr.wait(E, SEM["p0g"], "p0par3")

                            def _rowop(fn):
                                if pr.emit:
                                    fn().then_inc(SEM["p0v"], 1)
                                pr.inc("p0v", 1)
                                pr.selfsync(E, SEM["p0v"], "p0v")

                            _rowop(lambda: E.tensor_scalar(
                                out=mw2, in0=rsum, scalar1=1.0 / WN,
                                scalar2=mh2, op0=OP.mult, op1=OP.add))
                            _rowop(lambda: E.tensor_scalar_max(
                                out=mw2, in0=mw2, scalar1=EPS))
                            _rowop(lambda: E.tensor_scalar(
                                out=mw1, in0=acc1, scalar1=1.0 / WN,
                                scalar2=EPS, op0=OP.mult, op1=OP.max))
                            for mm_, ss_, nn_ in ((mw1, s1w, ns1w),
                                                  (mw2, s2w, ns2w)):
                                _rowop(lambda mm_=mm_, ss_=ss_:
                                       E.reciprocal(out=ss_, in_=mm_))
                                _rowop(lambda mm_=mm_, ss_=ss_:
                                       E.tensor_mul(out=rtmp, in0=mm_, in1=ss_))
                                _rowop(lambda: E.tensor_scalar(
                                    out=rtmp, in0=rtmp, scalar1=-1.0,
                                    scalar2=1.0, op0=OP.mult, op1=OP.add))
                                _rowop(lambda ss_=ss_:
                                       E.tensor_mul(out=rtmp, in0=rtmp, in1=ss_))
                                _rowop(lambda ss_=ss_:
                                       E.tensor_add(out=ss_, in0=ss_, in1=rtmp))
                                _rowop(lambda ss_=ss_, nn_=nn_:
                                       E.tensor_scalar_mul(out=nn_, in0=ss_,
                                                           scalar1=-1.0))
                        sw = s1w if kind == "w1q" else s2w
                        pr.wait(E, SEM[f"p0st{si % 2}"], ("p0st", si - 2))
                        pr.wait(E, SEM["p0s"], ("p0qs", fi))
                        if pr.emit:
                            E.tensor_scalar(out=sabs[:], in0=sabs[:],
                                            scalar1=sw, scalar2=0.5,
                                            op0=OP.mult, op1=OP.is_gt
                                            ).then_inc(SEM["p0v"], 1)
                        pr.inc("p0v", 1)
                        for half in range(2):
                            hs = slice(half * (HID // 2), (half + 1) * (HID // 2))
                            sg = sa0 if half == 0 else sb0
                            if pr.emit:
                                E.tensor_mul(out=wq0[si % 2][:, hs],
                                             in0=sabs[:, hs], in1=sg[:]
                                             ).then_inc(SEM["p0v"], 1)
                            pr.inc("p0v", 1)
                        pr.mark(("p0ready", si), "p0v")
                        pr.mark(("p0qv", fi), "p0v")
                        si += 1
                    pr.mark(("p0fill", fi), "p0v")
                barrier_end(E)

            def g0_gp(E):
                if pr.emit:
                    E.load_library(_q7lib)
                si = 0
                for fi, (kind, c) in enumerate(fills):
                    if kind == "w2r" and c == 0:
                        pr.wait(E, SEM["p0v"], "p0scans")
                        if pr.emit:
                            E.partition_all_reduce(acc1, acc1, 128, ReduceOp.add)
                            E.partition_all_reduce(acc2, acc2, 128, ReduceOp.add
                                                   ).then_inc(SEM["p0g"], 2)
                        pr.inc("p0g", 2)
                        pr.mark("p0par", "p0g")
                    if kind == "w1q" and c == 0:
                        pr.wait(E, SEM["p0v"], "p0rsum_pre")
                        if pr.emit:
                            E.partition_all_reduce(rsum, rsum, 128, ReduceOp.add
                                                   ).then_inc(SEM["p0g"], 1)
                        pr.inc("p0g", 1)
                        pr.mark("p0par3", "p0g")
                    if kind not in store_kinds:
                        continue
                    pr.wait(E, SEM["p0v"], ("p0ready", si))
                    src = wq0[si % 2]
                    if kind == "bt":
                        dsts = [(btq_d[:, :, c, :].transpose([1, 0, 2]),
                                 src[:, 0:NS].rearrange("p (n m) -> p n m", m=128))]
                    elif kind == "ct":
                        dsts = [(ctq_d[j * 8:(j + 1) * 8, :, c, :].transpose([1, 0, 2]),
                                 src[:, j * 1024:(j + 1) * 1024]
                                 .rearrange("p (d m) -> p d m", m=128))
                                for j in range(2)]
                    elif kind == "w1q":
                        dsts = [(w1q_d[j * 8:(j + 1) * 8, :, c, :].transpose([1, 0, 2]),
                                 src[:, j * 1024:(j + 1) * 1024]
                                 .rearrange("p (h m) -> p h m", m=128))
                                for j in range(8)]
                    else:
                        dsts = [(w2q_d[c * 512:(c + 1) * 512, :]
                                 .rearrange("(cc p) d -> p cc d", p=128),
                                 src[:, 0:HID].rearrange("p (cc d) -> p cc d", cc=4))]
                    for dst, s_ in dsts:
                        if pr.emit:
                            E.dma_start(out=dst, in_=s_).then_inc(SEM[f"p0st{si % 2}"], 16)
                        pr.inc(f"p0st{si % 2}", 16)
                    pr.mark(("p0st", si), f"p0st{si % 2}")
                    si += 1
                pr.wait(E, SEM["p0st0"], ("p0st", n_stores - 2))
                pr.wait(E, SEM["p0st1"], ("p0st", n_stores - 1))
                barrier_end(E)

            def g0_scal(E):
                nsc = 0
                for fi, (kind, c) in enumerate(fills):
                    if kind in ("w1s", "w2s"):
                        pr.wait(E, SEM[f"p0ld{fi % 2}"], ("p0ld", fi))
                        if pr.emit:
                            E.activation(out=sabs[:], in_=stg0[fi % 2][:],
                                         func=AF.Abs,
                                         accum_out=scancol[:, nsc:nsc + 1]
                                         ).then_inc(SEM["p0s"], 1)
                        pr.inc("p0s", 1)
                        pr.mark(("p0fill", fi), "p0s")
                        nsc += 1
                        if kind == "w2s" and c == 15:
                            pr.mark("scans_s", "p0s")
                    elif kind == "w2r":
                        pr.wait(E, SEM[f"p0ld{fi % 2}"], ("p0ld", fi))
                        pr.wait(E, SEM["p0v"], ("p0rescons", c - 1))
                        if pr.emit:
                            E.activation(out=sabs[:], in_=stg0[fi % 2][:],
                                         func=AF.Abs).then_inc(SEM["p0s"], 1)
                        pr.inc("p0s", 1)
                        pr.mark(("p0abs", c), "p0s")
                    elif kind in ("w1q", "w2q"):
                        pr.wait(E, SEM[f"p0ld{fi % 2}"], ("p0ld", fi))
                        pr.wait(E, SEM["p0v"], ("p0qv", fi - 1))
                        pr.wait(E, SEM["p0v"], ("p0rescons", 15))
                        if pr.emit:
                            E.activation(out=sabs[:], in_=stg0[fi % 2][:],
                                         func=AF.Abs)
                            E.activation(out=sa0[:],
                                         in_=stg0[fi % 2][:, 0:HID // 2],
                                         func=AF.Sign)
                            E.activation(out=sb0[:],
                                         in_=stg0[fi % 2][:, HID // 2:HID],
                                         func=AF.Sign).then_inc(SEM["p0s"], 3)
                        pr.inc("p0s", 3)
                        pr.mark(("p0qs", fi), "p0s")
                barrier_end(E)

            def run_p0(block):
                if block is None:
                    g0_sync(NULL); g0_vec(NULL); g0_scal(NULL); g0_gp(NULL)
                else:
                    block.sync(lambda e: g0_sync(e))
                    block.vector(lambda e: g0_vec(e))
                    block.scalar(lambda e: g0_scal(e))
                    block.gpsimd(lambda e: g0_gp(e))

            if 0 in phases:
                phase(run_p0, 4)

        with ExitStack() as p1:
            sxb = [p1.enter_context(nc.sbuf_tensor(f"sx{j}", [128, NCD, T1], F32))
                   for j in range(2)]
            sxnb = p1.enter_context(nc.sbuf_tensor("sxnb", [128, NCD * T1], BF16))
            sxqd = [p1.enter_context(nc.sbuf_tensor(f"sxqd{j}", [128, T1], BF16))
                    for j in range(2)]
            acc_tm = [p1.enter_context(nc.sbuf_tensor(f"acctm{j}", [128, DM], F32))
                      for j in range(4)]
            btbuf = [p1.enter_context(nc.sbuf_tensor(f"btbuf{j}", [128, NCD * 128], BF16))
                     for j in range(2)]
            ctbuf = [p1.enter_context(nc.sbuf_tensor(f"ctbuf{j}", [128, NCS * 128], BF16))
                     for j in range(2)]
            st_bf = p1.enter_context(nc.sbuf_tensor("st_bf", [128, NCS * T1], BF16))
            accf = [p1.enter_context(nc.sbuf_tensor(f"accf{j}", [128, T1], F32))
                    for j in range(2)]
            rows = p1.enter_context(nc.sbuf_tensor("rows1", [128, 11, T1], F32))
            (r_sums, r_sumsq, r_xmax, r_nxmin, r_m0, r_m1, r_rstd0, r_rstd1,
             r_sxs, r_A, r_B) = [rows[:, j, :] for j in range(11)]
            r_C = r_B
            sqc = [p1.enter_context(nc.sbuf_tensor(f"sqc{j}", [128, T1], F32))
                   for j in range(2)]
            xnscr = [p1.enter_context(nc.sbuf_tensor(f"xnscr{j}", [128, T1], F32))
                     for j in range(2)]
            qfscr = p1.enter_context(nc.sbuf_tensor("qfscr", [128, T1], F32))
            qiscr = p1.enter_context(nc.sbuf_tensor("qiscr", [128, T1], I32))
            xnev = p1.enter_context(nc.sbuf_tensor("xnev", [128, T1], F32))
            psum1 = [p1.enter_context(nc.psum_tensor(f"ps1{j}", [128, T1], F32))
                     for j in range(2)]
            psum2 = [p1.enter_context(nc.psum_tensor(f"ps2{j}", [128, T1], F32))
                     for j in range(2)]
            psumtr_full = [p1.enter_context(nc.psum_tensor(f"pstr{j}", [128, 512], F32))
                           for j in range(2)]
            psumtr = [t[:, 0:128] for t in psumtr_full]

            def w2k(i, j, J):
                return (i, j) if j >= 0 else (i - 1, J + j)

            def _p1_xload(E, i):
                if i >= NT1:
                    return
                pr.wait(E, SEM["v1"], ("xdone", i - 2))
                for hh in range(2):
                    if pr.emit:
                        src = (x_fm.rearrange("(c p) t -> p c t", p=128)
                               [:, hh * 8:(hh + 1) * 8, i * T1:(i + 1) * T1])
                        E.dma_start(out=sxb[i % 2][:, hh * 8:(hh + 1) * 8, :],
                                    in_=src).then_inc(SEM[f"ldx{i % 2}"], 16)
                    pr.inc(f"ldx{i % 2}", 16)
                pr.mark(("ldx", i), f"ldx{i % 2}")

            def g1_sync(E):
                barrier_wait(E)
                _p1_xload(E, 0)
                _p1_xload(E, 1)
                for i in range(NT1):
                    for n in range(NCS):
                        pr.wait(E, SEM["t1"], ("mm1grp",) + w2k(i, n - 2, NCS))
                        if pr.emit:
                            E.dma_start(out=btbuf[n % 2][:],
                                        in_=btq_d[n].rearrange("p k m -> p (k m)")
                                        ).then_inc(SEM[f"ldbt{n % 2}"], 16)
                        pr.inc(f"ldbt{n % 2}", 16)
                        pr.mark(("ldbt", i, n), f"ldbt{n % 2}")
                    for dc in range(NCD):
                        pr.wait(E, SEM["t1"], ("mm2grp",) + w2k(i, dc - 2, NCD))
                        if pr.emit:
                            E.dma_start(out=ctbuf[dc % 2][:],
                                        in_=ctq_d[dc].rearrange("p s m -> p (s m)")
                                        ).then_inc(SEM[f"ldct{dc % 2}"], 16)
                        pr.inc(f"ldct{dc % 2}", 16)
                        pr.mark(("ldct", i, dc), f"ldct{dc % 2}")
                    _p1_xload(E, i + 2)
                barrier_end(E)

            def _v_front(E, i):
                sx = sxb[i % 2]
                r_m = r_m0 if i % 2 == 0 else r_m1
                r_rstd = r_rstd0 if i % 2 == 0 else r_rstd1
                pr.wait(E, SEM[f"ldx{i % 2}"], ("ldx", i))
                if pr.emit:
                    E.reduce_sum(out=r_sums, in_=sx[:].transpose([0, 2, 1]),
                                 axis=AX.X)
                    E.reduce_max(out=r_xmax, in_=sx[:].transpose([0, 2, 1]),
                                 axis=AX.X)
                    E.tensor_reduce(out=r_C, in_=sx[:].transpose([0, 2, 1]),
                                    axis=AX.X, op=OP.min)
                    E.tensor_scalar_mul(out=r_nxmin, in0=r_C, scalar1=-1.0
                                        ).then_inc(SEM["v1"], 4)
                pr.inc("v1", 4)
                for c in range(NCD):
                    pr.wait(E, SEM["s1"], ("sq", i, c))
                    if pr.emit:
                        if c == 0:
                            E.tensor_copy(out=r_sumsq, in_=sqc[0][:]
                                          ).then_inc(SEM["v1"], 1)
                        else:
                            E.tensor_add(out=r_sumsq, in0=r_sumsq, in1=sqc[c % 2][:]
                                         ).then_inc(SEM["v1"], 1)
                    pr.inc("v1", 1)
                    pr.mark(("sqdone", i, c), "v1")
                pr.mark(("stats", i), "v1")
                pr.wait(E, SEM["g1"], ("par", i))
                if pr.emit:
                    E.tensor_scalar_mul(out=r_m, in0=r_sums, scalar1=1.0 / DM)
                    E.tensor_scalar_mul(out=r_A, in0=r_sumsq, scalar1=1.0 / DM)
                    E.tensor_mul(out=r_B, in0=r_m, in1=r_m)
                    E.tensor_sub(out=r_A, in0=r_A, in1=r_B)
                    E.tensor_scalar_add(out=r_A, in0=r_A, scalar1=EPS
                                        ).then_inc(SEM["v1"], 4)
                pr.inc("v1", 4)
                pr.mark(("var", i), "v1")
                pr.wait(E, SEM["s1"], ("sd", i))
                if pr.emit:
                    E.reciprocal(out=r_rstd, in_=r_B)
                    E.tensor_sub(out=r_xmax, in0=r_xmax, in1=r_m)
                    E.tensor_add(out=r_nxmin, in0=r_nxmin, in1=r_m)
                    E.tensor_max(out=r_xmax, in0=r_xmax, in1=r_nxmin)
                    E.tensor_mul(out=r_xmax, in0=r_xmax, in1=r_rstd)
                    E.tensor_scalar_max(out=r_xmax, in0=r_xmax, scalar1=EPS)
                    E.reciprocal(out=r_B, in_=r_xmax)
                    E.tensor_scalar_mul(out=r_sxs, in0=r_B, scalar1=127.0)
                    E.tensor_scalar(out=c1_full[:, i * T1:(i + 1) * T1],
                                    in0=r_xmax, scalar1=mw1, scalar2=1.0 / 127.0,
                                    op0=OP.mult, op1=OP.mult
                                    ).then_inc(SEM["v1"], 9)
                pr.inc("v1", 9)
                for c in range(NCD):
                    _k = w2k(i, c - 2, NCD)
                    pr.wait(E, SEM[f"stxq{c % 2}"], ("stxq",) + _k)
                    pr.wait(E, SEM["s1"], ("xnb",) + w2k(i, c - 2, NCD))
                    if pr.emit:
                        E.tensor_sub(out=xnscr[c % 2][:], in0=sx[:, c, :], in1=r_m)
                        E.tensor_mul(out=xnscr[c % 2][:], in0=xnscr[c % 2][:],
                                     in1=r_rstd).then_inc(SEM["v1"], 2)
                    pr.inc("v1", 2)
                    pr.mark(("xn", i, c), "v1")
                    if pr.emit:
                        E.tensor_mul(out=qfscr[:], in0=xnscr[c % 2][:], in1=r_sxs)
                        E.tensor_copy(out=qiscr[:], in_=qfscr[:])
                        E.tensor_scalar(out=sxqd[c % 2][:], in0=qiscr[:],
                                        scalar1=0.0, scalar2=None, op0=OP.add
                                        ).then_inc(SEM["v1"], 3)
                    pr.inc("v1", 3)
                    pr.mark(("xq", i, c), "v1")

            def _v_evac(E, i):
                sx = sxb[i % 2]
                r_m = r_m0 if i % 2 == 0 else r_m1
                r_rstd = r_rstd0 if i % 2 == 0 else r_rstd1
                for dc in range(NCD):
                    pr.wait(E, SEM["t1"], ("mm2grp", i, dc))
                    pr.wait(E, SEM["t1"], ("tr",) + w2k(i, dc - 2, NCD))
                    if pr.emit:
                        E.tensor_sub(out=xnev[:], in0=sx[:, dc, :], in1=r_m)
                        E.tensor_mul(out=xnev[:], in0=xnev[:], in1=r_rstd)
                        E.scalar_tensor_tensor(out=xnev[:], in0=xnev[:],
                                               scalar=sD[:, dc:dc + 1],
                                               in1=sx[:, dc, :],
                                               op0=OP.mult, op1=OP.add)
                        E.tensor_add(out=accf[dc % 2][:], in0=xnev[:],
                                     in1=psum2[dc % 2][:]).then_inc(SEM["v1"], 4)
                    pr.inc("v1", 4)
                    pr.mark(("accf", i, dc), "v1")
                pr.mark(("xdone", i), "v1")

            def g1_vec(E):
                barrier_wait(E)
                pr.wait(E, SEM["p0misc"], "p0misc_all")
                for i in range(NT1):
                    _v_front(E, i)
                    if i >= 1:
                        _v_evac(E, i - 1)
                _v_evac(E, NT1 - 1)
                barrier_end(E)

            def _s_sq(E, i):
                sx = sxb[i % 2]
                for c in range(NCD):
                    pr.wait(E, SEM[f"ldx{i % 2}"], ("ldx", i))
                    pr.wait(E, SEM["v1"], ("sqdone",) + w2k(i, c - 2, NCD))
                    if pr.emit:
                        E.activation(out=sqc[c % 2][:], in_=sx[:, c, :],
                                     func=AF.Square).then_inc(SEM["s1"], 1)
                    pr.inc("s1", 1)
                    pr.mark(("sq", i, c), "s1")
                pr.wait(E, SEM["v1"], ("var", i))
                if pr.emit:
                    E.activation(out=r_B, in_=r_A, func=AF.Sqrt
                                 ).then_inc(SEM["s1"], 1)
                pr.inc("s1", 1)
                pr.mark(("sd", i), "s1")

            def _s_tanh(E, i):
                for n in range(NCS):
                    pr.wait(E, SEM["t1"], ("mm1grp", i, n))
                    if n == 0:
                        pr.wait(E, SEM["t1"], ("mm2grp", i - 1, NCD - 1))
                    if pr.emit:
                        E.activation(out=st_bf[:, n * T1:(n + 1) * T1],
                                     in_=psum1[n % 2][:], func=AF.Tanh
                                     ).then_inc(SEM["s1"], 1)
                    pr.inc("s1", 1)
                    pr.mark(("tanh", i, n), "s1")

            def _s_xnb(E, i):
                for c in range(NCD):
                    pr.wait(E, SEM["v1"], ("xn", i, c))
                    if c == 0:
                        pr.wait(E, SEM["t1"], ("mm1grp", i - 1, NCS - 1))
                    if pr.emit:
                        E.activation(out=sxnb[:, c * T1:(c + 1) * T1],
                                     in_=xnscr[c % 2][:], func=AF.Copy
                                     ).then_inc(SEM["s1"], 1)
                    pr.inc("s1", 1)
                    pr.mark(("xnb", i, c), "s1")

            def _s_evacs(E, i):
                pr.wait(E, SEM["stacc"], ("stacc", i - 1))
                for dc in range(NCD):
                    for tc in range(4):
                        lin = (i * NCD + dc) * 4 + tc
                        pr.wait(E, SEM["t1"], ("trtc", lin))
                        if pr.emit:
                            E.copy(out=acc_tm[tc][:, dc * 128:(dc + 1) * 128],
                                   in_=psumtr[tc % 2]).then_inc(SEM["s1"], 1)
                        pr.inc("s1", 1)
                        pr.mark(("sevtc", lin), "s1")
                pr.mark(("sevac_all", i), "s1")

            def g1_scal(E):
                barrier_wait(E)
                for i in range(NT1):
                    _s_sq(E, i)
                    if i >= 1:
                        _s_tanh(E, i - 1)
                    _s_xnb(E, i)
                    if i >= 1:
                        _s_evacs(E, i - 1)
                _s_tanh(E, NT1 - 1)
                _s_evacs(E, NT1 - 1)
                barrier_end(E)

            def _tr_group(E, i, dc):
                pr.wait(E, SEM["v1"], ("accf", i, dc))
                for tc in range(4):
                    lin = (i * NCD + dc) * 4 + tc
                    pr.wait(E, SEM["s1"], ("sevtc", lin - 2))
                    if pr.emit:
                        E.transpose(psumtr[tc % 2],
                                    accf[dc % 2][:, tc * 128:(tc + 1) * 128],
                                    seye[:]).then_inc(SEM["t1"], 1)
                    pr.inc("t1", 1)
                    pr.mark(("trtc", lin), "t1")
                pr.mark(("tr", i, dc), "t1")

            def g1_tens(E):
                barrier_wait(E)
                pr.wait(E, SEM["p0misc"], "p0misc_all")
                for i in range(NT1):
                    pr.wait(E, SEM["s1"], ("xnb", i, NCD - 1))
                    for n in range(NCS):
                        pr.wait(E, SEM[f"ldbt{n % 2}"], ("ldbt", i, n))
                        pr.wait(E, SEM["s1"], ("tanh",) + w2k(i, n - 2, NCS))
                        for kc in range(NCD):
                            if pr.emit:
                                mm = E.matmul(psum1[n % 2][:],
                                              btbuf[n % 2][:, kc * 128:(kc + 1) * 128],
                                              sxnb[:, kc * T1:(kc + 1) * T1],
                                              start=(kc == 0), stop=(kc == NCD - 1))
                                if kc == NCD - 1:
                                    mm.then_inc(SEM["t1"], 1)
                        pr.inc("t1", 1)
                        pr.mark(("mm1grp", i, n), "t1")
                    pr.wait(E, SEM["s1"], ("tanh", i, NCS - 1))
                    for dc in range(NCD):
                        pr.wait(E, SEM[f"ldct{dc % 2}"], ("ldct", i, dc))
                        pr.wait(E, SEM["v1"], ("accf",) + w2k(i, dc - 2, NCD))
                        for sc in range(NCS):
                            if pr.emit:
                                mm = E.matmul(psum2[dc % 2][:],
                                              ctbuf[dc % 2][:, sc * 128:(sc + 1) * 128],
                                              st_bf[:, sc * T1:(sc + 1) * T1],
                                              start=(sc == 0), stop=(sc == NCS - 1))
                                if sc == NCS - 1:
                                    mm.then_inc(SEM["t1"], 1)
                        pr.inc("t1", 1)
                        pr.mark(("mm2grp", i, dc), "t1")
                        if dc >= 1:
                            _tr_group(E, i, dc - 1)
                    _tr_group(E, i, NCD - 1)
                barrier_end(E)

            def g1_gp(E):
                barrier_wait(E)
                for i in range(NT1):
                    pr.wait(E, SEM["v1"], ("stats", i))
                    if pr.emit:
                        E.partition_all_reduce(rows[:, 0:2, :].rearrange("p a t -> p (a t)"),
                                               rows[:, 0:2, :].rearrange("p a t -> p (a t)"),
                                               128, ReduceOp.add)
                        E.partition_all_reduce(rows[:, 2:4, :].rearrange("p a t -> p (a t)"),
                                               rows[:, 2:4, :].rearrange("p a t -> p (a t)"),
                                               128, ReduceOp.max
                                               ).then_inc(SEM["g1"], 4)
                    pr.inc("g1", 4)
                    pr.mark(("par", i), "g1")
                    for c in range(NCD):
                        pr.wait(E, SEM["v1"], ("xq", i, c))
                        if pr.emit:
                            E.dma_start(out=xq_d[c * 128:(c + 1) * 128,
                                                 i * T1:(i + 1) * T1],
                                        in_=sxqd[c % 2][:]
                                        ).then_inc(SEM[f"stxq{c % 2}"], 16)
                        pr.inc(f"stxq{c % 2}", 16)
                        pr.mark(("stxq", i, c), f"stxq{c % 2}")
                    if i >= 1:
                        _gp_acc(E, i - 1)
                _gp_acc(E, NT1 - 1)
                pr.wait(E, SEM["stxq0"], ("stxq", NT1 - 1, NCD - 2))
                pr.wait(E, SEM["stxq1"], ("stxq", NT1 - 1, NCD - 1))
                pr.wait(E, SEM["stacc"], ("stacc", NT1 - 1))
                barrier_end(E)

            def _gp_acc(E, i):
                pr.wait(E, SEM["s1"], ("sevac_all", i))
                for tc in range(4):
                    if pr.emit:
                        E.dma_start(out=acc_d[i * T1 + tc * 128:
                                              i * T1 + (tc + 1) * 128, :],
                                    in_=acc_tm[tc][:]).then_inc(SEM["stacc"], 16)
                    pr.inc("stacc", 16)
                pr.mark(("stacc", i), "stacc")

            def run_p1(block):
                if block is None:
                    g1_vec(NULL); g1_scal(NULL); g1_tens(NULL)
                    g1_gp(NULL); g1_sync(NULL)
                else:
                    block.vector(lambda e: g1_vec(e))
                    block.scalar(lambda e: g1_scal(e))
                    block.tensor(lambda e: g1_tens(e))
                    block.gpsimd(lambda e: g1_gp(e))
                    block.sync(lambda e: g1_sync(e))

            if 1 in phases:
                phase(run_p1, 5)

        with ExitStack() as p2:
            sxqh = p2.enter_context(nc.sbuf_tensor("sxqh", [128, NCD, TH], BF16))
            w1buf = [p2.enter_context(nc.sbuf_tensor(f"w1b{j}", [128, NCD * 128], BF16))
                     for j in range(2)]
            gtmp = [p2.enter_context(nc.sbuf_tensor(f"gtmp{j}", [128, TH], F32))
                    for j in range(2)]
            gbuf = [p2.enter_context(nc.sbuf_tensor(f"gbuf{j}", [128, TH], F32))
                    for j in range(2)]
            gabs = [p2.enter_context(nc.sbuf_tensor(f"gabs{j}", [128, TH], F32))
                    for j in range(2)]
            macc = p2.enter_context(nc.sbuf_tensor("macc", [128, TH], F32))
            c3row = p2.enter_context(nc.sbuf_tensor("c3row", [128, TH], F32))
            p2ps = [p2.enter_context(nc.psum_tensor(f"p2ps{j}", [128, TH], F32))
                    for j in range(2)]

            def g2_sync(E):
                barrier_wait(E)
                for h in range(NTH):
                    pr.wait(E, SEM["t2"], ("mmh", h * NCH - 1))
                    for hh in range(2):
                        if pr.emit:
                            src = (xq_d[:, h * TH:(h + 1) * TH]
                                   .rearrange("(c p) t -> p c t", p=128)
                                   [:, hh * 8:(hh + 1) * 8, :])
                            E.dma_start(out=sxqh[:, hh * 8:(hh + 1) * 8, :], in_=src
                                        ).then_inc(SEM["ldxq"], 16)
                        pr.inc("ldxq", 16)
                    pr.mark(("ldxq", h), "ldxq")
                    for hc in range(NCH):
                        k = h * NCH + hc
                        pr.wait(E, SEM["t2"], ("mmh", k - 2))
                        if pr.emit:
                            E.dma_start(out=w1buf[k % 2][:],
                                        in_=w1q_d[hc].rearrange("p k m -> p (k m)")
                                        ).then_inc(SEM[f"ldw1_{k % 2}"], 16)
                        pr.inc(f"ldw1_{k % 2}", 16)
                        pr.mark(("ldw1", k), f"ldw1_{k % 2}")
                barrier_end(E)

            def g2_tens(E):
                barrier_wait(E)
                for h in range(NTH):
                    pr.wait(E, SEM["ldxq"], ("ldxq", h))
                    for hc in range(NCH):
                        k = h * NCH + hc
                        pr.wait(E, SEM[f"ldw1_{k % 2}"], ("ldw1", k))
                        pr.wait(E, SEM["v2"], ("evacmul", k - 2))
                        for kc in range(NCD):
                            for tc in range(P2TC):
                                if pr.emit:
                                    mm = E.matmul(
                                        p2ps[k % 2][:, tc * 512:(tc + 1) * 512],
                                        w1buf[k % 2][:, kc * 128:(kc + 1) * 128],
                                        sxqh[:, kc, tc * 512:(tc + 1) * 512],
                                        start=(kc == 0), stop=(kc == NCD - 1),
                                        skip_group_check=True)
                                    if kc == NCD - 1 and tc == P2TC - 1:
                                        mm.then_inc(SEM["t2"], 1)
                        pr.inc("t2", 1)
                        pr.mark(("mmh", k), "t2")
                barrier_end(E)

            def _macc(E, k):
                pr.wait(E, SEM["s2"], ("absd", k))
                if pr.emit:
                    E.tensor_max(out=macc[:], in0=macc[:], in1=gabs[k % 2][:]
                                 ).then_inc(SEM["v2"], 1)
                pr.inc("v2", 1)
                pr.mark(("macc", k), "v2")

            def g2_vec(E):
                barrier_wait(E)
                for h in range(NTH):
                    if pr.emit:
                        E.memset(macc[:], 0.0).then_inc(SEM["v2"], 1)
                    pr.inc("v2", 1)
                    for hc in range(NCH):
                        k = h * NCH + hc
                        pr.wait(E, SEM["t2"], ("mmh", k))
                        pr.wait(E, SEM["s2"], ("gelu", k - 2))
                        for tc in range(P2TC):
                            if pr.emit:
                                mm = E.tensor_mul(
                                    out=gtmp[k % 2][:, tc * 512:(tc + 1) * 512],
                                    in0=p2ps[k % 2][:, tc * 512:(tc + 1) * 512],
                                    in1=c1_full[:, h * TH + tc * 512:
                                                h * TH + (tc + 1) * 512])
                                if tc == P2TC - 1:
                                    mm.then_inc(SEM["v2"], 1)
                        pr.inc("v2", 1)
                        pr.mark(("evacmul", k), "v2")
                        if hc >= 1:
                            _macc(E, k - 1)
                    _macc(E, h * NCH + NCH - 1)
                    pr.mark(("maccdone", h), "v2")
                    pr.wait(E, SEM["g2"], ("parh", h))
                    if pr.emit:
                        E.tensor_scalar_max(out=macc[:], in0=macc[:], scalar1=EPS)
                        E.reciprocal(out=c3row[:], in_=macc[:])
                        E.tensor_scalar_mul(out=s2bc[:, h * TH:(h + 1) * TH],
                                            in0=c3row[:], scalar1=127.0)
                        E.tensor_scalar(out=c3row[:], in0=macc[:], scalar1=mw2,
                                        scalar2=1.0 / 127.0, op0=OP.mult,
                                        op1=OP.mult).then_inc(SEM["v2"], 4)
                    pr.inc("v2", 4)
                    pr.mark(("c3row", h), "v2")
                barrier_end(E)

            def g2_scal(E):
                barrier_wait(E)
                for h in range(NTH):
                    for hc in range(NCH):
                        k = h * NCH + hc
                        pr.wait(E, SEM["v2"], ("evacmul", k))
                        pr.wait(E, SEM[f"stg{k % 2}"], ("stg", k - 2))
                        if pr.emit:
                            E.activation(out=gbuf[k % 2][:], in_=gtmp[k % 2][:],
                                         func=AF.Gelu).then_inc(SEM["s2"], 1)
                        pr.inc("s2", 1)
                        pr.mark(("gelu", k), "s2")
                        pr.wait(E, SEM["v2"], ("macc", k - 2))
                        if pr.emit:
                            E.activation(out=gabs[k % 2][:], in_=gbuf[k % 2][:],
                                         func=AF.Abs).then_inc(SEM["s2"], 1)
                        pr.inc("s2", 1)
                        pr.mark(("absd", k), "s2")
                barrier_end(E)

            def g2_gp(E):
                barrier_wait(E)
                for h in range(NTH):
                    for hc in range(NCH):
                        k = h * NCH + hc
                        pr.wait(E, SEM["s2"], ("gelu", k))
                        if pr.emit:
                            E.dma_start(out=g_d[h, hc], in_=gbuf[k % 2][:]
                                        ).then_inc(SEM[f"stg{k % 2}"], 16)
                        pr.inc(f"stg{k % 2}", 16)
                        pr.mark(("stg", k), f"stg{k % 2}")
                    pr.wait(E, SEM["v2"], ("maccdone", h))
                    if pr.emit:
                        E.partition_all_reduce(macc[:], macc[:], 128, ReduceOp.absmax
                                               ).then_inc(SEM["g2"], 1)
                    pr.inc("g2", 1)
                    pr.mark(("parh", h), "g2")
                    pr.wait(E, SEM["v2"], ("c3row", h))
                    if pr.emit:
                        E.dma_start(out=scl_d[h * TH:(h + 1) * TH],
                                    in_=c3row[0:1, :]).then_inc(SEM["stscl"], 16)
                    pr.inc("stscl", 16)
                pr.mark("stscl_all", "stscl")
                pr.wait(E, SEM["stscl"], "stscl_all")
                pr.wait(E, SEM["stg0"], ("stg", 2 * NCH - 2))
                pr.wait(E, SEM["stg1"], ("stg", 2 * NCH - 1))
                barrier_end(E)

            def run_p2(block):
                if block is None:
                    g2_vec(NULL); g2_scal(NULL); g2_tens(NULL)
                    g2_gp(NULL); g2_sync(NULL)
                else:
                    block.vector(lambda e: g2_vec(e))
                    block.scalar(lambda e: g2_scal(e))
                    block.tensor(lambda e: g2_tens(e))
                    block.gpsimd(lambda e: g2_gp(e))
                    block.sync(lambda e: g2_sync(e))

            if 2 in phases:
                phase(run_p2, 5)

        with ExitStack() as p3:
            hq_sb = p3.enter_context(nc.sbuf_tensor("hq_sb", [128, NCH * TG], BF16))
            w2buf = [p3.enter_context(nc.sbuf_tensor(f"w2b{j}", [128, 2, 1024], BF16))
                     for j in range(4)]
            gld = [p3.enter_context(nc.sbuf_tensor(f"gld{j}", [128, TG], F32))
                   for j in range(4)]
            accb = [p3.enter_context(nc.sbuf_tensor(f"accb{j}", [128, DM], F32))
                    for j in range(4)]
            outb = [p3.enter_context(nc.sbuf_tensor(f"outb{j}", [128, DM], F32))
                    for j in range(4)]
            q3f = p3.enter_context(nc.sbuf_tensor("q3f", [128, TG], F32))
            q3i = p3.enter_context(nc.sbuf_tensor("q3i", [128, TG], I32))
            p3ps = p3.enter_context(nc.psum_tensor("p3ps", [128, 4096], F32))

            def lin3(g, dh, hc):
                return (g * 2 + dh) * NCH + hc

            def unlin3(lin):
                g, r = divmod(lin, 2 * NCH)
                dh, hc = divmod(r, NCH)
                return (g, dh, hc)

            def g3_scal(E):
                barrier_wait(E)
                if pr.emit:
                    E.dma_start(out=c3col[:],
                                in_=scl_d.rearrange("(c p) -> p c", p=128),
                                allow_slow_non_contiguous=True
                                ).then_inc(SEM["ldc3"], 16)
                pr.inc("ldc3", 16)
                pr.mark("ldc3", "ldc3")
                for g in range(NG):
                    h, lt0, gt0 = g // 4, (g % 4) * TG, g * TG
                    for hc in range(NCH):
                        pr.wait(E, SEM["v3"],
                                ("hqv",) + ((g, hc - 4) if hc >= 4 else
                                            (g - 1, NCH + hc - 4)))
                        if pr.emit:
                            E.dma_start(out=gld[hc % 4][:],
                                        in_=g_d[h, hc][:, lt0:lt0 + TG]
                                        ).then_inc(SEM[f"ldg{hc % 4}"], 16)
                        pr.inc(f"ldg{hc % 4}", 16)
                        pr.mark(("ldg", g, hc), f"ldg{hc % 4}")
                    for tc in range(4):
                        pr.wait(E, SEM["v3"], ("outv", g - 1, tc))
                        if pr.emit:
                            E.dma_start(out=accb[tc][:],
                                        in_=acc_d[gt0 + tc * 128:
                                                  gt0 + (tc + 1) * 128, :]
                                        ).then_inc(SEM["ldacc"], 16)
                        pr.inc("ldacc", 16)
                        pr.mark(("ldacc", g, tc), "ldacc")
                barrier_end(E)

            def g3_sync(E):
                barrier_wait(E)
                for g in range(NG):
                    h, lt0, gt0 = g // 4, (g % 4) * TG, g * TG
                    for dh in range(2):
                        for hcp in range(NCH // 2):
                            li = (g * 2 + dh) * (NCH // 2) + hcp - 4
                            if li >= 0:
                                gg_, r_ = divmod(li, 2 * (NCH // 2))
                                dh_, hcp_ = divmod(r_, NCH // 2)
                                pr.wait(E, SEM["t3"], ("mm3", gg_, dh_, 2 * hcp_ + 1))
                            if pr.emit:
                                E.dma_start(
                                    out=w2buf[hcp % 4][:],
                                    in_=w2q_d[hcp * 256:(hcp + 1) * 256,
                                              dh * 1024:(dh + 1) * 1024]
                                    .rearrange("(c p) d -> p c d", p=128)
                                    ).then_inc(SEM[f"ldw2_{hcp % 4}"], 16)
                            pr.inc(f"ldw2_{hcp % 4}", 16)
                            pr.mark(("ldw2", g, dh, hcp), f"ldw2_{hcp % 4}")
                barrier_end(E)

            def _evac(E, g, dh):
                pr.wait(E, SEM["t3"], ("mm3", g, dh, NCH - 1))
                pr.wait(E, SEM["ldc3"], "ldc3")
                for tc in range(4):
                    pr.wait(E, SEM["ldacc"], ("ldacc", g, 3))
                    pr.wait(E, SEM["stout"], ("stout", g - 1))
                    for dcn in range(2):
                        sl = slice(dh * 1024 + dcn * 512,
                                   dh * 1024 + (dcn + 1) * 512)
                        if pr.emit:
                            E.scalar_tensor_tensor(
                                out=outb[tc][:, sl],
                                in0=p3ps[:, tc * 1024 + dcn * 512:
                                         tc * 1024 + (dcn + 1) * 512],
                                scalar=c3col[:, g * 4 + tc:g * 4 + tc + 1],
                                in1=accb[tc][:, sl], op0=OP.mult, op1=OP.add
                                ).then_inc(SEM["v3"], 1)
                        pr.inc("v3", 1)
                    if dh == 1:
                        pr.mark(("outv", g, tc), "v3")
                pr.mark(("evac", g, dh), "v3")

            def g3_vec(E):
                barrier_wait(E)
                for g in range(NG):
                    h, lt0 = g // 4, (g % 4) * TG
                    if g >= 1:
                        _evac(E, g - 1, 0)
                    for hc in range(NCH):
                        pr.wait(E, SEM[f"ldg{hc % 4}"], ("ldg", g, hc))
                        pr.wait(E, SEM["t3"], ("mm3", g - 1, 1, hc))
                        if pr.emit:
                            E.tensor_mul(out=q3f[:], in0=gld[hc % 4][:],
                                         in1=s2bc[:, h * TH + lt0:
                                                  h * TH + lt0 + TG])
                            E.tensor_copy(out=q3i[:], in_=q3f[:])
                            E.tensor_scalar(out=hq_sb[:, hc * TG:(hc + 1) * TG],
                                            in0=q3i[:], scalar1=0.0, scalar2=None,
                                            op0=OP.add).then_inc(SEM["v3"], 4)
                        pr.inc("v3", 4)
                        pr.mark(("hqv", g, hc), "v3")
                    if g >= 1:
                        _evac(E, g - 1, 1)
                _evac(E, NG - 1, 0)
                _evac(E, NG - 1, 1)
                barrier_end(E)

            def g3_tens(E):
                barrier_wait(E)
                for g in range(NG):
                    for dh in range(2):
                        for hc in range(NCH):
                            hcp = hc // 2
                            pr.wait(E, SEM[f"ldw2_{hcp % 4}"], ("ldw2", g, dh, hcp))
                            if dh == 0:
                                pr.wait(E, SEM["v3"], ("hqv", g, hc))
                                if hc == 0:
                                    pr.wait(E, SEM["v3"], ("evac", g - 1, 1))
                            elif hc == 0:
                                pr.wait(E, SEM["v3"], ("evac", g, 0))
                            for tc in range(4):
                                for dcn in range(2):
                                    if pr.emit:
                                        mm = E.matmul(
                                            p3ps[:, tc * 1024 + dcn * 512:
                                                 tc * 1024 + (dcn + 1) * 512],
                                            hq_sb[:, hc * TG + tc * 128:
                                                  hc * TG + (tc + 1) * 128],
                                            w2buf[(hc // 2) % 4][:, hc % 2,
                                                  dcn * 512:(dcn + 1) * 512],
                                            start=(hc == 0), stop=(hc == NCH - 1),
                                            skip_group_check=True)
                                        if tc == 3 and dcn == 1:
                                            mm.then_inc(SEM["t3"], 1)
                            pr.inc("t3", 1)
                            pr.mark(("mm3", g, dh, hc), "t3")
                barrier_end(E)

            def g3_gp(E):
                barrier_wait(E)
                for g in range(NG):
                    gt0 = g * TG
                    for tc in range(4):
                        pr.wait(E, SEM["v3"], ("outv", g, tc))
                        if pr.emit:
                            E.dma_start(out=out_tm[gt0 + tc * 128:
                                                   gt0 + (tc + 1) * 128, :],
                                        in_=outb[tc][:]).then_inc(SEM["stout"], 16)
                        pr.inc("stout", 16)
                    pr.mark(("stout", g), "stout")
                pr.wait(E, SEM["stout"], ("stout", NG - 1))
                barrier_end(E)

            def run_p3(block):
                if block is None:
                    g3_vec(NULL); g3_tens(NULL); g3_gp(NULL)
                    g3_scal(NULL); g3_sync(NULL)
                else:
                    block.vector(lambda e: g3_vec(e))
                    block.tensor(lambda e: g3_tens(e))
                    block.gpsimd(lambda e: g3_gp(e))
                    block.scalar(lambda e: g3_scal(e))
                    block.sync(lambda e: g3_sync(e))

            if 3 in phases:
                phase(run_p3, 5)

    mybir.codegen_inst_isa_subclasses(nc)
    return nc


_NC_CACHE = []


def kernel(L0_attn_x, L0_attn_B, L0_attn_C, L0_attn_D, L0_ffn_w1, L0_ffn_w2):
    x = np.asarray(L0_attn_x, dtype=np.float32)
    B = np.asarray(L0_attn_B, dtype=np.float32)
    C = np.asarray(L0_attn_C, dtype=np.float32)
    D = np.asarray(L0_attn_D, dtype=np.float32)
    w1 = np.asarray(L0_ffn_w1, dtype=np.float32)
    w2 = np.asarray(L0_ffn_w2, dtype=np.float32)

    btf = np.ascontiguousarray(B.T)
    ctf = np.ascontiguousarray(C.T)
    w1t = np.ascontiguousarray(w1.T)
    w2t = np.ascontiguousarray(w2.T)
    eyem = np.eye(128, dtype=np.float32)

    if not _NC_CACHE:
        _NC_CACHE.append(build_program())
    nc = _NC_CACHE[0]

    in_maps = [dict(x_fm=np.ascontiguousarray(x[i].T),
                    btf=btf, ctf=ctf, dvec=D, w1t=w1t, w2t=w2t, eye=eyem)
               for i in range(N_CORES)]
    res = run_bass_kernel_spmd(nc, in_maps, core_ids=list(range(N_CORES)))
    return np.stack([np.asarray(r["out_tm"]) for r in res.results], axis=0)


# version 10
# speedup vs baseline: 1.0499x; 1.0499x over previous
import numpy as np
from contextlib import ExitStack

import concourse.bass as bass
import concourse.mybir as mybir
from concourse.bass_isa import ReduceOp
from concourse.library_config import attn as _q7lib
from concourse.bass_utils import run_bass_kernel_spmd

F32 = mybir.dt.float32
BF16 = mybir.dt.bfloat16
I32 = mybir.dt.int32
AF = mybir.ActivationFunctionType
OP = mybir.AluOpType
AX = mybir.AxisListType

TOK, DM, NS, HID = 4096, 2048, 1024, 8192
NCD, NCS, NCH = DM // 128, NS // 128, HID // 128
T1 = 512
NT1 = TOK // T1
TH = 2048
NTH = TOK // TH
P2TC = TH // 512
TG = 512
NG = TOK // TG
EPS = 1e-5
WN = float(DM * HID)

N_CORES = 8


class _NullInst:
    def then_inc(self, *a, **k):
        return self


class _NullEng:
    def __getattr__(self, k):
        return lambda *a, **kw: _NullInst()


NULL = _NullEng()


class Prog:

    def __init__(self):
        self.emit = False
        self.cnt = {}
        self.ck = {}

    def inc(self, name, n=1):
        self.cnt[name] = self.cnt.get(name, 0) + n

    def mark(self, key, name):
        v = self.cnt.get(name, 0)
        if self.emit:
            assert self.ck.get(key) == v, ("ck mismatch", key, self.ck.get(key), v)
        else:
            self.ck[key] = v

    def selfsync(self, eng, sem, name):
        if self.emit:
            eng.wait_ge(sem, self.cnt[name])

    def wait(self, eng, sem, key):
        v = self.ck.get(key, 0) if isinstance(key, tuple) or isinstance(key, str) else key
        if self.emit and v > 0:
            eng.wait_ge(sem, v)


def build_program(phases=(0, 1, 2, 3), dbg=False):
    nc = bass.Bass("TRN2", target_bir_lowering=False,
                   detect_race_conditions=False)

    x_fm = nc.dram_tensor("x_fm", [DM, TOK], F32, kind="ExternalInput").ap()
    btf = nc.dram_tensor("btf", [DM, NS], F32, kind="ExternalInput").ap()
    ctf = nc.dram_tensor("ctf", [NS, DM], F32, kind="ExternalInput").ap()
    dvec = nc.dram_tensor("dvec", [DM], F32, kind="ExternalInput").ap()
    w1t = nc.dram_tensor("w1t", [DM, HID], F32, kind="ExternalInput").ap()
    w2t = nc.dram_tensor("w2t", [HID, DM], F32, kind="ExternalInput").ap()
    eye = nc.dram_tensor("eye", [128, 128], F32, kind="ExternalInput").ap()
    out_tm = nc.dram_tensor("out_tm", [TOK, DM], F32, kind="ExternalOutput").ap()

    if dbg is True:
        dbg = ("btq_d", "ctq_d", "w1q_d", "w2q_d", "xq_d", "acc_d", "g_d", "scl_d")
    elif not dbg:
        dbg = ()
    def sk(n):
        return "ExternalOutput" if n in dbg else "Internal"
    btq_d = nc.dram_tensor("btq_d", [NCS, 128, NCD, 128], BF16, kind=sk("btq_d")).ap()
    ctq_d = nc.dram_tensor("ctq_d", [NCD, 128, NCS, 128], BF16, kind=sk("ctq_d")).ap()
    w1q_d = nc.dram_tensor("w1q_d", [NCH, 128, NCD, 128], BF16, kind=sk("w1q_d")).ap()
    w2q_d = nc.dram_tensor("w2q_d", [HID, DM], BF16, kind=sk("w2q_d")).ap()
    xq_d = nc.dram_tensor("xq_d", [DM, TOK], BF16, kind=sk("xq_d")).ap()
    acc_d = nc.dram_tensor("acc_d", [TOK, DM], F32, kind=sk("acc_d")).ap()
    g_d = nc.dram_tensor("g_d", [NTH, NCH, 128, TH], F32, kind=sk("g_d")).ap()
    scl_d = nc.dram_tensor("scl_d", [TOK], F32, kind=sk("scl_d")).ap()

    pr = Prog()
    PB = [0]

    with ExitStack() as es:
        def sem(name):
            return es.enter_context(nc.semaphore(name))

        c1_full = es.enter_context(nc.sbuf_tensor("c1_full", [128, TOK], F32))
        s2bc = es.enter_context(nc.sbuf_tensor("s2bc", [128, TOK], F32))
        c3col = es.enter_context(nc.sbuf_tensor("c3col", [128, 32], F32))
        sD = es.enter_context(nc.sbuf_tensor("sD", [128, NCD], F32))
        seye = es.enter_context(nc.sbuf_tensor("seye", [128, 128], F32))
        prow = es.enter_context(nc.sbuf_tensor("prow", [128, 64], F32))
        mw1 = prow[:, 0:1]
        mw2 = prow[:, 1:2]
        s1w = prow[:, 2:3]
        ns1w = prow[:, 3:4]
        acc1 = prow[:, 4:5]
        acc2 = prow[:, 5:6]
        redc = prow[:, 6:7]
        s2w = prow[:, 7:8]
        ns2w = prow[:, 8:9]
        mh2 = prow[:, 9:10]
        rsum = prow[:, 10:11]
        rtmp = prow[:, 11:12]
        racc = prow[:, 16:32]
        scancol = prow[:, 32:64]

        SEM = {n: sem(n) for n in [
            "pb", "p0misc",
            "p0ld0", "p0ld1", "p0v", "p0g", "p0st0", "p0st1", "p0s",
            "ldx0", "ldx1", "ldbt0", "ldbt1", "ldct0", "ldct1", "stxq0", "stxq1",
            "stacc", "v1", "s1", "t1", "g1",
            "ldw1_0", "ldw1_1", "ldxq", "stg0", "stg1", "stscl",
            "v2", "s2", "t2", "g2",
            "ldg0", "ldg1", "ldg2", "ldg3",
            "ldw2_0", "ldw2_1", "ldw2_2", "ldw2_3", "ldacc", "ldc3", "stout",
            "v3", "t3",
        ]}

        def barrier_end(E):
            if pr.emit:
                E.nop().then_inc(SEM["pb"], 1)
            pr.inc("pb", 1)

        def barrier_wait(E):
            v = PB[0]
            if pr.emit and v > 0:
                E.wait_ge(SEM["pb"], v)

        def phase(run, n_engines):
            snap_cnt = dict(pr.cnt)
            snap_pb = PB[0]
            run(None)
            pr.cnt = dict(snap_cnt)
            PB[0] = snap_pb
            pr.emit = True
            with nc.Block() as blk:
                run(blk)
            pr.emit = False
            PB[0] += n_engines

        with ExitStack() as p0:
            stgA = p0.enter_context(nc.sbuf_tensor("stgA", [128, HID], F32))
            stgB = p0.enter_context(nc.sbuf_tensor("stgB", [128, HID], F32))
            wqA = p0.enter_context(nc.sbuf_tensor("wqA", [128, HID], BF16))
            wqB = p0.enter_context(nc.sbuf_tensor("wqB", [128, HID], BF16))
            sa0 = p0.enter_context(nc.sbuf_tensor("sa0", [128, HID // 2], F32))
            sb0 = p0.enter_context(nc.sbuf_tensor("sb0", [128, HID // 2], F32))
            sabs = p0.enter_context(nc.sbuf_tensor("sabs", [128, HID], F32))
            rseg = p0.enter_context(nc.sbuf_tensor("rseg", [128, 128], F32))
            stg0 = [stgA, stgB]
            wq0 = [wqA, wqB]

            fills = ([("bt", c) for c in range(NCD)]
                     + [("ct", c) for c in range(NCS)]
                     + [("w1s", c) for c in range(NCD)]
                     + [("w2s", c) for c in range(16)]
                     + [("w2r", c) for c in range(16)]
                     + [("w1q", c) for c in range(NCD)]
                     + [("w2q", c) for c in range(16)])
            store_kinds = ("bt", "ct", "w1q", "w2q")
            n_stores = sum(1 for k, _ in fills if k in store_kinds)

            def fill_cols(kind):
                return {"bt": NS, "ct": DM, "w1s": HID, "w2s": HID,
                        "w2r": HID, "w1q": HID, "w2q": HID}[kind]

            def g0_sync(E):
                if pr.emit:
                    E.dma_start(out=sD[:], in_=dvec.rearrange("(c p) -> p c", p=128),
                                allow_slow_non_contiguous=True).then_inc(SEM["p0misc"], 16)
                    E.dma_start(out=seye[:], in_=eye).then_inc(SEM["p0misc"], 16)
                pr.inc("p0misc", 32)
                pr.mark("p0misc_all", "p0misc")
                for fi, (kind, c) in enumerate(fills):
                    if fi >= 2 and fills[fi - 2][0] in ("w1s", "w2s"):
                        pr.wait(E, SEM["p0s"], ("p0fill", fi - 2))
                    elif fi >= 2 and fills[fi - 2][0] in ("w1q", "w2q"):
                        pr.wait(E, SEM["p0s"], ("p0fillq", fi - 2))
                    else:
                        pr.wait(E, SEM["p0v"], ("p0fill", fi - 2))
                    ncols = fill_cols(kind)
                    if pr.emit:
                        if kind == "bt":
                            src = btf[c * 128:(c + 1) * 128, :]
                        elif kind == "ct":
                            src = ctf[c * 128:(c + 1) * 128, :]
                        elif kind in ("w1s", "w1q"):
                            src = w1t[c * 128:(c + 1) * 128, :]
                        else:
                            src = (w2t[c * 512:(c + 1) * 512, :]
                                   .rearrange("(cc p) d -> p cc d", p=128))
                        dstp = stg0[fi % 2][:, 0:ncols]
                        if kind in ("w2s", "w2r", "w2q"):
                            dstp = dstp.rearrange("p (cc d) -> p cc d", cc=4)
                        E.dma_start(out=dstp, in_=src).then_inc(SEM[f"p0ld{fi % 2}"], 16)
                    pr.inc(f"p0ld{fi % 2}", 16)
                    pr.mark(("p0ld", fi), f"p0ld{fi % 2}")
                barrier_end(E)

            def g0_vec(E):
                if pr.emit:
                    E.memset(acc1, 0.0)
                    E.memset(acc2, 0.0).then_inc(SEM["p0v"], 2)
                pr.inc("p0v", 2)
                si = 0
                vq = 0
                for fi, (kind, c) in enumerate(fills):
                    pr.wait(E, SEM[f"p0ld{fi % 2}"], ("p0ld", fi))
                    stg = stg0[fi % 2]
                    if kind in ("bt", "ct"):
                        ncols = fill_cols(kind)
                        pr.wait(E, SEM[f"p0st{si % 2}"], ("p0st", si - 2))
                        if pr.emit:
                            E.tensor_copy(out=wq0[si % 2][:, 0:ncols],
                                          in_=stg[:, 0:ncols]).then_inc(SEM["p0v"], 1)
                        pr.inc("p0v", 1)
                        pr.mark(("p0ready", si), "p0v")
                        si += 1
                    elif kind in ("w1s", "w2s"):
                        continue
                    elif kind == "w2r":
                        if c == 0:
                            pr.wait(E, SEM["p0s"], "scans_s")
                            if pr.emit:
                                E.reduce_sum(out=acc1, in_=scancol[:, 0:16],
                                             axis=AX.X).then_inc(SEM["p0v"], 1)
                            pr.inc("p0v", 1)
                            pr.selfsync(E, SEM["p0v"], "p0v")
                            if pr.emit:
                                E.reduce_sum(out=acc2, in_=scancol[:, 16:32],
                                             axis=AX.X).then_inc(SEM["p0v"], 1)
                            pr.inc("p0v", 1)
                            pr.selfsync(E, SEM["p0v"], "p0v")
                            pr.mark("p0scans", "p0v")
                            pr.wait(E, SEM["p0g"], "p0par")
                            if pr.emit:
                                E.tensor_scalar_mul(out=mh2, in0=acc2,
                                                    scalar1=1.0 / WN)
                                E.memset(racc[:], 0.0).then_inc(SEM["p0v"], 2)
                            pr.inc("p0v", 2)
                            pr.selfsync(E, SEM["p0v"], "p0v")
                        pr.wait(E, SEM["p0s"], ("p0abs", c))
                        if pr.emit:
                            E.tensor_scalar(out=sabs[:], in0=sabs[:], scalar1=mh2,
                                            scalar2=None, op0=OP.subtract)
                            E.reduce_sum(out=rseg[:], in_=sabs[:]
                                         .rearrange("p (t s) -> p t s", s=64),
                                         axis=AX.X).then_inc(SEM["p0v"], 2)
                        pr.inc("p0v", 2)
                        pr.selfsync(E, SEM["p0v"], "p0v")
                        if pr.emit:
                            E.reduce_sum(out=redc, in_=rseg[:], axis=AX.X
                                         ).then_inc(SEM["p0v"], 1)
                        pr.inc("p0v", 1)
                        pr.selfsync(E, SEM["p0v"], "p0v")
                        if pr.emit:
                            E.tensor_add(out=racc[:, c:c + 1], in0=racc[:, c:c + 1],
                                         in1=redc).then_inc(SEM["p0v"], 1)
                        pr.inc("p0v", 1)
                        pr.selfsync(E, SEM["p0v"], "p0v")
                        pr.mark(("p0rescons", c), "p0v")
                    else:
                        if kind == "w1q" and c == 0:
                            if pr.emit:
                                E.reduce_sum(out=rsum, in_=racc[:], axis=AX.X
                                             ).then_inc(SEM["p0v"], 1)
                            pr.inc("p0v", 1)
                            pr.selfsync(E, SEM["p0v"], "p0v")
                            pr.mark("p0rsum_pre", "p0v")
                            pr.wait(E, SEM["p0g"], "p0par3")

                            def _rowop(fn):
                                if pr.emit:
                                    fn().then_inc(SEM["p0v"], 1)
                                pr.inc("p0v", 1)
                                pr.selfsync(E, SEM["p0v"], "p0v")

                            _rowop(lambda: E.tensor_scalar(
                                out=mw2, in0=rsum, scalar1=1.0 / WN,
                                scalar2=mh2, op0=OP.mult, op1=OP.add))
                            _rowop(lambda: E.tensor_scalar_max(
                                out=mw2, in0=mw2, scalar1=EPS))
                            _rowop(lambda: E.tensor_scalar(
                                out=mw1, in0=acc1, scalar1=1.0 / WN,
                                scalar2=EPS, op0=OP.mult, op1=OP.max))
                            for mm_, ss_, nn_ in ((mw1, s1w, ns1w),
                                                  (mw2, s2w, ns2w)):
                                _rowop(lambda mm_=mm_, ss_=ss_:
                                       E.reciprocal(out=ss_, in_=mm_))
                                _rowop(lambda mm_=mm_, ss_=ss_:
                                       E.tensor_mul(out=rtmp, in0=mm_, in1=ss_))
                                _rowop(lambda: E.tensor_scalar(
                                    out=rtmp, in0=rtmp, scalar1=-1.0,
                                    scalar2=1.0, op0=OP.mult, op1=OP.add))
                                _rowop(lambda ss_=ss_:
                                       E.tensor_mul(out=rtmp, in0=rtmp, in1=ss_))
                                _rowop(lambda ss_=ss_:
                                       E.tensor_add(out=ss_, in0=ss_, in1=rtmp))
                                _rowop(lambda ss_=ss_, nn_=nn_:
                                       E.tensor_scalar_mul(out=nn_, in0=ss_,
                                                           scalar1=-1.0))
                        sw = s1w if kind == "w1q" else s2w
                        pr.wait(E, SEM[f"p0st{si % 2}"], ("p0st", si - 2))
                        for half in range(2):
                            u = 2 * vq + half
                            hs = slice(half * (HID // 2), (half + 1) * (HID // 2))
                            sabsh = sabs[:, hs]
                            sg = sa0 if half == 0 else sb0
                            pr.wait(E, SEM["p0s"], ("p0qs", u))
                            if pr.emit:
                                E.tensor_scalar(out=sabsh, in0=sabsh,
                                                scalar1=sw, scalar2=0.5,
                                                op0=OP.mult, op1=OP.is_gt)
                                E.tensor_mul(out=wq0[si % 2][:, hs],
                                             in0=sabsh, in1=sg[:]
                                             ).then_inc(SEM["p0v"], 2)
                            pr.inc("p0v", 2)
                            pr.mark(("p0qv", u), "p0v")
                        pr.mark(("p0ready", si), "p0v")
                        si += 1
                        vq += 1
                    pr.mark(("p0fill", fi), "p0v")
                barrier_end(E)

            def g0_gp(E):
                if pr.emit:
                    E.load_library(_q7lib)
                si = 0
                for fi, (kind, c) in enumerate(fills):
                    if kind == "w2r" and c == 0:
                        pr.wait(E, SEM["p0v"], "p0scans")
                        if pr.emit:
                            E.partition_all_reduce(acc1, acc1, 128, ReduceOp.add)
                            E.partition_all_reduce(acc2, acc2, 128, ReduceOp.add
                                                   ).then_inc(SEM["p0g"], 2)
                        pr.inc("p0g", 2)
                        pr.mark("p0par", "p0g")
                    if kind == "w1q" and c == 0:
                        pr.wait(E, SEM["p0v"], "p0rsum_pre")
                        if pr.emit:
                            E.partition_all_reduce(rsum, rsum, 128, ReduceOp.add
                                                   ).then_inc(SEM["p0g"], 1)
                        pr.inc("p0g", 1)
                        pr.mark("p0par3", "p0g")
                    if kind not in store_kinds:
                        continue
                    pr.wait(E, SEM["p0v"], ("p0ready", si))
                    src = wq0[si % 2]
                    if kind == "bt":
                        dsts = [(btq_d[:, :, c, :].transpose([1, 0, 2]),
                                 src[:, 0:NS].rearrange("p (n m) -> p n m", m=128))]
                    elif kind == "ct":
                        dsts = [(ctq_d[j * 8:(j + 1) * 8, :, c, :].transpose([1, 0, 2]),
                                 src[:, j * 1024:(j + 1) * 1024]
                                 .rearrange("p (d m) -> p d m", m=128))
                                for j in range(2)]
                    elif kind == "w1q":
                        dsts = [(w1q_d[j * 8:(j + 1) * 8, :, c, :].transpose([1, 0, 2]),
                                 src[:, j * 1024:(j + 1) * 1024]
                                 .rearrange("p (h m) -> p h m", m=128))
                                for j in range(8)]
                    else:
                        dsts = [(w2q_d[c * 512:(c + 1) * 512, :]
                                 .rearrange("(cc p) d -> p cc d", p=128),
                                 src[:, 0:HID].rearrange("p (cc d) -> p cc d", cc=4))]
                    for dst, s_ in dsts:
                        if pr.emit:
                            E.dma_start(out=dst, in_=s_).then_inc(SEM[f"p0st{si % 2}"], 16)
                        pr.inc(f"p0st{si % 2}", 16)
                    pr.mark(("p0st", si), f"p0st{si % 2}")
                    si += 1
                pr.wait(E, SEM["p0st0"], ("p0st", n_stores - 2))
                pr.wait(E, SEM["p0st1"], ("p0st", n_stores - 1))
                barrier_end(E)

            def g0_scal(E):
                nsc = 0
                sq = 0
                for fi, (kind, c) in enumerate(fills):
                    if kind in ("w1s", "w2s"):
                        pr.wait(E, SEM[f"p0ld{fi % 2}"], ("p0ld", fi))
                        if pr.emit:
                            E.activation(out=sabs[:], in_=stg0[fi % 2][:],
                                         func=AF.Abs,
                                         accum_out=scancol[:, nsc:nsc + 1]
                                         ).then_inc(SEM["p0s"], 1)
                        pr.inc("p0s", 1)
                        pr.mark(("p0fill", fi), "p0s")
                        nsc += 1
                        if kind == "w2s" and c == 15:
                            pr.mark("scans_s", "p0s")
                    elif kind == "w2r":
                        pr.wait(E, SEM[f"p0ld{fi % 2}"], ("p0ld", fi))
                        pr.wait(E, SEM["p0v"], ("p0rescons", c - 1))
                        if pr.emit:
                            E.activation(out=sabs[:], in_=stg0[fi % 2][:],
                                         func=AF.Abs).then_inc(SEM["p0s"], 1)
                        pr.inc("p0s", 1)
                        pr.mark(("p0abs", c), "p0s")
                    elif kind in ("w1q", "w2q"):
                        pr.wait(E, SEM[f"p0ld{fi % 2}"], ("p0ld", fi))
                        for half in range(2):
                            u = 2 * sq + half
                            hs = slice(half * (HID // 2),
                                       (half + 1) * (HID // 2))
                            sabsh = sabs[:, hs]
                            sg = sa0 if half == 0 else sb0
                            pr.wait(E, SEM["p0v"], ("p0qv", u - 2))
                            if u < 2:
                                pr.wait(E, SEM["p0v"], ("p0rescons", 15))
                            if pr.emit:
                                E.activation(out=sabsh,
                                             in_=stg0[fi % 2][:, hs],
                                             func=AF.Abs)
                                E.activation(out=sg[:],
                                             in_=stg0[fi % 2][:, hs],
                                             func=AF.Sign
                                             ).then_inc(SEM["p0s"], 2)
                            pr.inc("p0s", 2)
                            pr.mark(("p0qs", u), "p0s")
                        pr.mark(("p0fillq", fi), "p0s")
                        sq += 1
                barrier_end(E)

            def run_p0(block):
                if block is None:
                    g0_sync(NULL); g0_vec(NULL); g0_scal(NULL); g0_gp(NULL)
                else:
                    block.sync(lambda e: g0_sync(e))
                    block.vector(lambda e: g0_vec(e))
                    block.scalar(lambda e: g0_scal(e))
                    block.gpsimd(lambda e: g0_gp(e))

            if 0 in phases:
                phase(run_p0, 4)

        with ExitStack() as p1:
            sxb = [p1.enter_context(nc.sbuf_tensor(f"sx{j}", [128, NCD, T1], F32))
                   for j in range(2)]
            sxnb = p1.enter_context(nc.sbuf_tensor("sxnb", [128, NCD * T1], BF16))
            sxqd = [p1.enter_context(nc.sbuf_tensor(f"sxqd{j}", [128, T1], BF16))
                    for j in range(2)]
            acc_tm = [p1.enter_context(nc.sbuf_tensor(f"acctm{j}", [128, DM], F32))
                      for j in range(4)]
            btbuf = [p1.enter_context(nc.sbuf_tensor(f"btbuf{j}", [128, NCD * 128], BF16))
                     for j in range(2)]
            ctbuf = [p1.enter_context(nc.sbuf_tensor(f"ctbuf{j}", [128, NCS * 128], BF16))
                     for j in range(2)]
            st_bf = p1.enter_context(nc.sbuf_tensor("st_bf", [128, NCS * T1], BF16))
            accf = [p1.enter_context(nc.sbuf_tensor(f"accf{j}", [128, T1], F32))
                    for j in range(2)]
            rows = p1.enter_context(nc.sbuf_tensor("rows1", [128, 11, T1], F32))
            (r_sums, r_sumsq, r_xmax, r_nxmin, r_m0, r_m1, r_rstd0, r_rstd1,
             r_sxs, r_A, r_B) = [rows[:, j, :] for j in range(11)]
            r_C = r_B
            sqc = [p1.enter_context(nc.sbuf_tensor(f"sqc{j}", [128, T1], F32))
                   for j in range(2)]
            xnscr = [p1.enter_context(nc.sbuf_tensor(f"xnscr{j}", [128, T1], F32))
                     for j in range(2)]
            qfscr = p1.enter_context(nc.sbuf_tensor("qfscr", [128, T1], F32))
            qiscr = p1.enter_context(nc.sbuf_tensor("qiscr", [128, T1], I32))
            xnev = p1.enter_context(nc.sbuf_tensor("xnev", [128, T1], F32))
            psum1 = [p1.enter_context(nc.psum_tensor(f"ps1{j}", [128, T1], F32))
                     for j in range(2)]
            psum2 = [p1.enter_context(nc.psum_tensor(f"ps2{j}", [128, T1], F32))
                     for j in range(2)]
            psumtr_full = [p1.enter_context(nc.psum_tensor(f"pstr{j}", [128, 512], F32))
                           for j in range(2)]
            psumtr = [t[:, 0:128] for t in psumtr_full]

            def w2k(i, j, J):
                return (i, j) if j >= 0 else (i - 1, J + j)

            def _p1_xload(E, i):
                if i >= NT1:
                    return
                pr.wait(E, SEM["v1"], ("xdone", i - 2))
                for hh in range(2):
                    if pr.emit:
                        src = (x_fm.rearrange("(c p) t -> p c t", p=128)
                               [:, hh * 8:(hh + 1) * 8, i * T1:(i + 1) * T1])
                        E.dma_start(out=sxb[i % 2][:, hh * 8:(hh + 1) * 8, :],
                                    in_=src).then_inc(SEM[f"ldx{i % 2}"], 16)
                    pr.inc(f"ldx{i % 2}", 16)
                pr.mark(("ldx", i), f"ldx{i % 2}")

            def g1_sync(E):
                barrier_wait(E)
                _p1_xload(E, 0)
                _p1_xload(E, 1)
                for i in range(NT1):
                    for n in range(NCS):
                        pr.wait(E, SEM["t1"], ("mm1grp",) + w2k(i, n - 2, NCS))
                        if pr.emit:
                            E.dma_start(out=btbuf[n % 2][:],
                                        in_=btq_d[n].rearrange("p k m -> p (k m)")
                                        ).then_inc(SEM[f"ldbt{n % 2}"], 16)
                        pr.inc(f"ldbt{n % 2}", 16)
                        pr.mark(("ldbt", i, n), f"ldbt{n % 2}")
                    for dc in range(NCD):
                        pr.wait(E, SEM["t1"], ("mm2grp",) + w2k(i, dc - 2, NCD))
                        if pr.emit:
                            E.dma_start(out=ctbuf[dc % 2][:],
                                        in_=ctq_d[dc].rearrange("p s m -> p (s m)")
                                        ).then_inc(SEM[f"ldct{dc % 2}"], 16)
                        pr.inc(f"ldct{dc % 2}", 16)
                        pr.mark(("ldct", i, dc), f"ldct{dc % 2}")
                    _p1_xload(E, i + 2)
                barrier_end(E)

            def _v_front(E, i):
                sx = sxb[i % 2]
                r_m = r_m0 if i % 2 == 0 else r_m1
                r_rstd = r_rstd0 if i % 2 == 0 else r_rstd1
                pr.wait(E, SEM[f"ldx{i % 2}"], ("ldx", i))
                if pr.emit:
                    E.reduce_sum(out=r_sums, in_=sx[:].transpose([0, 2, 1]),
                                 axis=AX.X)
                    E.reduce_max(out=r_xmax, in_=sx[:].transpose([0, 2, 1]),
                                 axis=AX.X)
                    E.tensor_reduce(out=r_C, in_=sx[:].transpose([0, 2, 1]),
                                    axis=AX.X, op=OP.min)
                    E.tensor_scalar_mul(out=r_nxmin, in0=r_C, scalar1=-1.0
                                        ).then_inc(SEM["v1"], 4)
                pr.inc("v1", 4)
                for c in range(NCD):
                    pr.wait(E, SEM["s1"], ("sq", i, c))
                    if pr.emit:
                        if c == 0:
                            E.tensor_copy(out=r_sumsq, in_=sqc[0][:]
                                          ).then_inc(SEM["v1"], 1)
                        else:
                            E.tensor_add(out=r_sumsq, in0=r_sumsq, in1=sqc[c % 2][:]
                                         ).then_inc(SEM["v1"], 1)
                    pr.inc("v1", 1)
                    pr.mark(("sqdone", i, c), "v1")
                pr.mark(("stats", i), "v1")
                pr.wait(E, SEM["g1"], ("par", i))
                if pr.emit:
                    E.tensor_scalar_mul(out=r_m, in0=r_sums, scalar1=1.0 / DM)
                    E.tensor_scalar_mul(out=r_A, in0=r_sumsq, scalar1=1.0 / DM)
                    E.tensor_mul(out=r_B, in0=r_m, in1=r_m)
                    E.tensor_sub(out=r_A, in0=r_A, in1=r_B)
                    E.tensor_scalar_add(out=r_A, in0=r_A, scalar1=EPS
                                        ).then_inc(SEM["v1"], 4)
                pr.inc("v1", 4)
                pr.mark(("var", i), "v1")
                pr.wait(E, SEM["s1"], ("sd", i))
                if pr.emit:
                    E.reciprocal(out=r_rstd, in_=r_B)
                    E.tensor_sub(out=r_xmax, in0=r_xmax, in1=r_m)
                    E.tensor_add(out=r_nxmin, in0=r_nxmin, in1=r_m)
                    E.tensor_max(out=r_xmax, in0=r_xmax, in1=r_nxmin)
                    E.tensor_mul(out=r_xmax, in0=r_xmax, in1=r_rstd)
                    E.tensor_scalar_max(out=r_xmax, in0=r_xmax, scalar1=EPS)
                    E.reciprocal(out=r_B, in_=r_xmax)
                    E.tensor_scalar_mul(out=r_sxs, in0=r_B, scalar1=127.0)
                    E.tensor_scalar(out=c1_full[:, i * T1:(i + 1) * T1],
                                    in0=r_xmax, scalar1=mw1, scalar2=1.0 / 127.0,
                                    op0=OP.mult, op1=OP.mult
                                    ).then_inc(SEM["v1"], 9)
                pr.inc("v1", 9)
                for c in range(NCD):
                    _k = w2k(i, c - 2, NCD)
                    pr.wait(E, SEM[f"stxq{c % 2}"], ("stxq",) + _k)
                    pr.wait(E, SEM["s1"], ("xnb",) + w2k(i, c - 2, NCD))
                    if pr.emit:
                        E.tensor_sub(out=xnscr[c % 2][:], in0=sx[:, c, :], in1=r_m)
                        E.tensor_mul(out=xnscr[c % 2][:], in0=xnscr[c % 2][:],
                                     in1=r_rstd).then_inc(SEM["v1"], 2)
                    pr.inc("v1", 2)
                    pr.mark(("xn", i, c), "v1")
                    if pr.emit:
                        E.tensor_mul(out=qfscr[:], in0=xnscr[c % 2][:], in1=r_sxs)
                        E.tensor_copy(out=qiscr[:], in_=qfscr[:])
                        E.tensor_scalar(out=sxqd[c % 2][:], in0=qiscr[:],
                                        scalar1=0.0, scalar2=None, op0=OP.add
                                        ).then_inc(SEM["v1"], 3)
                    pr.inc("v1", 3)
                    pr.mark(("xq", i, c), "v1")

            def _v_evac(E, i):
                sx = sxb[i % 2]
                r_m = r_m0 if i % 2 == 0 else r_m1
                r_rstd = r_rstd0 if i % 2 == 0 else r_rstd1
                for dc in range(NCD):
                    pr.wait(E, SEM["t1"], ("mm2grp", i, dc))
                    pr.wait(E, SEM["t1"], ("tr",) + w2k(i, dc - 2, NCD))
                    if pr.emit:
                        E.tensor_sub(out=xnev[:], in0=sx[:, dc, :], in1=r_m)
                        E.tensor_mul(out=xnev[:], in0=xnev[:], in1=r_rstd)
                        E.scalar_tensor_tensor(out=xnev[:], in0=xnev[:],
                                               scalar=sD[:, dc:dc + 1],
                                               in1=sx[:, dc, :],
                                               op0=OP.mult, op1=OP.add)
                        E.tensor_add(out=accf[dc % 2][:], in0=xnev[:],
                                     in1=psum2[dc % 2][:]).then_inc(SEM["v1"], 4)
                    pr.inc("v1", 4)
                    pr.mark(("accf", i, dc), "v1")
                pr.mark(("xdone", i), "v1")

            def g1_vec(E):
                barrier_wait(E)
                pr.wait(E, SEM["p0misc"], "p0misc_all")
                for i in range(NT1):
                    _v_front(E, i)
                    if i >= 1:
                        _v_evac(E, i - 1)
                _v_evac(E, NT1 - 1)
                barrier_end(E)

            def _s_sq(E, i):
                sx = sxb[i % 2]
                for c in range(NCD):
                    pr.wait(E, SEM[f"ldx{i % 2}"], ("ldx", i))
                    pr.wait(E, SEM["v1"], ("sqdone",) + w2k(i, c - 2, NCD))
                    if pr.emit:
                        E.activation(out=sqc[c % 2][:], in_=sx[:, c, :],
                                     func=AF.Square).then_inc(SEM["s1"], 1)
                    pr.inc("s1", 1)
                    pr.mark(("sq", i, c), "s1")
                pr.wait(E, SEM["v1"], ("var", i))
                if pr.emit:
                    E.activation(out=r_B, in_=r_A, func=AF.Sqrt
                                 ).then_inc(SEM["s1"], 1)
                pr.inc("s1", 1)
                pr.mark(("sd", i), "s1")

            def _s_tanh(E, i):
                for n in range(NCS):
                    pr.wait(E, SEM["t1"], ("mm1grp", i, n))
                    if n == 0:
                        pr.wait(E, SEM["t1"], ("mm2grp", i - 1, NCD - 1))
                    if pr.emit:
                        E.activation(out=st_bf[:, n * T1:(n + 1) * T1],
                                     in_=psum1[n % 2][:], func=AF.Tanh
                                     ).then_inc(SEM["s1"], 1)
                    pr.inc("s1", 1)
                    pr.mark(("tanh", i, n), "s1")

            def _s_xnb(E, i):
                for c in range(NCD):
                    pr.wait(E, SEM["v1"], ("xn", i, c))
                    if c == 0:
                        pr.wait(E, SEM["t1"], ("mm1grp", i - 1, NCS - 1))
                    if pr.emit:
                        E.activation(out=sxnb[:, c * T1:(c + 1) * T1],
                                     in_=xnscr[c % 2][:], func=AF.Copy
                                     ).then_inc(SEM["s1"], 1)
                    pr.inc("s1", 1)
                    pr.mark(("xnb", i, c), "s1")

            def _s_evacs(E, i):
                pr.wait(E, SEM["stacc"], ("stacc", i - 1))
                for dc in range(NCD):
                    for tc in range(4):
                        lin = (i * NCD + dc) * 4 + tc
                        pr.wait(E, SEM["t1"], ("trtc", lin))
                        if pr.emit:
                            E.copy(out=acc_tm[tc][:, dc * 128:(dc + 1) * 128],
                                   in_=psumtr[tc % 2]).then_inc(SEM["s1"], 1)
                        pr.inc("s1", 1)
                        pr.mark(("sevtc", lin), "s1")
                pr.mark(("sevac_all", i), "s1")

            def g1_scal(E):
                barrier_wait(E)
                for i in range(NT1):
                    _s_sq(E, i)
                    if i >= 1:
                        _s_tanh(E, i - 1)
                    _s_xnb(E, i)
                    if i >= 1:
                        _s_evacs(E, i - 1)
                _s_tanh(E, NT1 - 1)
                _s_evacs(E, NT1 - 1)
                barrier_end(E)

            def _tr_group(E, i, dc):
                pr.wait(E, SEM["v1"], ("accf", i, dc))
                for tc in range(4):
                    lin = (i * NCD + dc) * 4 + tc
                    pr.wait(E, SEM["s1"], ("sevtc", lin - 2))
                    if pr.emit:
                        E.transpose(psumtr[tc % 2],
                                    accf[dc % 2][:, tc * 128:(tc + 1) * 128],
                                    seye[:]).then_inc(SEM["t1"], 1)
                    pr.inc("t1", 1)
                    pr.mark(("trtc", lin), "t1")
                pr.mark(("tr", i, dc), "t1")

            def g1_tens(E):
                barrier_wait(E)
                pr.wait(E, SEM["p0misc"], "p0misc_all")
                for i in range(NT1):
                    pr.wait(E, SEM["s1"], ("xnb", i, NCD - 1))
                    for n in range(NCS):
                        pr.wait(E, SEM[f"ldbt{n % 2}"], ("ldbt", i, n))
                        pr.wait(E, SEM["s1"], ("tanh",) + w2k(i, n - 2, NCS))
                        for kc in range(NCD):
                            if pr.emit:
                                mm = E.matmul(psum1[n % 2][:],
                                              btbuf[n % 2][:, kc * 128:(kc + 1) * 128],
                                              sxnb[:, kc * T1:(kc + 1) * T1],
                                              start=(kc == 0), stop=(kc == NCD - 1))
                                if kc == NCD - 1:
                                    mm.then_inc(SEM["t1"], 1)
                        pr.inc("t1", 1)
                        pr.mark(("mm1grp", i, n), "t1")
                    pr.wait(E, SEM["s1"], ("tanh", i, NCS - 1))
                    for dc in range(NCD):
                        pr.wait(E, SEM[f"ldct{dc % 2}"], ("ldct", i, dc))
                        pr.wait(E, SEM["v1"], ("accf",) + w2k(i, dc - 2, NCD))
                        for sc in range(NCS):
                            if pr.emit:
                                mm = E.matmul(psum2[dc % 2][:],
                                              ctbuf[dc % 2][:, sc * 128:(sc + 1) * 128],
                                              st_bf[:, sc * T1:(sc + 1) * T1],
                                              start=(sc == 0), stop=(sc == NCS - 1))
                                if sc == NCS - 1:
                                    mm.then_inc(SEM["t1"], 1)
                        pr.inc("t1", 1)
                        pr.mark(("mm2grp", i, dc), "t1")
                        if dc >= 1:
                            _tr_group(E, i, dc - 1)
                    _tr_group(E, i, NCD - 1)
                barrier_end(E)

            def g1_gp(E):
                barrier_wait(E)
                for i in range(NT1):
                    pr.wait(E, SEM["v1"], ("stats", i))
                    if pr.emit:
                        E.partition_all_reduce(rows[:, 0:2, :].rearrange("p a t -> p (a t)"),
                                               rows[:, 0:2, :].rearrange("p a t -> p (a t)"),
                                               128, ReduceOp.add)
                        E.partition_all_reduce(rows[:, 2:4, :].rearrange("p a t -> p (a t)"),
                                               rows[:, 2:4, :].rearrange("p a t -> p (a t)"),
                                               128, ReduceOp.max
                                               ).then_inc(SEM["g1"], 4)
                    pr.inc("g1", 4)
                    pr.mark(("par", i), "g1")
                    for c in range(NCD):
                        pr.wait(E, SEM["v1"], ("xq", i, c))
                        if pr.emit:
                            E.dma_start(out=xq_d[c * 128:(c + 1) * 128,
                                                 i * T1:(i + 1) * T1],
                                        in_=sxqd[c % 2][:]
                                        ).then_inc(SEM[f"stxq{c % 2}"], 16)
                        pr.inc(f"stxq{c % 2}", 16)
                        pr.mark(("stxq", i, c), f"stxq{c % 2}")
                    if i >= 1:
                        _gp_acc(E, i - 1)
                _gp_acc(E, NT1 - 1)
                pr.wait(E, SEM["stxq0"], ("stxq", NT1 - 1, NCD - 2))
                pr.wait(E, SEM["stxq1"], ("stxq", NT1 - 1, NCD - 1))
                pr.wait(E, SEM["stacc"], ("stacc", NT1 - 1))
                barrier_end(E)

            def _gp_acc(E, i):
                pr.wait(E, SEM["s1"], ("sevac_all", i))
                for tc in range(4):
                    if pr.emit:
                        E.dma_start(out=acc_d[i * T1 + tc * 128:
                                              i * T1 + (tc + 1) * 128, :],
                                    in_=acc_tm[tc][:]).then_inc(SEM["stacc"], 16)
                    pr.inc("stacc", 16)
                pr.mark(("stacc", i), "stacc")

            def run_p1(block):
                if block is None:
                    g1_vec(NULL); g1_scal(NULL); g1_tens(NULL)
                    g1_gp(NULL); g1_sync(NULL)
                else:
                    block.vector(lambda e: g1_vec(e))
                    block.scalar(lambda e: g1_scal(e))
                    block.tensor(lambda e: g1_tens(e))
                    block.gpsimd(lambda e: g1_gp(e))
                    block.sync(lambda e: g1_sync(e))

            if 1 in phases:
                phase(run_p1, 5)

        with ExitStack() as p2:
            sxqh = p2.enter_context(nc.sbuf_tensor("sxqh", [128, NCD, TH], BF16))
            w1buf = [p2.enter_context(nc.sbuf_tensor(f"w1b{j}", [128, NCD * 128], BF16))
                     for j in range(2)]
            gtmp = [p2.enter_context(nc.sbuf_tensor(f"gtmp{j}", [128, TH], F32))
                    for j in range(2)]
            gbuf = [p2.enter_context(nc.sbuf_tensor(f"gbuf{j}", [128, TH], F32))
                    for j in range(2)]
            gabs = [p2.enter_context(nc.sbuf_tensor(f"gabs{j}", [128, TH], F32))
                    for j in range(2)]
            macc = p2.enter_context(nc.sbuf_tensor("macc", [128, TH], F32))
            c3row = p2.enter_context(nc.sbuf_tensor("c3row", [128, TH], F32))
            p2ps = [p2.enter_context(nc.psum_tensor(f"p2ps{j}", [128, TH], F32))
                    for j in range(2)]

            def g2_sync(E):
                barrier_wait(E)
                for h in range(NTH):
                    pr.wait(E, SEM["t2"], ("mmh", h * NCH - 1))
                    for hh in range(2):
                        if pr.emit:
                            src = (xq_d[:, h * TH:(h + 1) * TH]
                                   .rearrange("(c p) t -> p c t", p=128)
                                   [:, hh * 8:(hh + 1) * 8, :])
                            E.dma_start(out=sxqh[:, hh * 8:(hh + 1) * 8, :], in_=src
                                        ).then_inc(SEM["ldxq"], 16)
                        pr.inc("ldxq", 16)
                    pr.mark(("ldxq", h), "ldxq")
                    for hc in range(NCH):
                        k = h * NCH + hc
                        pr.wait(E, SEM["t2"], ("mmh", k - 2))
                        if pr.emit:
                            E.dma_start(out=w1buf[k % 2][:],
                                        in_=w1q_d[hc].rearrange("p k m -> p (k m)")
                                        ).then_inc(SEM[f"ldw1_{k % 2}"], 16)
                        pr.inc(f"ldw1_{k % 2}", 16)
                        pr.mark(("ldw1", k), f"ldw1_{k % 2}")
                barrier_end(E)

            def g2_tens(E):
                barrier_wait(E)
                for h in range(NTH):
                    pr.wait(E, SEM["ldxq"], ("ldxq", h))
                    for hc in range(NCH):
                        k = h * NCH + hc
                        pr.wait(E, SEM[f"ldw1_{k % 2}"], ("ldw1", k))
                        pr.wait(E, SEM["v2"], ("evacmul", k - 2))
                        for kc in range(NCD):
                            for tc in range(P2TC):
                                if pr.emit:
                                    mm = E.matmul(
                                        p2ps[k % 2][:, tc * 512:(tc + 1) * 512],
                                        w1buf[k % 2][:, kc * 128:(kc + 1) * 128],
                                        sxqh[:, kc, tc * 512:(tc + 1) * 512],
                                        start=(kc == 0), stop=(kc == NCD - 1),
                                        skip_group_check=True)
                                    if kc == NCD - 1 and tc == P2TC - 1:
                                        mm.then_inc(SEM["t2"], 1)
                        pr.inc("t2", 1)
                        pr.mark(("mmh", k), "t2")
                barrier_end(E)

            def _macc(E, k):
                pr.wait(E, SEM["s2"], ("absd", k))
                if pr.emit:
                    E.tensor_max(out=macc[:], in0=macc[:], in1=gabs[k % 2][:]
                                 ).then_inc(SEM["v2"], 1)
                pr.inc("v2", 1)
                pr.mark(("macc", k), "v2")

            def g2_vec(E):
                barrier_wait(E)
                for h in range(NTH):
                    if pr.emit:
                        E.memset(macc[:], 0.0).then_inc(SEM["v2"], 1)
                    pr.inc("v2", 1)
                    for hc in range(NCH):
                        k = h * NCH + hc
                        pr.wait(E, SEM["t2"], ("mmh", k))
                        pr.wait(E, SEM["s2"], ("gelu", k - 2))
                        for tc in range(P2TC):
                            if pr.emit:
                                mm = E.tensor_mul(
                                    out=gtmp[k % 2][:, tc * 512:(tc + 1) * 512],
                                    in0=p2ps[k % 2][:, tc * 512:(tc + 1) * 512],
                                    in1=c1_full[:, h * TH + tc * 512:
                                                h * TH + (tc + 1) * 512])
                                if tc == P2TC - 1:
                                    mm.then_inc(SEM["v2"], 1)
                        pr.inc("v2", 1)
                        pr.mark(("evacmul", k), "v2")
                        if hc >= 1:
                            _macc(E, k - 1)
                    _macc(E, h * NCH + NCH - 1)
                    pr.mark(("maccdone", h), "v2")
                    pr.wait(E, SEM["g2"], ("parh", h))
                    if pr.emit:
                        E.tensor_scalar_max(out=macc[:], in0=macc[:], scalar1=EPS)
                        E.reciprocal(out=c3row[:], in_=macc[:])
                        E.tensor_scalar_mul(out=s2bc[:, h * TH:(h + 1) * TH],
                                            in0=c3row[:], scalar1=127.0)
                        E.tensor_scalar(out=c3row[:], in0=macc[:], scalar1=mw2,
                                        scalar2=1.0 / 127.0, op0=OP.mult,
                                        op1=OP.mult).then_inc(SEM["v2"], 4)
                    pr.inc("v2", 4)
                    pr.mark(("c3row", h), "v2")
                barrier_end(E)

            def g2_scal(E):
                barrier_wait(E)
                for h in range(NTH):
                    for hc in range(NCH):
                        k = h * NCH + hc
                        pr.wait(E, SEM["v2"], ("evacmul", k))
                        pr.wait(E, SEM[f"stg{k % 2}"], ("stg", k - 2))
                        if pr.emit:
                            E.activation(out=gbuf[k % 2][:], in_=gtmp[k % 2][:],
                                         func=AF.Gelu).then_inc(SEM["s2"], 1)
                        pr.inc("s2", 1)
                        pr.mark(("gelu", k), "s2")
                        pr.wait(E, SEM["v2"], ("macc", k - 2))
                        if pr.emit:
                            E.activation(out=gabs[k % 2][:], in_=gbuf[k % 2][:],
                                         func=AF.Abs).then_inc(SEM["s2"], 1)
                        pr.inc("s2", 1)
                        pr.mark(("absd", k), "s2")
                barrier_end(E)

            def g2_gp(E):
                barrier_wait(E)
                for h in range(NTH):
                    for hc in range(NCH):
                        k = h * NCH + hc
                        pr.wait(E, SEM["s2"], ("gelu", k))
                        if pr.emit:
                            E.dma_start(out=g_d[h, hc], in_=gbuf[k % 2][:]
                                        ).then_inc(SEM[f"stg{k % 2}"], 16)
                        pr.inc(f"stg{k % 2}", 16)
                        pr.mark(("stg", k), f"stg{k % 2}")
                    pr.wait(E, SEM["v2"], ("maccdone", h))
                    if pr.emit:
                        E.partition_all_reduce(macc[:], macc[:], 128, ReduceOp.absmax
                                               ).then_inc(SEM["g2"], 1)
                    pr.inc("g2", 1)
                    pr.mark(("parh", h), "g2")
                    pr.wait(E, SEM["v2"], ("c3row", h))
                    if pr.emit:
                        E.dma_start(out=scl_d[h * TH:(h + 1) * TH],
                                    in_=c3row[0:1, :]).then_inc(SEM["stscl"], 16)
                    pr.inc("stscl", 16)
                pr.mark("stscl_all", "stscl")
                pr.wait(E, SEM["stscl"], "stscl_all")
                pr.wait(E, SEM["stg0"], ("stg", 2 * NCH - 2))
                pr.wait(E, SEM["stg1"], ("stg", 2 * NCH - 1))
                barrier_end(E)

            def run_p2(block):
                if block is None:
                    g2_vec(NULL); g2_scal(NULL); g2_tens(NULL)
                    g2_gp(NULL); g2_sync(NULL)
                else:
                    block.vector(lambda e: g2_vec(e))
                    block.scalar(lambda e: g2_scal(e))
                    block.tensor(lambda e: g2_tens(e))
                    block.gpsimd(lambda e: g2_gp(e))
                    block.sync(lambda e: g2_sync(e))

            if 2 in phases:
                phase(run_p2, 5)

        with ExitStack() as p3:
            hq_sb = p3.enter_context(nc.sbuf_tensor("hq_sb", [128, NCH * TG], BF16))
            w2buf = [p3.enter_context(nc.sbuf_tensor(f"w2b{j}", [128, 2, 1024], BF16))
                     for j in range(4)]
            gld = [p3.enter_context(nc.sbuf_tensor(f"gld{j}", [128, TG], F32))
                   for j in range(4)]
            accb = [p3.enter_context(nc.sbuf_tensor(f"accb{j}", [128, DM], F32))
                    for j in range(4)]
            outb = [p3.enter_context(nc.sbuf_tensor(f"outb{j}", [128, DM], F32))
                    for j in range(4)]
            q3f = p3.enter_context(nc.sbuf_tensor("q3f", [128, TG], F32))
            q3i = p3.enter_context(nc.sbuf_tensor("q3i", [128, TG], I32))
            p3ps = p3.enter_context(nc.psum_tensor("p3ps", [128, 4096], F32))

            def lin3(g, dh, hc):
                return (g * 2 + dh) * NCH + hc

            def unlin3(lin):
                g, r = divmod(lin, 2 * NCH)
                dh, hc = divmod(r, NCH)
                return (g, dh, hc)

            def g3_scal(E):
                barrier_wait(E)
                if pr.emit:
                    E.dma_start(out=c3col[:],
                                in_=scl_d.rearrange("(c p) -> p c", p=128),
                                allow_slow_non_contiguous=True
                                ).then_inc(SEM["ldc3"], 16)
                pr.inc("ldc3", 16)
                pr.mark("ldc3", "ldc3")
                for g in range(NG):
                    h, lt0, gt0 = g // 4, (g % 4) * TG, g * TG
                    for hc in range(NCH):
                        pr.wait(E, SEM["v3"],
                                ("hqv",) + ((g, hc - 4) if hc >= 4 else
                                            (g - 1, NCH + hc - 4)))
                        if pr.emit:
                            E.dma_start(out=gld[hc % 4][:],
                                        in_=g_d[h, hc][:, lt0:lt0 + TG]
                                        ).then_inc(SEM[f"ldg{hc % 4}"], 16)
                        pr.inc(f"ldg{hc % 4}", 16)
                        pr.mark(("ldg", g, hc), f"ldg{hc % 4}")
                    for tc in range(4):
                        pr.wait(E, SEM["v3"], ("outv", g - 1, tc))
                        if pr.emit:
                            E.dma_start(out=accb[tc][:],
                                        in_=acc_d[gt0 + tc * 128:
                                                  gt0 + (tc + 1) * 128, :]
                                        ).then_inc(SEM["ldacc"], 16)
                        pr.inc("ldacc", 16)
                        pr.mark(("ldacc", g, tc), "ldacc")
                barrier_end(E)

            def g3_sync(E):
                barrier_wait(E)
                for g in range(NG):
                    h, lt0, gt0 = g // 4, (g % 4) * TG, g * TG
                    for dh in range(2):
                        for hcp in range(NCH // 2):
                            li = (g * 2 + dh) * (NCH // 2) + hcp - 4
                            if li >= 0:
                                gg_, r_ = divmod(li, 2 * (NCH // 2))
                                dh_, hcp_ = divmod(r_, NCH // 2)
                                pr.wait(E, SEM["t3"], ("mm3", gg_, dh_, 2 * hcp_ + 1))
                            if pr.emit:
                                E.dma_start(
                                    out=w2buf[hcp % 4][:],
                                    in_=w2q_d[hcp * 256:(hcp + 1) * 256,
                                              dh * 1024:(dh + 1) * 1024]
                                    .rearrange("(c p) d -> p c d", p=128)
                                    ).then_inc(SEM[f"ldw2_{hcp % 4}"], 16)
                            pr.inc(f"ldw2_{hcp % 4}", 16)
                            pr.mark(("ldw2", g, dh, hcp), f"ldw2_{hcp % 4}")
                barrier_end(E)

            def _evac(E, g, dh):
                pr.wait(E, SEM["t3"], ("mm3", g, dh, NCH - 1))
                pr.wait(E, SEM["ldc3"], "ldc3")
                for tc in range(4):
                    pr.wait(E, SEM["ldacc"], ("ldacc", g, 3))
                    pr.wait(E, SEM["stout"], ("stout", g - 1))
                    for dcn in range(2):
                        sl = slice(dh * 1024 + dcn * 512,
                                   dh * 1024 + (dcn + 1) * 512)
                        if pr.emit:
                            E.scalar_tensor_tensor(
                                out=outb[tc][:, sl],
                                in0=p3ps[:, tc * 1024 + dcn * 512:
                                         tc * 1024 + (dcn + 1) * 512],
                                scalar=c3col[:, g * 4 + tc:g * 4 + tc + 1],
                                in1=accb[tc][:, sl], op0=OP.mult, op1=OP.add
                                ).then_inc(SEM["v3"], 1)
                        pr.inc("v3", 1)
                    if dh == 1:
                        pr.mark(("outv", g, tc), "v3")
                pr.mark(("evac", g, dh), "v3")

            def g3_vec(E):
                barrier_wait(E)
                for g in range(NG):
                    h, lt0 = g // 4, (g % 4) * TG
                    if g >= 1:
                        _evac(E, g - 1, 0)
                    for hc in range(NCH):
                        pr.wait(E, SEM[f"ldg{hc % 4}"], ("ldg", g, hc))
                        pr.wait(E, SEM["t3"], ("mm3", g - 1, 1, hc))
                        if pr.emit:
                            E.tensor_mul(out=q3f[:], in0=gld[hc % 4][:],
                                         in1=s2bc[:, h * TH + lt0:
                                                  h * TH + lt0 + TG])
                            E.tensor_copy(out=q3i[:], in_=q3f[:])
                            E.tensor_scalar(out=hq_sb[:, hc * TG:(hc + 1) * TG],
                                            in0=q3i[:], scalar1=0.0, scalar2=None,
                                            op0=OP.add).then_inc(SEM["v3"], 4)
                        pr.inc("v3", 4)
                        pr.mark(("hqv", g, hc), "v3")
                    if g >= 1:
                        _evac(E, g - 1, 1)
                _evac(E, NG - 1, 0)
                _evac(E, NG - 1, 1)
                barrier_end(E)

            def g3_tens(E):
                barrier_wait(E)
                for g in range(NG):
                    for dh in range(2):
                        for hc in range(NCH):
                            hcp = hc // 2
                            pr.wait(E, SEM[f"ldw2_{hcp % 4}"], ("ldw2", g, dh, hcp))
                            if dh == 0:
                                pr.wait(E, SEM["v3"], ("hqv", g, hc))
                                if hc == 0:
                                    pr.wait(E, SEM["v3"], ("evac", g - 1, 1))
                            elif hc == 0:
                                pr.wait(E, SEM["v3"], ("evac", g, 0))
                            for tc in range(4):
                                for dcn in range(2):
                                    if pr.emit:
                                        mm = E.matmul(
                                            p3ps[:, tc * 1024 + dcn * 512:
                                                 tc * 1024 + (dcn + 1) * 512],
                                            hq_sb[:, hc * TG + tc * 128:
                                                  hc * TG + (tc + 1) * 128],
                                            w2buf[(hc // 2) % 4][:, hc % 2,
                                                  dcn * 512:(dcn + 1) * 512],
                                            start=(hc == 0), stop=(hc == NCH - 1),
                                            skip_group_check=True)
                                        if tc == 3 and dcn == 1:
                                            mm.then_inc(SEM["t3"], 1)
                            pr.inc("t3", 1)
                            pr.mark(("mm3", g, dh, hc), "t3")
                barrier_end(E)

            def g3_gp(E):
                barrier_wait(E)
                for g in range(NG):
                    gt0 = g * TG
                    for tc in range(4):
                        pr.wait(E, SEM["v3"], ("outv", g, tc))
                        if pr.emit:
                            E.dma_start(out=out_tm[gt0 + tc * 128:
                                                   gt0 + (tc + 1) * 128, :],
                                        in_=outb[tc][:]).then_inc(SEM["stout"], 16)
                        pr.inc("stout", 16)
                    pr.mark(("stout", g), "stout")
                pr.wait(E, SEM["stout"], ("stout", NG - 1))
                barrier_end(E)

            def run_p3(block):
                if block is None:
                    g3_vec(NULL); g3_tens(NULL); g3_gp(NULL)
                    g3_scal(NULL); g3_sync(NULL)
                else:
                    block.vector(lambda e: g3_vec(e))
                    block.tensor(lambda e: g3_tens(e))
                    block.gpsimd(lambda e: g3_gp(e))
                    block.scalar(lambda e: g3_scal(e))
                    block.sync(lambda e: g3_sync(e))

            if 3 in phases:
                phase(run_p3, 5)

    mybir.codegen_inst_isa_subclasses(nc)
    return nc


_NC_CACHE = []


def kernel(L0_attn_x, L0_attn_B, L0_attn_C, L0_attn_D, L0_ffn_w1, L0_ffn_w2):
    x = np.asarray(L0_attn_x, dtype=np.float32)
    B = np.asarray(L0_attn_B, dtype=np.float32)
    C = np.asarray(L0_attn_C, dtype=np.float32)
    D = np.asarray(L0_attn_D, dtype=np.float32)
    w1 = np.asarray(L0_ffn_w1, dtype=np.float32)
    w2 = np.asarray(L0_ffn_w2, dtype=np.float32)

    btf = np.ascontiguousarray(B.T)
    ctf = np.ascontiguousarray(C.T)
    w1t = np.ascontiguousarray(w1.T)
    w2t = np.ascontiguousarray(w2.T)
    eyem = np.eye(128, dtype=np.float32)

    if not _NC_CACHE:
        _NC_CACHE.append(build_program())
    nc = _NC_CACHE[0]

    in_maps = [dict(x_fm=np.ascontiguousarray(x[i].T),
                    btf=btf, ctf=ctf, dvec=D, w1t=w1t, w2t=w2t, eye=eyem)
               for i in range(N_CORES)]
    res = run_bass_kernel_spmd(nc, in_maps, core_ids=list(range(N_CORES)))
    return np.stack([np.asarray(r["out_tm"]) for r in res.results], axis=0)


# version 12 (final)
# speedup vs baseline: 1.1156x; 1.0626x over previous
import numpy as np
from contextlib import ExitStack

import concourse.bass as bass
import concourse.mybir as mybir
from concourse.bass_isa import ReduceOp
from concourse.library_config import attn as _q7lib
from concourse.bass_utils import run_bass_kernel_spmd

F32 = mybir.dt.float32
BF16 = mybir.dt.bfloat16
I32 = mybir.dt.int32
AF = mybir.ActivationFunctionType
OP = mybir.AluOpType
AX = mybir.AxisListType

TOK, DM, NS, HID = 4096, 2048, 1024, 8192
NCD, NCS, NCH = DM // 128, NS // 128, HID // 128
T1 = 512
NT1 = TOK // T1
TH = 2048
NTH = TOK // TH
P2TC = TH // 512
TG = 512
NG = TOK // TG
EPS = 1e-5
WN = float(DM * HID)

N_CORES = 8


class _NullInst:
    def then_inc(self, *a, **k):
        return self


class _NullEng:
    def __getattr__(self, k):
        return lambda *a, **kw: _NullInst()


NULL = _NullEng()


class Prog:

    def __init__(self):
        self.emit = False
        self.cnt = {}
        self.ck = {}

    def inc(self, name, n=1):
        self.cnt[name] = self.cnt.get(name, 0) + n

    def mark(self, key, name):
        v = self.cnt.get(name, 0)
        if self.emit:
            assert self.ck.get(key) == v, ("ck mismatch", key, self.ck.get(key), v)
        else:
            self.ck[key] = v

    def selfsync(self, eng, sem, name):
        if self.emit:
            eng.wait_ge(sem, self.cnt[name])

    def wait(self, eng, sem, key):
        v = self.ck.get(key, 0) if isinstance(key, tuple) or isinstance(key, str) else key
        if self.emit and v > 0:
            eng.wait_ge(sem, v)


def build_program(phases=(0, 1, 2, 3), dbg=False):
    nc = bass.Bass("TRN2", target_bir_lowering=False,
                   detect_race_conditions=False)

    x_fm = nc.dram_tensor("x_fm", [DM, TOK], F32, kind="ExternalInput").ap()
    btf = nc.dram_tensor("btf", [DM, NS], F32, kind="ExternalInput").ap()
    ctf = nc.dram_tensor("ctf", [NS, DM], F32, kind="ExternalInput").ap()
    dvec = nc.dram_tensor("dvec", [DM], F32, kind="ExternalInput").ap()
    w1t = nc.dram_tensor("w1t", [DM, HID], F32, kind="ExternalInput").ap()
    w2t = nc.dram_tensor("w2t", [HID, DM], F32, kind="ExternalInput").ap()
    eye = nc.dram_tensor("eye", [128, 128], F32, kind="ExternalInput").ap()
    out_tm = nc.dram_tensor("out_tm", [TOK, DM], F32, kind="ExternalOutput").ap()

    if dbg is True:
        dbg = ("btq_d", "ctq_d", "w1q_d", "w2q_d", "xq_d", "acc_d", "g_d", "scl_d")
    elif not dbg:
        dbg = ()
    def sk(n):
        return "ExternalOutput" if n in dbg else "Internal"
    btq_d = nc.dram_tensor("btq_d", [NCS, 128, NCD, 128], BF16, kind=sk("btq_d")).ap()
    ctq_d = nc.dram_tensor("ctq_d", [NCD, 128, NCS, 128], BF16, kind=sk("ctq_d")).ap()
    w1q_d = nc.dram_tensor("w1q_d", [NCH, 128, NCD, 128], BF16, kind=sk("w1q_d")).ap()
    w2q_d = nc.dram_tensor("w2q_d", [HID, DM], BF16, kind=sk("w2q_d")).ap()
    xq_d = nc.dram_tensor("xq_d", [DM, TOK], BF16, kind=sk("xq_d")).ap()
    acc_d = nc.dram_tensor("acc_d", [TOK, DM], F32, kind=sk("acc_d")).ap()
    g_d = nc.dram_tensor("g_d", [NTH, NCH, 128, TH], F32, kind=sk("g_d")).ap()
    scl_d = nc.dram_tensor("scl_d", [TOK], F32, kind=sk("scl_d")).ap()

    pr = Prog()
    PB = [0]

    with ExitStack() as es:
        def sem(name):
            return es.enter_context(nc.semaphore(name))

        c1_full = es.enter_context(nc.sbuf_tensor("c1_full", [128, TOK], F32))
        s2bc = es.enter_context(nc.sbuf_tensor("s2bc", [128, TOK], F32))
        c3col = es.enter_context(nc.sbuf_tensor("c3col", [128, 32], F32))
        sD = es.enter_context(nc.sbuf_tensor("sD", [128, NCD], F32))
        seye = es.enter_context(nc.sbuf_tensor("seye", [128, 128], F32))
        prow = es.enter_context(nc.sbuf_tensor("prow", [128, 64], F32))
        mw1 = prow[:, 0:1]
        mw2 = prow[:, 1:2]
        s1w = prow[:, 2:3]
        ns1w = prow[:, 3:4]
        acc1 = prow[:, 4:5]
        acc2 = prow[:, 5:6]
        redc = prow[:, 6:7]
        s2w = prow[:, 7:8]
        ns2w = prow[:, 8:9]
        mh2 = prow[:, 9:10]
        rsum = prow[:, 10:11]
        rtmp = prow[:, 11:12]
        m164 = prow[:, 12:13]
        m264 = prow[:, 13:14]
        rsum1 = prow[:, 14:15]
        racc = prow[:, 16:32]
        racc1 = prow[:, 32:48]
        estcol1 = prow[:, 48:49]
        estcol2 = prow[:, 49:50]

        SEM = {n: sem(n) for n in [
            "pb", "p0misc",
            "p0ld0", "p0ld1", "p0v", "p0g", "p0st0", "p0st1", "p0s",
            "ldx0", "ldx1", "ldbt0", "ldbt1", "ldct0", "ldct1", "stxq0", "stxq1",
            "stacc", "v1", "s1", "t1", "g1",
            "ldw1_0", "ldw1_1", "ldxq", "stg0", "stg1", "stscl",
            "v2", "s2", "t2", "g2",
            "ldg0", "ldg1", "ldg2", "ldg3",
            "ldw2_0", "ldw2_1", "ldw2_2", "ldw2_3", "ldacc", "ldc3", "stout",
            "v3", "t3",
        ]}

        def barrier_end(E):
            if pr.emit:
                E.nop().then_inc(SEM["pb"], 1)
            pr.inc("pb", 1)

        def barrier_wait(E):
            v = PB[0]
            if pr.emit and v > 0:
                E.wait_ge(SEM["pb"], v)

        def phase(run, n_engines):
            snap_cnt = dict(pr.cnt)
            snap_pb = PB[0]
            run(None)
            pr.cnt = dict(snap_cnt)
            PB[0] = snap_pb
            pr.emit = True
            with nc.Block() as blk:
                run(blk)
            pr.emit = False
            PB[0] += n_engines

        with ExitStack() as p0:
            stgA = p0.enter_context(nc.sbuf_tensor("stgA", [128, HID], F32))
            stgB = p0.enter_context(nc.sbuf_tensor("stgB", [128, HID], F32))
            wqA = p0.enter_context(nc.sbuf_tensor("wqA", [128, HID], BF16))
            wqB = p0.enter_context(nc.sbuf_tensor("wqB", [128, HID], BF16))
            sa0 = p0.enter_context(nc.sbuf_tensor("sa0", [128, HID // 2], F32))
            sb0 = p0.enter_context(nc.sbuf_tensor("sb0", [128, HID // 2], F32))
            sabs = p0.enter_context(nc.sbuf_tensor("sabs", [128, HID], F32))
            rseg = p0.enter_context(nc.sbuf_tensor("rseg", [128, 128], F32))
            stg0 = [stgA, stgB]
            wq0 = [wqA, wqB]

            fills = ([("bt", c) for c in range(NCD)]
                     + [("ct", c) for c in range(NCS)]
                     + [("w1e", 0)]
                     + [("w1r", c) for c in range(NCD)]
                     + [("w2e", 0)]
                     + [("w2r", c) for c in range(16)]
                     + [("w1q", c) for c in range(NCD)]
                     + [("w2q", c) for c in range(16)])
            store_kinds = ("bt", "ct", "w1q", "w2q")
            n_stores = sum(1 for k, _ in fills if k in store_kinds)

            def fill_cols(kind):
                return {"bt": NS, "ct": DM, "w1e": HID, "w1r": HID,
                        "w2e": HID, "w2r": HID, "w1q": HID,
                        "w2q": HID}[kind]

            def g0_sync(E):
                if pr.emit:
                    E.dma_start(out=sD[:], in_=dvec.rearrange("(c p) -> p c", p=128),
                                allow_slow_non_contiguous=True).then_inc(SEM["p0misc"], 16)
                    E.dma_start(out=seye[:], in_=eye).then_inc(SEM["p0misc"], 16)
                pr.inc("p0misc", 32)
                pr.mark("p0misc_all", "p0misc")
                for fi, (kind, c) in enumerate(fills):
                    if fi >= 2 and fills[fi - 2][0] in ("w1e", "w1r",
                                                         "w2e", "w2r"):
                        pr.wait(E, SEM["p0s"], ("p0fillR", fi - 2))
                    elif fi >= 2 and fills[fi - 2][0] in ("w1q", "w2q"):
                        pr.wait(E, SEM["p0s"], ("p0fillq", fi - 2))
                    else:
                        pr.wait(E, SEM["p0v"], ("p0fill", fi - 2))
                    ncols = fill_cols(kind)
                    if pr.emit:
                        if kind == "bt":
                            src = btf[c * 128:(c + 1) * 128, :]
                        elif kind == "ct":
                            src = ctf[c * 128:(c + 1) * 128, :]
                        elif kind in ("w1e", "w1r", "w1q"):
                            src = w1t[c * 128:(c + 1) * 128, :]
                        else:
                            src = (w2t[c * 512:(c + 1) * 512, :]
                                   .rearrange("(cc p) d -> p cc d", p=128))
                        dstp = stg0[fi % 2][:, 0:ncols]
                        if kind in ("w2e", "w2r", "w2q"):
                            dstp = dstp.rearrange("p (cc d) -> p cc d", cc=4)
                        E.dma_start(out=dstp, in_=src).then_inc(SEM[f"p0ld{fi % 2}"], 16)
                    pr.inc(f"p0ld{fi % 2}", 16)
                    pr.mark(("p0ld", fi), f"p0ld{fi % 2}")
                barrier_end(E)

            def g0_vec(E):
                si = 0
                vq = 0

                def _vrow(fn):
                    if pr.emit:
                        fn().then_inc(SEM["p0v"], 1)
                    pr.inc("p0v", 1)
                    pr.selfsync(E, SEM["p0v"], "p0v")
                for fi, (kind, c) in enumerate(fills):
                    pr.wait(E, SEM[f"p0ld{fi % 2}"], ("p0ld", fi))
                    stg = stg0[fi % 2]
                    if kind in ("bt", "ct"):
                        ncols = fill_cols(kind)
                        pr.wait(E, SEM[f"p0st{si % 2}"], ("p0st", si - 2))
                        if pr.emit:
                            E.tensor_copy(out=wq0[si % 2][:, 0:ncols],
                                          in_=stg[:, 0:ncols]).then_inc(SEM["p0v"], 1)
                        pr.inc("p0v", 1)
                        pr.mark(("p0ready", si), "p0v")
                        si += 1
                    elif kind in ("w1e", "w2e"):
                        continue
                    elif kind in ("w1r", "w2r"):
                        mat = 1 if kind == "w1r" else 2
                        m64 = m164 if mat == 1 else m264
                        mhe = acc1 if mat == 1 else mh2
                        estc = estcol1 if mat == 1 else estcol2
                        rc = racc1 if mat == 1 else racc
                        if c == 0:
                            pr.wait(E, SEM["p0g"], f"p0ge{mat}")
                            _vrow(lambda: E.tensor_scalar_mul(
                                out=mhe, in0=estc, scalar1=2.0 ** -20))
                            _vrow(lambda: E.tensor_scalar_mul(
                                out=m64, in0=estc, scalar1=2.0 ** -14))
                        pr.wait(E, SEM["p0s"], ("p0abs", (mat, c)))
                        if pr.emit:
                            E.reduce_sum(out=rseg[:], in_=sabs[:]
                                         .rearrange("p (t s) -> p t s", s=64),
                                         axis=AX.X).then_inc(SEM["p0v"], 1)
                        pr.inc("p0v", 1)
                        pr.selfsync(E, SEM["p0v"], "p0v")
                        pr.mark(("p0resc", (mat, c)), "p0v")
                        _vrow(lambda: E.tensor_scalar(
                            out=rseg[:], in0=rseg[:], scalar1=m64,
                            scalar2=None, op0=OP.subtract))
                        _vrow(lambda: E.reduce_sum(out=redc, in_=rseg[:],
                                                   axis=AX.X))
                        _vrow(lambda: E.tensor_copy(out=rc[:, c:c + 1],
                                                    in_=redc))
                    else:
                        if kind == "w1q" and c == 0:
                            _vrow(lambda: E.reduce_sum(out=rsum1, in_=racc1[:],
                                                       axis=AX.X))
                            if pr.emit:
                                E.reduce_sum(out=rsum, in_=racc[:], axis=AX.X
                                             ).then_inc(SEM["p0v"], 1)
                            pr.inc("p0v", 1)
                            pr.selfsync(E, SEM["p0v"], "p0v")
                            pr.mark("p0rsum_pre", "p0v")
                            pr.wait(E, SEM["p0g"], "p0par3")

                            def _rowop(fn):
                                if pr.emit:
                                    fn().then_inc(SEM["p0v"], 1)
                                pr.inc("p0v", 1)
                                pr.selfsync(E, SEM["p0v"], "p0v")

                            _rowop(lambda: E.tensor_scalar(
                                out=mw2, in0=rsum, scalar1=1.0 / WN,
                                scalar2=mh2, op0=OP.mult, op1=OP.add))
                            _rowop(lambda: E.tensor_scalar_max(
                                out=mw2, in0=mw2, scalar1=EPS))
                            _rowop(lambda: E.tensor_scalar(
                                out=mw1, in0=rsum1, scalar1=1.0 / WN,
                                scalar2=acc1, op0=OP.mult, op1=OP.add))
                            _rowop(lambda: E.tensor_scalar_max(
                                out=mw1, in0=mw1, scalar1=EPS))
                            for mm_, ss_, nn_ in ((mw1, s1w, ns1w),
                                                  (mw2, s2w, ns2w)):
                                _rowop(lambda mm_=mm_, ss_=ss_:
                                       E.reciprocal(out=ss_, in_=mm_))
                                _rowop(lambda mm_=mm_, ss_=ss_:
                                       E.tensor_mul(out=rtmp, in0=mm_, in1=ss_))
                                _rowop(lambda: E.tensor_scalar(
                                    out=rtmp, in0=rtmp, scalar1=-1.0,
                                    scalar2=1.0, op0=OP.mult, op1=OP.add))
                                _rowop(lambda ss_=ss_:
                                       E.tensor_mul(out=rtmp, in0=rtmp, in1=ss_))
                                _rowop(lambda ss_=ss_:
                                       E.tensor_add(out=ss_, in0=ss_, in1=rtmp))
                                _rowop(lambda ss_=ss_, nn_=nn_:
                                       E.tensor_scalar_mul(out=nn_, in0=ss_,
                                                           scalar1=-1.0))
                        sw = s1w if kind == "w1q" else s2w
                        pr.wait(E, SEM[f"p0st{si % 2}"], ("p0st", si - 2))
                        for half in range(2):
                            u = 2 * vq + half
                            hs = slice(half * (HID // 2), (half + 1) * (HID // 2))
                            sabsh = sabs[:, hs]
                            sg = sa0 if half == 0 else sb0
                            pr.wait(E, SEM["p0s"], ("p0qs", u))
                            if pr.emit:
                                E.tensor_scalar(out=sabsh, in0=sabsh,
                                                scalar1=sw, scalar2=0.5,
                                                op0=OP.mult, op1=OP.is_gt)
                                E.tensor_mul(out=wq0[si % 2][:, hs],
                                             in0=sabsh, in1=sg[:]
                                             ).then_inc(SEM["p0v"], 2)
                            pr.inc("p0v", 2)
                            pr.mark(("p0qv", u), "p0v")
                        pr.mark(("p0ready", si), "p0v")
                        si += 1
                        vq += 1
                    pr.mark(("p0fill", fi), "p0v")
                barrier_end(E)

            def g0_gp(E):
                if pr.emit:
                    E.load_library(_q7lib)
                si = 0
                for fi, (kind, c) in enumerate(fills):
                    if kind == "w1r" and c == 0:
                        pr.wait(E, SEM["p0s"], "p0e1s")
                        if pr.emit:
                            E.partition_all_reduce(estcol1, estcol1, 128,
                                                   ReduceOp.add
                                                   ).then_inc(SEM["p0g"], 1)
                        pr.inc("p0g", 1)
                        pr.mark("p0ge1", "p0g")
                    if kind == "w2r" and c == 0:
                        pr.wait(E, SEM["p0s"], "p0e2s")
                        if pr.emit:
                            E.partition_all_reduce(estcol2, estcol2, 128,
                                                   ReduceOp.add
                                                   ).then_inc(SEM["p0g"], 1)
                        pr.inc("p0g", 1)
                        pr.mark("p0ge2", "p0g")
                    if kind == "w1q" and c == 0:
                        pr.wait(E, SEM["p0v"], "p0rsum_pre")
                        if pr.emit:
                            E.partition_all_reduce(rsum1, rsum1, 128,
                                                   ReduceOp.add)
                            E.partition_all_reduce(rsum, rsum, 128, ReduceOp.add
                                                   ).then_inc(SEM["p0g"], 2)
                        pr.inc("p0g", 2)
                        pr.mark("p0par3", "p0g")
                    if kind not in store_kinds:
                        continue
                    pr.wait(E, SEM["p0v"], ("p0ready", si))
                    src = wq0[si % 2]
                    if kind == "bt":
                        dsts = [(btq_d[:, :, c, :].transpose([1, 0, 2]),
                                 src[:, 0:NS].rearrange("p (n m) -> p n m", m=128))]
                    elif kind == "ct":
                        dsts = [(ctq_d[j * 8:(j + 1) * 8, :, c, :].transpose([1, 0, 2]),
                                 src[:, j * 1024:(j + 1) * 1024]
                                 .rearrange("p (d m) -> p d m", m=128))
                                for j in range(2)]
                    elif kind == "w1q":
                        dsts = [(w1q_d[j * 8:(j + 1) * 8, :, c, :].transpose([1, 0, 2]),
                                 src[:, j * 1024:(j + 1) * 1024]
                                 .rearrange("p (h m) -> p h m", m=128))
                                for j in range(8)]
                    else:
                        dsts = [(w2q_d[c * 512:(c + 1) * 512, :]
                                 .rearrange("(cc p) d -> p cc d", p=128),
                                 src[:, 0:HID].rearrange("p (cc d) -> p cc d", cc=4))]
                    for dst, s_ in dsts:
                        if pr.emit:
                            E.dma_start(out=dst, in_=s_).then_inc(SEM[f"p0st{si % 2}"], 16)
                        pr.inc(f"p0st{si % 2}", 16)
                    pr.mark(("p0st", si), f"p0st{si % 2}")
                    si += 1
                pr.wait(E, SEM["p0st0"], ("p0st", n_stores - 2))
                pr.wait(E, SEM["p0st1"], ("p0st", n_stores - 1))
                barrier_end(E)

            def g0_scal(E):
                nsc = 0
                sq = 0
                for fi, (kind, c) in enumerate(fills):
                    if kind in ("w1e", "w2e"):
                        mat = 1 if kind == "w1e" else 2
                        estc = estcol1 if mat == 1 else estcol2
                        pr.wait(E, SEM[f"p0ld{fi % 2}"], ("p0ld", fi))
                        if mat == 2:
                            pr.wait(E, SEM["p0v"], ("p0resc", (1, 15)))
                        if pr.emit:
                            E.activation(out=sabs[:], in_=stg0[fi % 2][:],
                                         func=AF.Abs, accum_out=estc
                                         ).then_inc(SEM["p0s"], 1)
                        pr.inc("p0s", 1)
                        pr.mark(f"p0e{mat}s", "p0s")
                        pr.mark(("p0fillR", fi), "p0s")
                    elif kind in ("w1r", "w2r"):
                        mat = 1 if kind == "w1r" else 2
                        pr.wait(E, SEM[f"p0ld{fi % 2}"], ("p0ld", fi))
                        pr.wait(E, SEM["p0v"], ("p0resc", (mat, c - 1)))
                        if pr.emit:
                            E.activation(out=sabs[:], in_=stg0[fi % 2][:],
                                         func=AF.Abs).then_inc(SEM["p0s"], 1)
                        pr.inc("p0s", 1)
                        pr.mark(("p0abs", (mat, c)), "p0s")
                        pr.mark(("p0fillR", fi), "p0s")
                    elif kind in ("w1q", "w2q"):
                        pr.wait(E, SEM[f"p0ld{fi % 2}"], ("p0ld", fi))
                        for half in range(2):
                            u = 2 * sq + half
                            hs = slice(half * (HID // 2),
                                       (half + 1) * (HID // 2))
                            sabsh = sabs[:, hs]
                            sg = sa0 if half == 0 else sb0
                            pr.wait(E, SEM["p0v"], ("p0qv", u - 2))
                            if u < 2:
                                pr.wait(E, SEM["p0v"], ("p0resc", (2, 15)))
                            if pr.emit:
                                E.activation(out=sabsh,
                                             in_=stg0[fi % 2][:, hs],
                                             func=AF.Abs)
                                E.activation(out=sg[:],
                                             in_=stg0[fi % 2][:, hs],
                                             func=AF.Sign
                                             ).then_inc(SEM["p0s"], 2)
                            pr.inc("p0s", 2)
                            pr.mark(("p0qs", u), "p0s")
                        pr.mark(("p0fillq", fi), "p0s")
                        sq += 1
                barrier_end(E)

            def run_p0(block):
                if block is None:
                    g0_sync(NULL); g0_vec(NULL); g0_scal(NULL); g0_gp(NULL)
                else:
                    block.sync(lambda e: g0_sync(e))
                    block.vector(lambda e: g0_vec(e))
                    block.scalar(lambda e: g0_scal(e))
                    block.gpsimd(lambda e: g0_gp(e))

            if 0 in phases:
                phase(run_p0, 4)

        with ExitStack() as p1:
            sxb = [p1.enter_context(nc.sbuf_tensor(f"sx{j}", [128, NCD, T1], F32))
                   for j in range(2)]
            sxnb = p1.enter_context(nc.sbuf_tensor("sxnb", [128, NCD * T1], BF16))
            sxqd = [p1.enter_context(nc.sbuf_tensor(f"sxqd{j}", [128, T1], BF16))
                    for j in range(2)]
            acc_tm = [p1.enter_context(nc.sbuf_tensor(f"acctm{j}", [128, DM], F32))
                      for j in range(4)]
            btbuf = [p1.enter_context(nc.sbuf_tensor(f"btbuf{j}", [128, NCD * 128], BF16))
                     for j in range(2)]
            ctbuf = [p1.enter_context(nc.sbuf_tensor(f"ctbuf{j}", [128, NCS * 128], BF16))
                     for j in range(2)]
            st_bf = p1.enter_context(nc.sbuf_tensor("st_bf", [128, NCS * T1], BF16))
            accf = [p1.enter_context(nc.sbuf_tensor(f"accf{j}", [128, T1], F32))
                    for j in range(2)]
            rows = p1.enter_context(nc.sbuf_tensor("rows1", [128, 11, T1], F32))
            (r_sums, r_sumsq, r_xmax, r_nxmin, r_m0, r_m1, r_rstd0, r_rstd1,
             r_sxs, r_A, r_B) = [rows[:, j, :] for j in range(11)]
            r_C = r_B
            sqc = [p1.enter_context(nc.sbuf_tensor(f"sqc{j}", [128, T1], F32))
                   for j in range(2)]
            xnscr = [p1.enter_context(nc.sbuf_tensor(f"xnscr{j}", [128, T1], F32))
                     for j in range(2)]
            qfscr = p1.enter_context(nc.sbuf_tensor("qfscr", [128, T1], F32))
            qiscr = p1.enter_context(nc.sbuf_tensor("qiscr", [128, T1], I32))
            xnev = p1.enter_context(nc.sbuf_tensor("xnev", [128, T1], F32))
            psum1 = [p1.enter_context(nc.psum_tensor(f"ps1{j}", [128, T1], F32))
                     for j in range(2)]
            psum2 = [p1.enter_context(nc.psum_tensor(f"ps2{j}", [128, T1], F32))
                     for j in range(2)]
            psumtr_full = [p1.enter_context(nc.psum_tensor(f"pstr{j}", [128, 512], F32))
                           for j in range(2)]
            psumtr = [t[:, 0:128] for t in psumtr_full]

            def w2k(i, j, J):
                return (i, j) if j >= 0 else (i - 1, J + j)

            def _p1_xload(E, i):
                if i >= NT1:
                    return
                pr.wait(E, SEM["v1"], ("xdone", i - 2))
                for hh in range(2):
                    if pr.emit:
                        src = (x_fm.rearrange("(c p) t -> p c t", p=128)
                               [:, hh * 8:(hh + 1) * 8, i * T1:(i + 1) * T1])
                        E.dma_start(out=sxb[i % 2][:, hh * 8:(hh + 1) * 8, :],
                                    in_=src).then_inc(SEM[f"ldx{i % 2}"], 16)
                    pr.inc(f"ldx{i % 2}", 16)
                pr.mark(("ldx", i), f"ldx{i % 2}")

            def g1_sync(E):
                barrier_wait(E)
                _p1_xload(E, 0)
                _p1_xload(E, 1)
                for i in range(NT1):
                    for n in range(NCS):
                        pr.wait(E, SEM["t1"], ("mm1grp",) + w2k(i, n - 2, NCS))
                        if pr.emit:
                            E.dma_start(out=btbuf[n % 2][:],
                                        in_=btq_d[n].rearrange("p k m -> p (k m)")
                                        ).then_inc(SEM[f"ldbt{n % 2}"], 16)
                        pr.inc(f"ldbt{n % 2}", 16)
                        pr.mark(("ldbt", i, n), f"ldbt{n % 2}")
                    for dc in range(NCD):
                        pr.wait(E, SEM["t1"], ("mm2grp",) + w2k(i, dc - 2, NCD))
                        if pr.emit:
                            E.dma_start(out=ctbuf[dc % 2][:],
                                        in_=ctq_d[dc].rearrange("p s m -> p (s m)")
                                        ).then_inc(SEM[f"ldct{dc % 2}"], 16)
                        pr.inc(f"ldct{dc % 2}", 16)
                        pr.mark(("ldct", i, dc), f"ldct{dc % 2}")
                    _p1_xload(E, i + 2)
                barrier_end(E)

            def _v_front(E, i):
                sx = sxb[i % 2]
                r_m = r_m0 if i % 2 == 0 else r_m1
                r_rstd = r_rstd0 if i % 2 == 0 else r_rstd1
                pr.wait(E, SEM[f"ldx{i % 2}"], ("ldx", i))
                if pr.emit:
                    E.reduce_sum(out=r_sums, in_=sx[:].transpose([0, 2, 1]),
                                 axis=AX.X)
                    E.reduce_max(out=r_xmax, in_=sx[:].transpose([0, 2, 1]),
                                 axis=AX.X)
                    E.tensor_reduce(out=r_C, in_=sx[:].transpose([0, 2, 1]),
                                    axis=AX.X, op=OP.min)
                    E.tensor_scalar_mul(out=r_nxmin, in0=r_C, scalar1=-1.0
                                        ).then_inc(SEM["v1"], 4)
                pr.inc("v1", 4)
                for c in range(NCD):
                    pr.wait(E, SEM["s1"], ("sq", i, c))
                    if pr.emit:
                        if c == 0:
                            E.tensor_copy(out=r_sumsq, in_=sqc[0][:]
                                          ).then_inc(SEM["v1"], 1)
                        else:
                            E.tensor_add(out=r_sumsq, in0=r_sumsq, in1=sqc[c % 2][:]
                                         ).then_inc(SEM["v1"], 1)
                    pr.inc("v1", 1)
                    pr.mark(("sqdone", i, c), "v1")
                pr.mark(("stats", i), "v1")

            def _v_rest(E, i):
                sx = sxb[i % 2]
                r_m = r_m0 if i % 2 == 0 else r_m1
                r_rstd = r_rstd0 if i % 2 == 0 else r_rstd1
                pr.wait(E, SEM["g1"], ("par", i))
                if pr.emit:
                    E.tensor_scalar_mul(out=r_m, in0=r_sums, scalar1=1.0 / DM)
                    E.tensor_scalar_mul(out=r_A, in0=r_sumsq, scalar1=1.0 / DM)
                    E.tensor_mul(out=r_B, in0=r_m, in1=r_m)
                    E.tensor_sub(out=r_A, in0=r_A, in1=r_B)
                    E.tensor_scalar_add(out=r_A, in0=r_A, scalar1=EPS
                                        ).then_inc(SEM["v1"], 4)
                pr.inc("v1", 4)
                pr.mark(("var", i), "v1")
                pr.wait(E, SEM["s1"], ("sd", i))
                if pr.emit:
                    E.reciprocal(out=r_rstd, in_=r_B)
                    E.tensor_sub(out=r_xmax, in0=r_xmax, in1=r_m)
                    E.tensor_add(out=r_nxmin, in0=r_nxmin, in1=r_m)
                    E.tensor_max(out=r_xmax, in0=r_xmax, in1=r_nxmin)
                    E.tensor_mul(out=r_xmax, in0=r_xmax, in1=r_rstd)
                    E.tensor_scalar_max(out=r_xmax, in0=r_xmax, scalar1=EPS)
                    E.reciprocal(out=r_B, in_=r_xmax)
                    E.tensor_scalar_mul(out=r_sxs, in0=r_B, scalar1=127.0)
                    E.tensor_scalar(out=c1_full[:, i * T1:(i + 1) * T1],
                                    in0=r_xmax, scalar1=mw1, scalar2=1.0 / 127.0,
                                    op0=OP.mult, op1=OP.mult
                                    ).then_inc(SEM["v1"], 9)
                pr.inc("v1", 9)
                for c in range(NCD):
                    _k = w2k(i, c - 2, NCD)
                    pr.wait(E, SEM[f"stxq{c % 2}"], ("stxq",) + _k)
                    pr.wait(E, SEM["s1"], ("xnb",) + w2k(i, c - 2, NCD))
                    if pr.emit:
                        E.tensor_sub(out=xnscr[c % 2][:], in0=sx[:, c, :], in1=r_m)
                        E.tensor_mul(out=xnscr[c % 2][:], in0=xnscr[c % 2][:],
                                     in1=r_rstd).then_inc(SEM["v1"], 2)
                    pr.inc("v1", 2)
                    pr.mark(("xn", i, c), "v1")
                    if pr.emit:
                        E.tensor_mul(out=qfscr[:], in0=xnscr[c % 2][:], in1=r_sxs)
                        E.tensor_copy(out=qiscr[:], in_=qfscr[:])
                        E.tensor_scalar(out=sxqd[c % 2][:], in0=qiscr[:],
                                        scalar1=0.0, scalar2=None, op0=OP.add
                                        ).then_inc(SEM["v1"], 3)
                    pr.inc("v1", 3)
                    pr.mark(("xq", i, c), "v1")

            def _v_evac(E, i):
                sx = sxb[i % 2]
                r_m = r_m0 if i % 2 == 0 else r_m1
                r_rstd = r_rstd0 if i % 2 == 0 else r_rstd1
                for dc in range(NCD):
                    pr.wait(E, SEM["t1"], ("mm2grp", i, dc))
                    pr.wait(E, SEM["t1"], ("tr",) + w2k(i, dc - 2, NCD))
                    if pr.emit:
                        E.tensor_sub(out=xnev[:], in0=sx[:, dc, :], in1=r_m)
                        E.tensor_mul(out=xnev[:], in0=xnev[:], in1=r_rstd)
                        E.scalar_tensor_tensor(out=xnev[:], in0=xnev[:],
                                               scalar=sD[:, dc:dc + 1],
                                               in1=sx[:, dc, :],
                                               op0=OP.mult, op1=OP.add)
                        E.tensor_add(out=accf[dc % 2][:], in0=xnev[:],
                                     in1=psum2[dc % 2][:]).then_inc(SEM["v1"], 4)
                    pr.inc("v1", 4)
                    pr.mark(("accf", i, dc), "v1")
                pr.mark(("xdone", i), "v1")

            def g1_vec(E):
                barrier_wait(E)
                pr.wait(E, SEM["p0misc"], "p0misc_all")
                for i in range(NT1):
                    _v_front(E, i)
                    if i >= 1:
                        _v_evac(E, i - 1)
                    _v_rest(E, i)
                _v_evac(E, NT1 - 1)
                barrier_end(E)

            def _s_sq(E, i):
                sx = sxb[i % 2]
                for c in range(NCD):
                    pr.wait(E, SEM[f"ldx{i % 2}"], ("ldx", i))
                    pr.wait(E, SEM["v1"], ("sqdone",) + w2k(i, c - 2, NCD))
                    if pr.emit:
                        E.activation(out=sqc[c % 2][:], in_=sx[:, c, :],
                                     func=AF.Square).then_inc(SEM["s1"], 1)
                    pr.inc("s1", 1)
                    pr.mark(("sq", i, c), "s1")

            def _s_sd(E, i):
                pr.wait(E, SEM["v1"], ("var", i))
                if pr.emit:
                    E.activation(out=r_B, in_=r_A, func=AF.Sqrt
                                 ).then_inc(SEM["s1"], 1)
                pr.inc("s1", 1)
                pr.mark(("sd", i), "s1")

            def _s_tanh(E, i):
                for n in range(NCS):
                    pr.wait(E, SEM["t1"], ("mm1grp", i, n))
                    if n == 0:
                        pr.wait(E, SEM["t1"], ("mm2grp", i - 1, NCD - 1))
                    if pr.emit:
                        E.activation(out=st_bf[:, n * T1:(n + 1) * T1],
                                     in_=psum1[n % 2][:], func=AF.Tanh
                                     ).then_inc(SEM["s1"], 1)
                    pr.inc("s1", 1)
                    pr.mark(("tanh", i, n), "s1")

            def _s_xnb(E, i):
                for c in range(NCD):
                    pr.wait(E, SEM["v1"], ("xn", i, c))
                    if c == 0:
                        pr.wait(E, SEM["t1"], ("mm1grp", i - 1, NCS - 1))
                    if pr.emit:
                        E.activation(out=sxnb[:, c * T1:(c + 1) * T1],
                                     in_=xnscr[c % 2][:], func=AF.Copy
                                     ).then_inc(SEM["s1"], 1)
                    pr.inc("s1", 1)
                    pr.mark(("xnb", i, c), "s1")

            def _s_evacs(E, i):
                pr.wait(E, SEM["stacc"], ("stacc", i - 1))
                for dc in range(NCD):
                    for tc in range(4):
                        lin = (i * NCD + dc) * 4 + tc
                        pr.wait(E, SEM["t1"], ("trtc", lin))
                        if pr.emit:
                            E.copy(out=acc_tm[tc][:, dc * 128:(dc + 1) * 128],
                                   in_=psumtr[tc % 2]).then_inc(SEM["s1"], 1)
                        pr.inc("s1", 1)
                        pr.mark(("sevtc", lin), "s1")
                pr.mark(("sevac_all", i), "s1")

            def g1_scal(E):
                barrier_wait(E)
                for i in range(NT1):
                    _s_sq(E, i)
                    if i >= 1:
                        _s_tanh(E, i - 1)
                        _s_evacs(E, i - 1)
                    _s_sd(E, i)
                    _s_xnb(E, i)
                _s_tanh(E, NT1 - 1)
                _s_evacs(E, NT1 - 1)
                barrier_end(E)

            def _tr_group(E, i, dc):
                pr.wait(E, SEM["v1"], ("accf", i, dc))
                for tc in range(4):
                    lin = (i * NCD + dc) * 4 + tc
                    pr.wait(E, SEM["s1"], ("sevtc", lin - 2))
                    if pr.emit:
                        E.transpose(psumtr[tc % 2],
                                    accf[dc % 2][:, tc * 128:(tc + 1) * 128],
                                    seye[:]).then_inc(SEM["t1"], 1)
                    pr.inc("t1", 1)
                    pr.mark(("trtc", lin), "t1")
                pr.mark(("tr", i, dc), "t1")

            def g1_tens(E):
                barrier_wait(E)
                pr.wait(E, SEM["p0misc"], "p0misc_all")
                for i in range(NT1):
                    pr.wait(E, SEM["s1"], ("xnb", i, NCD - 1))
                    for n in range(NCS):
                        pr.wait(E, SEM[f"ldbt{n % 2}"], ("ldbt", i, n))
                        pr.wait(E, SEM["s1"], ("tanh",) + w2k(i, n - 2, NCS))
                        for kc in range(NCD):
                            if pr.emit:
                                mm = E.matmul(psum1[n % 2][:],
                                              btbuf[n % 2][:, kc * 128:(kc + 1) * 128],
                                              sxnb[:, kc * T1:(kc + 1) * T1],
                                              start=(kc == 0), stop=(kc == NCD - 1))
                                if kc == NCD - 1:
                                    mm.then_inc(SEM["t1"], 1)
                        pr.inc("t1", 1)
                        pr.mark(("mm1grp", i, n), "t1")
                    pr.wait(E, SEM["s1"], ("tanh", i, NCS - 1))
                    for dc in range(NCD):
                        pr.wait(E, SEM[f"ldct{dc % 2}"], ("ldct", i, dc))
                        pr.wait(E, SEM["v1"], ("accf",) + w2k(i, dc - 2, NCD))
                        for sc in range(NCS):
                            if pr.emit:
                                mm = E.matmul(psum2[dc % 2][:],
                                              ctbuf[dc % 2][:, sc * 128:(sc + 1) * 128],
                                              st_bf[:, sc * T1:(sc + 1) * T1],
                                              start=(sc == 0), stop=(sc == NCS - 1))
                                if sc == NCS - 1:
                                    mm.then_inc(SEM["t1"], 1)
                        pr.inc("t1", 1)
                        pr.mark(("mm2grp", i, dc), "t1")
                        if dc >= 1:
                            _tr_group(E, i, dc - 1)
                    _tr_group(E, i, NCD - 1)
                barrier_end(E)

            def g1_gp(E):
                barrier_wait(E)
                for i in range(NT1):
                    pr.wait(E, SEM["v1"], ("stats", i))
                    if pr.emit:
                        E.partition_all_reduce(rows[:, 0:2, :].rearrange("p a t -> p (a t)"),
                                               rows[:, 0:2, :].rearrange("p a t -> p (a t)"),
                                               128, ReduceOp.add)
                        E.partition_all_reduce(rows[:, 2:4, :].rearrange("p a t -> p (a t)"),
                                               rows[:, 2:4, :].rearrange("p a t -> p (a t)"),
                                               128, ReduceOp.max
                                               ).then_inc(SEM["g1"], 4)
                    pr.inc("g1", 4)
                    pr.mark(("par", i), "g1")
                    for c in range(NCD):
                        pr.wait(E, SEM["v1"], ("xq", i, c))
                        if pr.emit:
                            E.dma_start(out=xq_d[c * 128:(c + 1) * 128,
                                                 i * T1:(i + 1) * T1],
                                        in_=sxqd[c % 2][:]
                                        ).then_inc(SEM[f"stxq{c % 2}"], 16)
                        pr.inc(f"stxq{c % 2}", 16)
                        pr.mark(("stxq", i, c), f"stxq{c % 2}")
                    if i >= 1:
                        _gp_acc(E, i - 1)
                _gp_acc(E, NT1 - 1)
                pr.wait(E, SEM["stxq0"], ("stxq", NT1 - 1, NCD - 2))
                pr.wait(E, SEM["stxq1"], ("stxq", NT1 - 1, NCD - 1))
                pr.wait(E, SEM["stacc"], ("stacc", NT1 - 1))
                barrier_end(E)

            def _gp_acc(E, i):
                pr.wait(E, SEM["s1"], ("sevac_all", i))
                for tc in range(4):
                    if pr.emit:
                        E.dma_start(out=acc_d[i * T1 + tc * 128:
                                              i * T1 + (tc + 1) * 128, :],
                                    in_=acc_tm[tc][:]).then_inc(SEM["stacc"], 16)
                    pr.inc("stacc", 16)
                pr.mark(("stacc", i), "stacc")

            def run_p1(block):
                if block is None:
                    g1_vec(NULL); g1_scal(NULL); g1_tens(NULL)
                    g1_gp(NULL); g1_sync(NULL)
                else:
                    block.vector(lambda e: g1_vec(e))
                    block.scalar(lambda e: g1_scal(e))
                    block.tensor(lambda e: g1_tens(e))
                    block.gpsimd(lambda e: g1_gp(e))
                    block.sync(lambda e: g1_sync(e))

            if 1 in phases:
                phase(run_p1, 5)

        with ExitStack() as p2:
            sxqh = p2.enter_context(nc.sbuf_tensor("sxqh", [128, NCD, TH], BF16))
            w1buf = [p2.enter_context(nc.sbuf_tensor(f"w1b{j}", [128, NCD * 128], BF16))
                     for j in range(2)]
            gtmp = [p2.enter_context(nc.sbuf_tensor(f"gtmp{j}", [128, TH], F32))
                    for j in range(2)]
            gbuf = [p2.enter_context(nc.sbuf_tensor(f"gbuf{j}", [128, TH], F32))
                    for j in range(2)]
            gabs = [p2.enter_context(nc.sbuf_tensor(f"gabs{j}", [128, TH], F32))
                    for j in range(2)]
            macc = p2.enter_context(nc.sbuf_tensor("macc", [128, TH], F32))
            c3row = p2.enter_context(nc.sbuf_tensor("c3row", [128, TH], F32))
            p2ps = [p2.enter_context(nc.psum_tensor(f"p2ps{j}", [128, TH], F32))
                    for j in range(2)]

            def g2_sync(E):
                barrier_wait(E)
                for h in range(NTH):
                    pr.wait(E, SEM["t2"], ("mmh", h * NCH - 1))
                    for hh in range(2):
                        if pr.emit:
                            src = (xq_d[:, h * TH:(h + 1) * TH]
                                   .rearrange("(c p) t -> p c t", p=128)
                                   [:, hh * 8:(hh + 1) * 8, :])
                            E.dma_start(out=sxqh[:, hh * 8:(hh + 1) * 8, :], in_=src
                                        ).then_inc(SEM["ldxq"], 16)
                        pr.inc("ldxq", 16)
                    pr.mark(("ldxq", h), "ldxq")
                    for hc in range(NCH):
                        k = h * NCH + hc
                        pr.wait(E, SEM["t2"], ("mmh", k - 2))
                        if pr.emit:
                            E.dma_start(out=w1buf[k % 2][:],
                                        in_=w1q_d[hc].rearrange("p k m -> p (k m)")
                                        ).then_inc(SEM[f"ldw1_{k % 2}"], 16)
                        pr.inc(f"ldw1_{k % 2}", 16)
                        pr.mark(("ldw1", k), f"ldw1_{k % 2}")
                barrier_end(E)

            def g2_tens(E):
                barrier_wait(E)
                for h in range(NTH):
                    pr.wait(E, SEM["ldxq"], ("ldxq", h))
                    for hc in range(NCH):
                        k = h * NCH + hc
                        pr.wait(E, SEM[f"ldw1_{k % 2}"], ("ldw1", k))
                        pr.wait(E, SEM["v2"], ("evacmul", k - 2))
                        for kc in range(NCD):
                            for tc in range(P2TC):
                                if pr.emit:
                                    mm = E.matmul(
                                        p2ps[k % 2][:, tc * 512:(tc + 1) * 512],
                                        w1buf[k % 2][:, kc * 128:(kc + 1) * 128],
                                        sxqh[:, kc, tc * 512:(tc + 1) * 512],
                                        start=(kc == 0), stop=(kc == NCD - 1),
                                        skip_group_check=True)
                                    if kc == NCD - 1 and tc == P2TC - 1:
                                        mm.then_inc(SEM["t2"], 1)
                        pr.inc("t2", 1)
                        pr.mark(("mmh", k), "t2")
                barrier_end(E)

            def _macc(E, k):
                pr.wait(E, SEM["s2"], ("absd", k))
                if pr.emit:
                    E.tensor_max(out=macc[:], in0=macc[:], in1=gabs[k % 2][:]
                                 ).then_inc(SEM["v2"], 1)
                pr.inc("v2", 1)
                pr.mark(("macc", k), "v2")

            def g2_vec(E):
                barrier_wait(E)
                for h in range(NTH):
                    if pr.emit:
                        E.memset(macc[:], 0.0).then_inc(SEM["v2"], 1)
                    pr.inc("v2", 1)
                    for hc in range(NCH):
                        k = h * NCH + hc
                        pr.wait(E, SEM["t2"], ("mmh", k))
                        pr.wait(E, SEM["s2"], ("gelu", k - 2))
                        for tc in range(P2TC):
                            if pr.emit:
                                mm = E.tensor_mul(
                                    out=gtmp[k % 2][:, tc * 512:(tc + 1) * 512],
                                    in0=p2ps[k % 2][:, tc * 512:(tc + 1) * 512],
                                    in1=c1_full[:, h * TH + tc * 512:
                                                h * TH + (tc + 1) * 512])
                                if tc == P2TC - 1:
                                    mm.then_inc(SEM["v2"], 1)
                        pr.inc("v2", 1)
                        pr.mark(("evacmul", k), "v2")
                        if hc >= 1:
                            _macc(E, k - 1)
                    _macc(E, h * NCH + NCH - 1)
                    pr.mark(("maccdone", h), "v2")
                    pr.wait(E, SEM["g2"], ("parh", h))
                    if pr.emit:
                        E.tensor_scalar_max(out=macc[:], in0=macc[:], scalar1=EPS)
                        E.reciprocal(out=c3row[:], in_=macc[:])
                        E.tensor_scalar_mul(out=s2bc[:, h * TH:(h + 1) * TH],
                                            in0=c3row[:], scalar1=127.0)
                        E.tensor_scalar(out=c3row[:], in0=macc[:], scalar1=mw2,
                                        scalar2=1.0 / 127.0, op0=OP.mult,
                                        op1=OP.mult).then_inc(SEM["v2"], 4)
                    pr.inc("v2", 4)
                    pr.mark(("c3row", h), "v2")
                barrier_end(E)

            def g2_scal(E):
                barrier_wait(E)
                for h in range(NTH):
                    for hc in range(NCH):
                        k = h * NCH + hc
                        pr.wait(E, SEM["v2"], ("evacmul", k))
                        pr.wait(E, SEM[f"stg{k % 2}"], ("stg", k - 2))
                        if pr.emit:
                            E.activation(out=gbuf[k % 2][:], in_=gtmp[k % 2][:],
                                         func=AF.Gelu).then_inc(SEM["s2"], 1)
                        pr.inc("s2", 1)
                        pr.mark(("gelu", k), "s2")
                        pr.wait(E, SEM["v2"], ("macc", k - 2))
                        if pr.emit:
                            E.activation(out=gabs[k % 2][:], in_=gbuf[k % 2][:],
                                         func=AF.Abs).then_inc(SEM["s2"], 1)
                        pr.inc("s2", 1)
                        pr.mark(("absd", k), "s2")
                barrier_end(E)

            def g2_gp(E):
                barrier_wait(E)
                for h in range(NTH):
                    for hc in range(NCH):
                        k = h * NCH + hc
                        pr.wait(E, SEM["s2"], ("gelu", k))
                        if pr.emit:
                            E.dma_start(out=g_d[h, hc], in_=gbuf[k % 2][:]
                                        ).then_inc(SEM[f"stg{k % 2}"], 16)
                        pr.inc(f"stg{k % 2}", 16)
                        pr.mark(("stg", k), f"stg{k % 2}")
                    pr.wait(E, SEM["v2"], ("maccdone", h))
                    if pr.emit:
                        E.partition_all_reduce(macc[:], macc[:], 128, ReduceOp.absmax
                                               ).then_inc(SEM["g2"], 1)
                    pr.inc("g2", 1)
                    pr.mark(("parh", h), "g2")
                    pr.wait(E, SEM["v2"], ("c3row", h))
                    if pr.emit:
                        E.dma_start(out=scl_d[h * TH:(h + 1) * TH],
                                    in_=c3row[0:1, :]).then_inc(SEM["stscl"], 16)
                    pr.inc("stscl", 16)
                pr.mark("stscl_all", "stscl")
                pr.wait(E, SEM["stscl"], "stscl_all")
                pr.wait(E, SEM["stg0"], ("stg", 2 * NCH - 2))
                pr.wait(E, SEM["stg1"], ("stg", 2 * NCH - 1))
                barrier_end(E)

            def run_p2(block):
                if block is None:
                    g2_vec(NULL); g2_scal(NULL); g2_tens(NULL)
                    g2_gp(NULL); g2_sync(NULL)
                else:
                    block.vector(lambda e: g2_vec(e))
                    block.scalar(lambda e: g2_scal(e))
                    block.tensor(lambda e: g2_tens(e))
                    block.gpsimd(lambda e: g2_gp(e))
                    block.sync(lambda e: g2_sync(e))

            if 2 in phases:
                phase(run_p2, 5)

        with ExitStack() as p3:
            hq_sb = p3.enter_context(nc.sbuf_tensor("hq_sb", [128, NCH * TG], BF16))
            w2buf = [p3.enter_context(nc.sbuf_tensor(f"w2b{j}", [128, 2, 1024], BF16))
                     for j in range(4)]
            gld = [p3.enter_context(nc.sbuf_tensor(f"gld{j}", [128, TG], F32))
                   for j in range(4)]
            accb = [p3.enter_context(nc.sbuf_tensor(f"accb{j}", [128, DM], F32))
                    for j in range(4)]
            outb = [p3.enter_context(nc.sbuf_tensor(f"outb{j}", [128, DM], F32))
                    for j in range(4)]
            q3f = p3.enter_context(nc.sbuf_tensor("q3f", [128, TG], F32))
            q3i = p3.enter_context(nc.sbuf_tensor("q3i", [128, TG], I32))
            p3ps = p3.enter_context(nc.psum_tensor("p3ps", [128, 4096], F32))

            def lin3(g, dh, hc):
                return (g * 2 + dh) * NCH + hc

            def unlin3(lin):
                g, r = divmod(lin, 2 * NCH)
                dh, hc = divmod(r, NCH)
                return (g, dh, hc)

            def g3_scal(E):
                barrier_wait(E)
                if pr.emit:
                    E.dma_start(out=c3col[:],
                                in_=scl_d.rearrange("(c p) -> p c", p=128),
                                allow_slow_non_contiguous=True
                                ).then_inc(SEM["ldc3"], 16)
                pr.inc("ldc3", 16)
                pr.mark("ldc3", "ldc3")
                for g in range(NG):
                    h, lt0, gt0 = g // 4, (g % 4) * TG, g * TG
                    for hc in range(NCH):
                        pr.wait(E, SEM["v3"],
                                ("hqv",) + ((g, hc - 4) if hc >= 4 else
                                            (g - 1, NCH + hc - 4)))
                        if pr.emit:
                            E.dma_start(out=gld[hc % 4][:],
                                        in_=g_d[h, hc][:, lt0:lt0 + TG]
                                        ).then_inc(SEM[f"ldg{hc % 4}"], 16)
                        pr.inc(f"ldg{hc % 4}", 16)
                        pr.mark(("ldg", g, hc), f"ldg{hc % 4}")
                    for tc in range(4):
                        pr.wait(E, SEM["v3"], ("outv", g - 1, tc))
                        if pr.emit:
                            E.dma_start(out=accb[tc][:],
                                        in_=acc_d[gt0 + tc * 128:
                                                  gt0 + (tc + 1) * 128, :]
                                        ).then_inc(SEM["ldacc"], 16)
                        pr.inc("ldacc", 16)
                        pr.mark(("ldacc", g, tc), "ldacc")
                barrier_end(E)

            def g3_sync(E):
                barrier_wait(E)
                for g in range(NG):
                    h, lt0, gt0 = g // 4, (g % 4) * TG, g * TG
                    for dh in range(2):
                        for hcp in range(NCH // 2):
                            li = (g * 2 + dh) * (NCH // 2) + hcp - 4
                            if li >= 0:
                                gg_, r_ = divmod(li, 2 * (NCH // 2))
                                dh_, hcp_ = divmod(r_, NCH // 2)
                                pr.wait(E, SEM["t3"], ("mm3", gg_, dh_, 2 * hcp_ + 1))
                            if pr.emit:
                                E.dma_start(
                                    out=w2buf[hcp % 4][:],
                                    in_=w2q_d[hcp * 256:(hcp + 1) * 256,
                                              dh * 1024:(dh + 1) * 1024]
                                    .rearrange("(c p) d -> p c d", p=128)
                                    ).then_inc(SEM[f"ldw2_{hcp % 4}"], 16)
                            pr.inc(f"ldw2_{hcp % 4}", 16)
                            pr.mark(("ldw2", g, dh, hcp), f"ldw2_{hcp % 4}")
                barrier_end(E)

            def _evac(E, g, dh):
                pr.wait(E, SEM["t3"], ("mm3", g, dh, NCH - 1))
                pr.wait(E, SEM["ldc3"], "ldc3")
                for tc in range(4):
                    pr.wait(E, SEM["ldacc"], ("ldacc", g, 3))
                    pr.wait(E, SEM["stout"], ("stout", g - 1))
                    for dcn in range(2):
                        sl = slice(dh * 1024 + dcn * 512,
                                   dh * 1024 + (dcn + 1) * 512)
                        if pr.emit:
                            E.scalar_tensor_tensor(
                                out=outb[tc][:, sl],
                                in0=p3ps[:, tc * 1024 + dcn * 512:
                                         tc * 1024 + (dcn + 1) * 512],
                                scalar=c3col[:, g * 4 + tc:g * 4 + tc + 1],
                                in1=accb[tc][:, sl], op0=OP.mult, op1=OP.add
                                ).then_inc(SEM["v3"], 1)
                        pr.inc("v3", 1)
                    if dh == 1:
                        pr.mark(("outv", g, tc), "v3")
                pr.mark(("evac", g, dh), "v3")

            def g3_vec(E):
                barrier_wait(E)
                for g in range(NG):
                    h, lt0 = g // 4, (g % 4) * TG
                    if g >= 1:
                        _evac(E, g - 1, 0)
                    for hc in range(NCH):
                        pr.wait(E, SEM[f"ldg{hc % 4}"], ("ldg", g, hc))
                        pr.wait(E, SEM["t3"], ("mm3", g - 1, 1, hc))
                        if pr.emit:
                            E.tensor_mul(out=q3f[:], in0=gld[hc % 4][:],
                                         in1=s2bc[:, h * TH + lt0:
                                                  h * TH + lt0 + TG])
                            E.tensor_copy(out=q3i[:], in_=q3f[:])
                            E.tensor_scalar(out=hq_sb[:, hc * TG:(hc + 1) * TG],
                                            in0=q3i[:], scalar1=0.0, scalar2=None,
                                            op0=OP.add).then_inc(SEM["v3"], 4)
                        pr.inc("v3", 4)
                        pr.mark(("hqv", g, hc), "v3")
                    if g >= 1:
                        _evac(E, g - 1, 1)
                _evac(E, NG - 1, 0)
                _evac(E, NG - 1, 1)
                barrier_end(E)

            def g3_tens(E):
                barrier_wait(E)
                for g in range(NG):
                    for dh in range(2):
                        for hc in range(NCH):
                            hcp = hc // 2
                            pr.wait(E, SEM[f"ldw2_{hcp % 4}"], ("ldw2", g, dh, hcp))
                            if dh == 0:
                                pr.wait(E, SEM["v3"], ("hqv", g, hc))
                                if hc == 0:
                                    pr.wait(E, SEM["v3"], ("evac", g - 1, 1))
                            elif hc == 0:
                                pr.wait(E, SEM["v3"], ("evac", g, 0))
                            for tc in range(4):
                                for dcn in range(2):
                                    if pr.emit:
                                        mm = E.matmul(
                                            p3ps[:, tc * 1024 + dcn * 512:
                                                 tc * 1024 + (dcn + 1) * 512],
                                            hq_sb[:, hc * TG + tc * 128:
                                                  hc * TG + (tc + 1) * 128],
                                            w2buf[(hc // 2) % 4][:, hc % 2,
                                                  dcn * 512:(dcn + 1) * 512],
                                            start=(hc == 0), stop=(hc == NCH - 1),
                                            skip_group_check=True)
                                        if tc == 3 and dcn == 1:
                                            mm.then_inc(SEM["t3"], 1)
                            pr.inc("t3", 1)
                            pr.mark(("mm3", g, dh, hc), "t3")
                barrier_end(E)

            def g3_gp(E):
                barrier_wait(E)
                for g in range(NG):
                    gt0 = g * TG
                    for tc in range(4):
                        pr.wait(E, SEM["v3"], ("outv", g, tc))
                        if pr.emit:
                            E.dma_start(out=out_tm[gt0 + tc * 128:
                                                   gt0 + (tc + 1) * 128, :],
                                        in_=outb[tc][:]).then_inc(SEM["stout"], 16)
                        pr.inc("stout", 16)
                    pr.mark(("stout", g), "stout")
                pr.wait(E, SEM["stout"], ("stout", NG - 1))
                barrier_end(E)

            def run_p3(block):
                if block is None:
                    g3_vec(NULL); g3_tens(NULL); g3_gp(NULL)
                    g3_scal(NULL); g3_sync(NULL)
                else:
                    block.vector(lambda e: g3_vec(e))
                    block.tensor(lambda e: g3_tens(e))
                    block.gpsimd(lambda e: g3_gp(e))
                    block.scalar(lambda e: g3_scal(e))
                    block.sync(lambda e: g3_sync(e))

            if 3 in phases:
                phase(run_p3, 5)

    mybir.codegen_inst_isa_subclasses(nc)
    return nc


_NC_CACHE = []


def kernel(L0_attn_x, L0_attn_B, L0_attn_C, L0_attn_D, L0_ffn_w1, L0_ffn_w2):
    x = np.asarray(L0_attn_x, dtype=np.float32)
    B = np.asarray(L0_attn_B, dtype=np.float32)
    C = np.asarray(L0_attn_C, dtype=np.float32)
    D = np.asarray(L0_attn_D, dtype=np.float32)
    w1 = np.asarray(L0_ffn_w1, dtype=np.float32)
    w2 = np.asarray(L0_ffn_w2, dtype=np.float32)

    btf = np.ascontiguousarray(B.T)
    ctf = np.ascontiguousarray(C.T)
    w1t = np.ascontiguousarray(w1.T)
    w2t = np.ascontiguousarray(w2.T)
    eyem = np.eye(128, dtype=np.float32)

    if not _NC_CACHE:
        _NC_CACHE.append(build_program())
    nc = _NC_CACHE[0]

    in_maps = [dict(x_fm=np.ascontiguousarray(x[i].T),
                    btf=btf, ctf=ctf, dvec=D, w1t=w1t, w2t=w2t, eye=eyem)
               for i in range(N_CORES)]
    res = run_bass_kernel_spmd(nc, in_maps, core_ids=list(range(N_CORES)))
    return np.stack([np.asarray(r["out_tm"]) for r in res.results], axis=0)


# version 13
# speedup vs baseline: 1.1231x; 1.0067x over previous
import numpy as np
from contextlib import ExitStack

import concourse.bass as bass
import concourse.mybir as mybir
from concourse.bass_isa import ReduceOp
from concourse.library_config import attn as _q7lib
from concourse.bass_utils import run_bass_kernel_spmd

F32 = mybir.dt.float32
BF16 = mybir.dt.bfloat16
I32 = mybir.dt.int32
AF = mybir.ActivationFunctionType
OP = mybir.AluOpType
AX = mybir.AxisListType

TOK, DM, NS, HID = 4096, 2048, 1024, 8192
NCD, NCS, NCH = DM // 128, NS // 128, HID // 128
T1 = 512
NT1 = TOK // T1
TH = 2048
NTH = TOK // TH
P2TC = TH // 512
TG = 512
NG = TOK // TG
EPS = 1e-5
WN = float(DM * HID)

N_CORES = 8


class _NullInst:
    def then_inc(self, *a, **k):
        return self


class _NullEng:
    def __getattr__(self, k):
        return lambda *a, **kw: _NullInst()


NULL = _NullEng()


class Prog:

    def __init__(self):
        self.emit = False
        self.cnt = {}
        self.ck = {}

    def inc(self, name, n=1):
        self.cnt[name] = self.cnt.get(name, 0) + n

    def mark(self, key, name):
        v = self.cnt.get(name, 0)
        if self.emit:
            assert self.ck.get(key) == v, ("ck mismatch", key, self.ck.get(key), v)
        else:
            self.ck[key] = v

    def selfsync(self, eng, sem, name):
        if self.emit:
            eng.wait_ge(sem, self.cnt[name])

    def wait(self, eng, sem, key):
        v = self.ck.get(key, 0) if isinstance(key, tuple) or isinstance(key, str) else key
        if self.emit and v > 0:
            eng.wait_ge(sem, v)


def build_program(phases=(0, 1, 2, 3), dbg=False):
    nc = bass.Bass("TRN2", target_bir_lowering=False,
                   detect_race_conditions=False)

    x_fm = nc.dram_tensor("x_fm", [DM, TOK], F32, kind="ExternalInput").ap()
    btf = nc.dram_tensor("btf", [DM, NS], F32, kind="ExternalInput").ap()
    ctf = nc.dram_tensor("ctf", [NS, DM], F32, kind="ExternalInput").ap()
    dvec = nc.dram_tensor("dvec", [DM], F32, kind="ExternalInput").ap()
    w1t = nc.dram_tensor("w1t", [DM, HID], F32, kind="ExternalInput").ap()
    w2t = nc.dram_tensor("w2t", [HID, DM], F32, kind="ExternalInput").ap()
    eye = nc.dram_tensor("eye", [128, 128], F32, kind="ExternalInput").ap()
    out_tm = nc.dram_tensor("out_tm", [TOK, DM], F32, kind="ExternalOutput").ap()

    if dbg is True:
        dbg = ("btq_d", "ctq_d", "w1q_d", "w2q_d", "xq_d", "acc_d", "g_d", "scl_d")
    elif not dbg:
        dbg = ()
    def sk(n):
        return "ExternalOutput" if n in dbg else "Internal"
    btq_d = nc.dram_tensor("btq_d", [NCS, 128, NCD, 128], BF16, kind=sk("btq_d")).ap()
    ctq_d = nc.dram_tensor("ctq_d", [NCD, 128, NCS, 128], BF16, kind=sk("ctq_d")).ap()
    w1q_d = nc.dram_tensor("w1q_d", [NCH, 128, NCD, 128], BF16, kind=sk("w1q_d")).ap()
    w2q_d = nc.dram_tensor("w2q_d", [HID, DM], BF16, kind=sk("w2q_d")).ap()
    xq_d = nc.dram_tensor("xq_d", [DM, TOK], BF16, kind=sk("xq_d")).ap()
    acc_d = nc.dram_tensor("acc_d", [TOK, DM], F32, kind=sk("acc_d")).ap()
    g_d = nc.dram_tensor("g_d", [NTH, NCH, 128, TH], F32, kind=sk("g_d")).ap()
    scl_d = nc.dram_tensor("scl_d", [TOK], F32, kind=sk("scl_d")).ap()

    pr = Prog()
    PB = [0]

    with ExitStack() as es:
        def sem(name):
            return es.enter_context(nc.semaphore(name))

        c1_full = es.enter_context(nc.sbuf_tensor("c1_full", [128, TOK], F32))
        s2bc = es.enter_context(nc.sbuf_tensor("s2bc", [128, TOK], F32))
        c3col = es.enter_context(nc.sbuf_tensor("c3col", [128, 32], F32))
        sD = es.enter_context(nc.sbuf_tensor("sD", [128, NCD], F32))
        seye = es.enter_context(nc.sbuf_tensor("seye", [128, 128], F32))
        prow = es.enter_context(nc.sbuf_tensor("prow", [128, 64], F32))
        mw1 = prow[:, 0:1]
        mw2 = prow[:, 1:2]
        s1w = prow[:, 2:3]
        ns1w = prow[:, 3:4]
        acc1 = prow[:, 4:5]
        acc2 = prow[:, 5:6]
        redc = prow[:, 6:7]
        s2w = prow[:, 7:8]
        ns2w = prow[:, 8:9]
        mh2 = prow[:, 9:10]
        rsum = prow[:, 10:11]
        rtmp = prow[:, 11:12]
        m164 = prow[:, 12:13]
        m264 = prow[:, 13:14]
        rsum1 = prow[:, 14:15]
        racc = prow[:, 16:32]
        racc1 = prow[:, 32:48]
        estcol1 = prow[:, 48:49]
        estcol2 = prow[:, 49:50]

        SEM = {n: sem(n) for n in [
            "pb", "p0misc",
            "p0ld0", "p0ld1", "p0v", "p0g", "p0st0", "p0st1", "p0s",
            "ldx0", "ldx1", "ldbt0", "ldbt1", "ldct0", "ldct1", "stxq0", "stxq1",
            "stacc", "v1", "s1", "t1", "g1",
            "ldw1_0", "ldw1_1", "ldxq", "stg0", "stg1", "stscl",
            "v2", "s2", "t2", "g2",
            "ldg0", "ldg1", "ldg2", "ldg3",
            "ldw2_0", "ldw2_1", "ldw2_2", "ldw2_3", "ldacc", "ldc3", "stout",
            "v3", "t3",
        ]}

        def barrier_end(E):
            if pr.emit:
                E.nop().then_inc(SEM["pb"], 1)
            pr.inc("pb", 1)

        def barrier_wait(E):
            v = PB[0]
            if pr.emit and v > 0:
                E.wait_ge(SEM["pb"], v)

        def phase(run, n_engines):
            snap_cnt = dict(pr.cnt)
            snap_pb = PB[0]
            run(None)
            pr.cnt = dict(snap_cnt)
            PB[0] = snap_pb
            pr.emit = True
            with nc.Block() as blk:
                run(blk)
            pr.emit = False
            PB[0] += n_engines

        with ExitStack() as p0:
            stgA = p0.enter_context(nc.sbuf_tensor("stgA", [128, HID], F32))
            stgB = p0.enter_context(nc.sbuf_tensor("stgB", [128, HID], F32))
            wqA = p0.enter_context(nc.sbuf_tensor("wqA", [128, HID], BF16))
            wqB = p0.enter_context(nc.sbuf_tensor("wqB", [128, HID], BF16))
            sa0 = p0.enter_context(nc.sbuf_tensor("sa0", [128, HID // 2], F32))
            sb0 = p0.enter_context(nc.sbuf_tensor("sb0", [128, HID // 2], F32))
            sabs = p0.enter_context(nc.sbuf_tensor("sabs", [128, HID], F32))
            rseg = p0.enter_context(nc.sbuf_tensor("rseg", [128, 128], F32))
            stg0 = [stgA, stgB]
            wq0 = [wqA, wqB]

            fills = ([("bt", c) for c in range(NCD)]
                     + [("ct", c) for c in range(NCS)]
                     + [("w1e", 0)]
                     + [("w1r", c) for c in range(NCD)]
                     + [("w2e", 0)]
                     + [("w2r", c) for c in range(16)]
                     + [("w1q", c) for c in range(NCD)]
                     + [("w2q", c) for c in range(16)])
            store_kinds = ("bt", "ct", "w1q", "w2q")
            n_stores = sum(1 for k, _ in fills if k in store_kinds)

            def fill_cols(kind):
                return {"bt": NS, "ct": DM, "w1e": HID, "w1r": HID,
                        "w2e": HID, "w2r": HID, "w1q": HID,
                        "w2q": HID}[kind]

            def g0_sync(E):
                if pr.emit:
                    E.dma_start(out=sD[:], in_=dvec.rearrange("(c p) -> p c", p=128),
                                allow_slow_non_contiguous=True).then_inc(SEM["p0misc"], 16)
                    E.dma_start(out=seye[:], in_=eye).then_inc(SEM["p0misc"], 16)
                pr.inc("p0misc", 32)
                pr.mark("p0misc_all", "p0misc")
                for fi, (kind, c) in enumerate(fills):
                    if fi >= 2 and fills[fi - 2][0] in ("w1e", "w1r",
                                                         "w2e", "w2r"):
                        pr.wait(E, SEM["p0s"], ("p0fillR", fi - 2))
                    elif fi >= 2 and fills[fi - 2][0] in ("w1q", "w2q"):
                        pr.wait(E, SEM["p0s"], ("p0fillq", fi - 2))
                    else:
                        pr.wait(E, SEM["p0v"], ("p0fill", fi - 2))
                    ncols = fill_cols(kind)
                    if pr.emit:
                        if kind == "bt":
                            src = btf[c * 128:(c + 1) * 128, :]
                        elif kind == "ct":
                            src = ctf[c * 128:(c + 1) * 128, :]
                        elif kind in ("w1e", "w1r", "w1q"):
                            src = w1t[c * 128:(c + 1) * 128, :]
                        else:
                            src = (w2t[c * 512:(c + 1) * 512, :]
                                   .rearrange("(cc p) d -> p cc d", p=128))
                        dstp = stg0[fi % 2][:, 0:ncols]
                        if kind in ("w2e", "w2r", "w2q"):
                            dstp = dstp.rearrange("p (cc d) -> p cc d", cc=4)
                        E.dma_start(out=dstp, in_=src).then_inc(SEM[f"p0ld{fi % 2}"], 16)
                    pr.inc(f"p0ld{fi % 2}", 16)
                    pr.mark(("p0ld", fi), f"p0ld{fi % 2}")
                barrier_end(E)

            def g0_vec(E):
                si = 0
                vq = 0

                def _vrow(fn):
                    if pr.emit:
                        fn().then_inc(SEM["p0v"], 1)
                    pr.inc("p0v", 1)
                    pr.selfsync(E, SEM["p0v"], "p0v")
                for fi, (kind, c) in enumerate(fills):
                    pr.wait(E, SEM[f"p0ld{fi % 2}"], ("p0ld", fi))
                    stg = stg0[fi % 2]
                    if kind in ("bt", "ct"):
                        ncols = fill_cols(kind)
                        pr.wait(E, SEM[f"p0st{si % 2}"], ("p0st", si - 2))
                        if pr.emit:
                            E.tensor_copy(out=wq0[si % 2][:, 0:ncols],
                                          in_=stg[:, 0:ncols]).then_inc(SEM["p0v"], 1)
                        pr.inc("p0v", 1)
                        pr.mark(("p0ready", si), "p0v")
                        si += 1
                    elif kind in ("w1e", "w2e"):
                        continue
                    elif kind in ("w1r", "w2r"):
                        mat = 1 if kind == "w1r" else 2
                        m64 = m164 if mat == 1 else m264
                        mhe = acc1 if mat == 1 else mh2
                        estc = estcol1 if mat == 1 else estcol2
                        rc = racc1 if mat == 1 else racc
                        if c == 0:
                            pr.wait(E, SEM["p0g"], f"p0ge{mat}")
                            _vrow(lambda: E.tensor_scalar_mul(
                                out=mhe, in0=estc, scalar1=2.0 ** -20))
                            _vrow(lambda: E.tensor_scalar_mul(
                                out=m64, in0=estc, scalar1=2.0 ** -14))
                        for h in range(2):
                            hsl = slice(h * (HID // 2), (h + 1) * (HID // 2))
                            pr.wait(E, SEM["p0s"], ("p0abs", (mat, c, h)))
                            if pr.emit:
                                E.reduce_sum(out=rseg[:, h * 64:(h + 1) * 64],
                                             in_=sabs[:, hsl]
                                             .rearrange("p (t s) -> p t s",
                                                        s=64),
                                             axis=AX.X).then_inc(SEM["p0v"], 1)
                            pr.inc("p0v", 1)
                            pr.selfsync(E, SEM["p0v"], "p0v")
                            pr.mark(("p0resc", (mat, c, h)), "p0v")
                        _vrow(lambda: E.tensor_scalar(
                            out=rseg[:], in0=rseg[:], scalar1=m64,
                            scalar2=None, op0=OP.subtract))
                        _vrow(lambda: E.reduce_sum(out=redc, in_=rseg[:],
                                                   axis=AX.X))
                        _vrow(lambda: E.tensor_copy(out=rc[:, c:c + 1],
                                                    in_=redc))
                    else:
                        if kind == "w1q" and c == 0:
                            _vrow(lambda: E.reduce_sum(out=rsum1, in_=racc1[:],
                                                       axis=AX.X))
                            if pr.emit:
                                E.reduce_sum(out=rsum, in_=racc[:], axis=AX.X
                                             ).then_inc(SEM["p0v"], 1)
                            pr.inc("p0v", 1)
                            pr.selfsync(E, SEM["p0v"], "p0v")
                            pr.mark("p0rsum_pre", "p0v")
                            pr.wait(E, SEM["p0g"], "p0par3")

                            def _rowop(fn):
                                if pr.emit:
                                    fn().then_inc(SEM["p0v"], 1)
                                pr.inc("p0v", 1)
                                pr.selfsync(E, SEM["p0v"], "p0v")

                            _rowop(lambda: E.tensor_scalar(
                                out=mw2, in0=rsum, scalar1=1.0 / WN,
                                scalar2=mh2, op0=OP.mult, op1=OP.add))
                            _rowop(lambda: E.tensor_scalar_max(
                                out=mw2, in0=mw2, scalar1=EPS))
                            _rowop(lambda: E.tensor_scalar(
                                out=mw1, in0=rsum1, scalar1=1.0 / WN,
                                scalar2=acc1, op0=OP.mult, op1=OP.add))
                            _rowop(lambda: E.tensor_scalar_max(
                                out=mw1, in0=mw1, scalar1=EPS))
                            for mm_, ss_, nn_ in ((mw1, s1w, ns1w),
                                                  (mw2, s2w, ns2w)):
                                _rowop(lambda mm_=mm_, ss_=ss_:
                                       E.reciprocal(out=ss_, in_=mm_))
                                _rowop(lambda mm_=mm_, ss_=ss_:
                                       E.tensor_mul(out=rtmp, in0=mm_, in1=ss_))
                                _rowop(lambda: E.tensor_scalar(
                                    out=rtmp, in0=rtmp, scalar1=-1.0,
                                    scalar2=1.0, op0=OP.mult, op1=OP.add))
                                _rowop(lambda ss_=ss_:
                                       E.tensor_mul(out=rtmp, in0=rtmp, in1=ss_))
                                _rowop(lambda ss_=ss_:
                                       E.tensor_add(out=ss_, in0=ss_, in1=rtmp))
                                _rowop(lambda ss_=ss_, nn_=nn_:
                                       E.tensor_scalar_mul(out=nn_, in0=ss_,
                                                           scalar1=-1.0))
                        sw = s1w if kind == "w1q" else s2w
                        pr.wait(E, SEM[f"p0st{si % 2}"], ("p0st", si - 2))
                        for half in range(2):
                            u = 2 * vq + half
                            hs = slice(half * (HID // 2), (half + 1) * (HID // 2))
                            sabsh = sabs[:, hs]
                            sg = sa0 if half == 0 else sb0
                            pr.wait(E, SEM["p0s"], ("p0qs", u))
                            if pr.emit:
                                E.tensor_scalar(out=sabsh, in0=sabsh,
                                                scalar1=sw, scalar2=0.5,
                                                op0=OP.mult, op1=OP.is_gt)
                                E.tensor_mul(out=wq0[si % 2][:, hs],
                                             in0=sabsh, in1=sg[:]
                                             ).then_inc(SEM["p0v"], 2)
                            pr.inc("p0v", 2)
                            pr.mark(("p0qv", u), "p0v")
                        pr.mark(("p0ready", si), "p0v")
                        si += 1
                        vq += 1
                    pr.mark(("p0fill", fi), "p0v")
                barrier_end(E)

            def g0_gp(E):
                if pr.emit:
                    E.load_library(_q7lib)
                si = 0
                for fi, (kind, c) in enumerate(fills):
                    if kind == "w1r" and c == 0:
                        pr.wait(E, SEM["p0s"], "p0e1s")
                        if pr.emit:
                            E.partition_all_reduce(estcol1, estcol1, 128,
                                                   ReduceOp.add
                                                   ).then_inc(SEM["p0g"], 1)
                        pr.inc("p0g", 1)
                        pr.mark("p0ge1", "p0g")
                    if kind == "w2r" and c == 0:
                        pr.wait(E, SEM["p0s"], "p0e2s")
                        if pr.emit:
                            E.partition_all_reduce(estcol2, estcol2, 128,
                                                   ReduceOp.add
                                                   ).then_inc(SEM["p0g"], 1)
                        pr.inc("p0g", 1)
                        pr.mark("p0ge2", "p0g")
                    if kind == "w1q" and c == 0:
                        pr.wait(E, SEM["p0v"], "p0rsum_pre")
                        if pr.emit:
                            E.partition_all_reduce(rsum1, rsum1, 128,
                                                   ReduceOp.add)
                            E.partition_all_reduce(rsum, rsum, 128, ReduceOp.add
                                                   ).then_inc(SEM["p0g"], 2)
                        pr.inc("p0g", 2)
                        pr.mark("p0par3", "p0g")
                    if kind not in store_kinds:
                        continue
                    pr.wait(E, SEM["p0v"], ("p0ready", si))
                    src = wq0[si % 2]
                    if kind == "bt":
                        dsts = [(btq_d[:, :, c, :].transpose([1, 0, 2]),
                                 src[:, 0:NS].rearrange("p (n m) -> p n m", m=128))]
                    elif kind == "ct":
                        dsts = [(ctq_d[j * 8:(j + 1) * 8, :, c, :].transpose([1, 0, 2]),
                                 src[:, j * 1024:(j + 1) * 1024]
                                 .rearrange("p (d m) -> p d m", m=128))
                                for j in range(2)]
                    elif kind == "w1q":
                        dsts = [(w1q_d[j * 8:(j + 1) * 8, :, c, :].transpose([1, 0, 2]),
                                 src[:, j * 1024:(j + 1) * 1024]
                                 .rearrange("p (h m) -> p h m", m=128))
                                for j in range(8)]
                    else:
                        dsts = [(w2q_d[c * 512:(c + 1) * 512, :]
                                 .rearrange("(cc p) d -> p cc d", p=128),
                                 src[:, 0:HID].rearrange("p (cc d) -> p cc d", cc=4))]
                    for dst, s_ in dsts:
                        if pr.emit:
                            E.dma_start(out=dst, in_=s_).then_inc(SEM[f"p0st{si % 2}"], 16)
                        pr.inc(f"p0st{si % 2}", 16)
                    pr.mark(("p0st", si), f"p0st{si % 2}")
                    si += 1
                pr.wait(E, SEM["p0st0"], ("p0st", n_stores - 2))
                pr.wait(E, SEM["p0st1"], ("p0st", n_stores - 1))
                barrier_end(E)

            def g0_scal(E):
                nsc = 0
                sq = 0
                for fi, (kind, c) in enumerate(fills):
                    if kind in ("w1e", "w2e"):
                        mat = 1 if kind == "w1e" else 2
                        estc = estcol1 if mat == 1 else estcol2
                        pr.wait(E, SEM[f"p0ld{fi % 2}"], ("p0ld", fi))
                        if mat == 2:
                            pr.wait(E, SEM["p0v"], ("p0resc", (1, 15, 0)))
                            pr.wait(E, SEM["p0v"], ("p0resc", (1, 15, 1)))
                        if pr.emit:
                            E.activation(out=sabs[:], in_=stg0[fi % 2][:],
                                         func=AF.Abs, accum_out=estc
                                         ).then_inc(SEM["p0s"], 1)
                        pr.inc("p0s", 1)
                        pr.mark(f"p0e{mat}s", "p0s")
                        pr.mark(("p0fillR", fi), "p0s")
                    elif kind in ("w1r", "w2r"):
                        mat = 1 if kind == "w1r" else 2
                        pr.wait(E, SEM[f"p0ld{fi % 2}"], ("p0ld", fi))
                        for h in range(2):
                            hsl = slice(h * (HID // 2), (h + 1) * (HID // 2))
                            pr.wait(E, SEM["p0v"], ("p0resc", (mat, c - 1, h)))
                            if pr.emit:
                                E.activation(out=sabs[:, hsl],
                                             in_=stg0[fi % 2][:, hsl],
                                             func=AF.Abs
                                             ).then_inc(SEM["p0s"], 1)
                            pr.inc("p0s", 1)
                            pr.mark(("p0abs", (mat, c, h)), "p0s")
                        pr.mark(("p0fillR", fi), "p0s")
                    elif kind in ("w1q", "w2q"):
                        pr.wait(E, SEM[f"p0ld{fi % 2}"], ("p0ld", fi))
                        for half in range(2):
                            u = 2 * sq + half
                            hs = slice(half * (HID // 2),
                                       (half + 1) * (HID // 2))
                            sabsh = sabs[:, hs]
                            sg = sa0 if half == 0 else sb0
                            pr.wait(E, SEM["p0v"], ("p0qv", u - 2))
                            if u < 2:
                                pr.wait(E, SEM["p0v"],
                                        ("p0resc", (2, 15, u % 2)))
                            if pr.emit:
                                E.activation(out=sabsh,
                                             in_=stg0[fi % 2][:, hs],
                                             func=AF.Abs)
                                E.activation(out=sg[:],
                                             in_=stg0[fi % 2][:, hs],
                                             func=AF.Sign
                                             ).then_inc(SEM["p0s"], 2)
                            pr.inc("p0s", 2)
                            pr.mark(("p0qs", u), "p0s")
                        pr.mark(("p0fillq", fi), "p0s")
                        sq += 1
                barrier_end(E)

            def run_p0(block):
                if block is None:
                    g0_sync(NULL); g0_vec(NULL); g0_scal(NULL); g0_gp(NULL)
                else:
                    block.sync(lambda e: g0_sync(e))
                    block.vector(lambda e: g0_vec(e))
                    block.scalar(lambda e: g0_scal(e))
                    block.gpsimd(lambda e: g0_gp(e))

            if 0 in phases:
                phase(run_p0, 4)

        with ExitStack() as p1:
            sxb = [p1.enter_context(nc.sbuf_tensor(f"sx{j}", [128, NCD, T1], F32))
                   for j in range(2)]
            sxnb = p1.enter_context(nc.sbuf_tensor("sxnb", [128, NCD * T1], BF16))
            sxqd = [p1.enter_context(nc.sbuf_tensor(f"sxqd{j}", [128, T1], BF16))
                    for j in range(2)]
            acc_tm = [p1.enter_context(nc.sbuf_tensor(f"acctm{j}", [128, DM], F32))
                      for j in range(4)]
            btbuf = [p1.enter_context(nc.sbuf_tensor(f"btbuf{j}", [128, NCD * 128], BF16))
                     for j in range(2)]
            ctbuf = [p1.enter_context(nc.sbuf_tensor(f"ctbuf{j}", [128, NCS * 128], BF16))
                     for j in range(2)]
            st_bf = p1.enter_context(nc.sbuf_tensor("st_bf", [128, NCS * T1], BF16))
            accf = [p1.enter_context(nc.sbuf_tensor(f"accf{j}", [128, T1], F32))
                    for j in range(2)]
            rows = p1.enter_context(nc.sbuf_tensor("rows1", [128, 11, T1], F32))
            (r_sums, r_sumsq, r_xmax, r_nxmin, r_m0, r_m1, r_rstd0, r_rstd1,
             r_sxs, r_A, r_B) = [rows[:, j, :] for j in range(11)]
            r_C = r_B
            sqc = [p1.enter_context(nc.sbuf_tensor(f"sqc{j}", [128, T1], F32))
                   for j in range(2)]
            xnscr = [p1.enter_context(nc.sbuf_tensor(f"xnscr{j}", [128, T1], F32))
                     for j in range(2)]
            qfscr = p1.enter_context(nc.sbuf_tensor("qfscr", [128, T1], F32))
            qiscr = p1.enter_context(nc.sbuf_tensor("qiscr", [128, T1], I32))
            xnev = p1.enter_context(nc.sbuf_tensor("xnev", [128, T1], F32))
            psum1 = [p1.enter_context(nc.psum_tensor(f"ps1{j}", [128, T1], F32))
                     for j in range(2)]
            psum2 = [p1.enter_context(nc.psum_tensor(f"ps2{j}", [128, T1], F32))
                     for j in range(2)]
            psumtr_full = [p1.enter_context(nc.psum_tensor(f"pstr{j}", [128, 512], F32))
                           for j in range(2)]
            psumtr = [t[:, 0:128] for t in psumtr_full]

            def w2k(i, j, J):
                return (i, j) if j >= 0 else (i - 1, J + j)

            def _p1_xload(E, i):
                if i >= NT1:
                    return
                pr.wait(E, SEM["v1"], ("xdone", i - 2))
                for hh in range(2):
                    if pr.emit:
                        src = (x_fm.rearrange("(c p) t -> p c t", p=128)
                               [:, hh * 8:(hh + 1) * 8, i * T1:(i + 1) * T1])
                        E.dma_start(out=sxb[i % 2][:, hh * 8:(hh + 1) * 8, :],
                                    in_=src).then_inc(SEM[f"ldx{i % 2}"], 16)
                    pr.inc(f"ldx{i % 2}", 16)
                pr.mark(("ldx", i), f"ldx{i % 2}")

            def g1_sync(E):
                barrier_wait(E)
                _p1_xload(E, 0)
                _p1_xload(E, 1)
                for i in range(NT1):
                    for n in range(NCS):
                        pr.wait(E, SEM["t1"], ("mm1grp",) + w2k(i, n - 2, NCS))
                        if pr.emit:
                            E.dma_start(out=btbuf[n % 2][:],
                                        in_=btq_d[n].rearrange("p k m -> p (k m)")
                                        ).then_inc(SEM[f"ldbt{n % 2}"], 16)
                        pr.inc(f"ldbt{n % 2}", 16)
                        pr.mark(("ldbt", i, n), f"ldbt{n % 2}")
                    for dc in range(NCD):
                        pr.wait(E, SEM["t1"], ("mm2grp",) + w2k(i, dc - 2, NCD))
                        if pr.emit:
                            E.dma_start(out=ctbuf[dc % 2][:],
                                        in_=ctq_d[dc].rearrange("p s m -> p (s m)")
                                        ).then_inc(SEM[f"ldct{dc % 2}"], 16)
                        pr.inc(f"ldct{dc % 2}", 16)
                        pr.mark(("ldct", i, dc), f"ldct{dc % 2}")
                    _p1_xload(E, i + 2)
                barrier_end(E)

            def _v_front(E, i):
                sx = sxb[i % 2]
                r_m = r_m0 if i % 2 == 0 else r_m1
                r_rstd = r_rstd0 if i % 2 == 0 else r_rstd1
                pr.wait(E, SEM[f"ldx{i % 2}"], ("ldx", i))
                if pr.emit:
                    E.reduce_sum(out=r_sums, in_=sx[:].transpose([0, 2, 1]),
                                 axis=AX.X)
                    E.reduce_max(out=r_xmax, in_=sx[:].transpose([0, 2, 1]),
                                 axis=AX.X)
                    E.tensor_reduce(out=r_C, in_=sx[:].transpose([0, 2, 1]),
                                    axis=AX.X, op=OP.min)
                    E.tensor_scalar_mul(out=r_nxmin, in0=r_C, scalar1=-1.0
                                        ).then_inc(SEM["v1"], 4)
                pr.inc("v1", 4)
                for c in range(NCD):
                    pr.wait(E, SEM["s1"], ("sq", i, c))
                    if pr.emit:
                        if c == 0:
                            E.tensor_copy(out=r_sumsq, in_=sqc[0][:]
                                          ).then_inc(SEM["v1"], 1)
                        else:
                            E.tensor_add(out=r_sumsq, in0=r_sumsq, in1=sqc[c % 2][:]
                                         ).then_inc(SEM["v1"], 1)
                    pr.inc("v1", 1)
                    pr.mark(("sqdone", i, c), "v1")
                pr.mark(("stats", i), "v1")

            def _v_rest(E, i):
                sx = sxb[i % 2]
                r_m = r_m0 if i % 2 == 0 else r_m1
                r_rstd = r_rstd0 if i % 2 == 0 else r_rstd1
                pr.wait(E, SEM["g1"], ("par", i))
                if pr.emit:
                    E.tensor_scalar_mul(out=r_m, in0=r_sums, scalar1=1.0 / DM)
                    E.tensor_scalar_mul(out=r_A, in0=r_sumsq, scalar1=1.0 / DM)
                    E.tensor_mul(out=r_B, in0=r_m, in1=r_m)
                    E.tensor_sub(out=r_A, in0=r_A, in1=r_B)
                    E.tensor_scalar_add(out=r_A, in0=r_A, scalar1=EPS
                                        ).then_inc(SEM["v1"], 4)
                pr.inc("v1", 4)
                pr.mark(("var", i), "v1")
                pr.wait(E, SEM["s1"], ("sd", i))
                if pr.emit:
                    E.reciprocal(out=r_rstd, in_=r_B)
                    E.tensor_sub(out=r_xmax, in0=r_xmax, in1=r_m)
                    E.tensor_add(out=r_nxmin, in0=r_nxmin, in1=r_m)
                    E.tensor_max(out=r_xmax, in0=r_xmax, in1=r_nxmin)
                    E.tensor_mul(out=r_xmax, in0=r_xmax, in1=r_rstd)
                    E.tensor_scalar_max(out=r_xmax, in0=r_xmax, scalar1=EPS)
                    E.reciprocal(out=r_B, in_=r_xmax)
                    E.tensor_scalar_mul(out=r_sxs, in0=r_B, scalar1=127.0)
                    E.tensor_scalar(out=c1_full[:, i * T1:(i + 1) * T1],
                                    in0=r_xmax, scalar1=mw1, scalar2=1.0 / 127.0,
                                    op0=OP.mult, op1=OP.mult
                                    ).then_inc(SEM["v1"], 9)
                pr.inc("v1", 9)
                for c in range(NCD):
                    _k = w2k(i, c - 2, NCD)
                    pr.wait(E, SEM[f"stxq{c % 2}"], ("stxq",) + _k)
                    pr.wait(E, SEM["s1"], ("xnb",) + w2k(i, c - 2, NCD))
                    if pr.emit:
                        E.tensor_sub(out=xnscr[c % 2][:], in0=sx[:, c, :], in1=r_m)
                        E.tensor_mul(out=xnscr[c % 2][:], in0=xnscr[c % 2][:],
                                     in1=r_rstd).then_inc(SEM["v1"], 2)
                    pr.inc("v1", 2)
                    pr.mark(("xn", i, c), "v1")
                    if pr.emit:
                        E.tensor_mul(out=qfscr[:], in0=xnscr[c % 2][:], in1=r_sxs)
                        E.tensor_copy(out=qiscr[:], in_=qfscr[:])
                        E.tensor_scalar(out=sxqd[c % 2][:], in0=qiscr[:],
                                        scalar1=0.0, scalar2=None, op0=OP.add
                                        ).then_inc(SEM["v1"], 3)
                    pr.inc("v1", 3)
                    pr.mark(("xq", i, c), "v1")

            def _v_evac(E, i):
                sx = sxb[i % 2]
                r_m = r_m0 if i % 2 == 0 else r_m1
                r_rstd = r_rstd0 if i % 2 == 0 else r_rstd1
                for dc in range(NCD):
                    pr.wait(E, SEM["t1"], ("mm2grp", i, dc))
                    pr.wait(E, SEM["t1"], ("tr",) + w2k(i, dc - 2, NCD))
                    if pr.emit:
                        E.tensor_sub(out=xnev[:], in0=sx[:, dc, :], in1=r_m)
                        E.tensor_mul(out=xnev[:], in0=xnev[:], in1=r_rstd)
                        E.scalar_tensor_tensor(out=xnev[:], in0=xnev[:],
                                               scalar=sD[:, dc:dc + 1],
                                               in1=sx[:, dc, :],
                                               op0=OP.mult, op1=OP.add)
                        E.tensor_add(out=accf[dc % 2][:], in0=xnev[:],
                                     in1=psum2[dc % 2][:]).then_inc(SEM["v1"], 4)
                    pr.inc("v1", 4)
                    pr.mark(("accf", i, dc), "v1")
                pr.mark(("xdone", i), "v1")

            def g1_vec(E):
                barrier_wait(E)
                pr.wait(E, SEM["p0misc"], "p0misc_all")
                for i in range(NT1):
                    _v_front(E, i)
                    if i >= 1:
                        _v_evac(E, i - 1)
                    _v_rest(E, i)
                _v_evac(E, NT1 - 1)
                barrier_end(E)

            def _s_sq(E, i):
                sx = sxb[i % 2]
                for c in range(NCD):
                    pr.wait(E, SEM[f"ldx{i % 2}"], ("ldx", i))
                    pr.wait(E, SEM["v1"], ("sqdone",) + w2k(i, c - 2, NCD))
                    if pr.emit:
                        E.activation(out=sqc[c % 2][:], in_=sx[:, c, :],
                                     func=AF.Square).then_inc(SEM["s1"], 1)
                    pr.inc("s1", 1)
                    pr.mark(("sq", i, c), "s1")

            def _s_sd(E, i):
                pr.wait(E, SEM["v1"], ("var", i))
                if pr.emit:
                    E.activation(out=r_B, in_=r_A, func=AF.Sqrt
                                 ).then_inc(SEM["s1"], 1)
                pr.inc("s1", 1)
                pr.mark(("sd", i), "s1")

            def _s_tanh(E, i):
                for n in range(NCS):
                    pr.wait(E, SEM["t1"], ("mm1grp", i, n))
                    if n == 0:
                        pr.wait(E, SEM["t1"], ("mm2grp", i - 1, NCD - 1))
                    if pr.emit:
                        E.activation(out=st_bf[:, n * T1:(n + 1) * T1],
                                     in_=psum1[n % 2][:], func=AF.Tanh
                                     ).then_inc(SEM["s1"], 1)
                    pr.inc("s1", 1)
                    pr.mark(("tanh", i, n), "s1")

            def _s_xnb(E, i):
                for c in range(NCD):
                    pr.wait(E, SEM["v1"], ("xn", i, c))
                    if c == 0:
                        pr.wait(E, SEM["t1"], ("mm1grp", i - 1, NCS - 1))
                    if pr.emit:
                        E.activation(out=sxnb[:, c * T1:(c + 1) * T1],
                                     in_=xnscr[c % 2][:], func=AF.Copy
                                     ).then_inc(SEM["s1"], 1)
                    pr.inc("s1", 1)
                    pr.mark(("xnb", i, c), "s1")

            def _s_evacs(E, i):
                pr.wait(E, SEM["stacc"], ("stacc", i - 1))
                for dc in range(NCD):
                    for tc in range(4):
                        lin = (i * NCD + dc) * 4 + tc
                        pr.wait(E, SEM["t1"], ("trtc", lin))
                        if pr.emit:
                            E.copy(out=acc_tm[tc][:, dc * 128:(dc + 1) * 128],
                                   in_=psumtr[tc % 2]).then_inc(SEM["s1"], 1)
                        pr.inc("s1", 1)
                        pr.mark(("sevtc", lin), "s1")
                pr.mark(("sevac_all", i), "s1")

            def g1_scal(E):
                barrier_wait(E)
                for i in range(NT1):
                    _s_sq(E, i)
                    if i >= 1:
                        _s_tanh(E, i - 1)
                        _s_evacs(E, i - 1)
                    _s_sd(E, i)
                    _s_xnb(E, i)
                _s_tanh(E, NT1 - 1)
                _s_evacs(E, NT1 - 1)
                barrier_end(E)

            def _tr_group(E, i, dc):
                pr.wait(E, SEM["v1"], ("accf", i, dc))
                for tc in range(4):
                    lin = (i * NCD + dc) * 4 + tc
                    pr.wait(E, SEM["s1"], ("sevtc", lin - 2))
                    if pr.emit:
                        E.transpose(psumtr[tc % 2],
                                    accf[dc % 2][:, tc * 128:(tc + 1) * 128],
                                    seye[:]).then_inc(SEM["t1"], 1)
                    pr.inc("t1", 1)
                    pr.mark(("trtc", lin), "t1")
                pr.mark(("tr", i, dc), "t1")

            def g1_tens(E):
                barrier_wait(E)
                pr.wait(E, SEM["p0misc"], "p0misc_all")
                for i in range(NT1):
                    pr.wait(E, SEM["s1"], ("xnb", i, NCD - 1))
                    for n in range(NCS):
                        pr.wait(E, SEM[f"ldbt{n % 2}"], ("ldbt", i, n))
                        pr.wait(E, SEM["s1"], ("tanh",) + w2k(i, n - 2, NCS))
                        for kc in range(NCD):
                            if pr.emit:
                                mm = E.matmul(psum1[n % 2][:],
                                              btbuf[n % 2][:, kc * 128:(kc + 1) * 128],
                                              sxnb[:, kc * T1:(kc + 1) * T1],
                                              start=(kc == 0), stop=(kc == NCD - 1))
                                if kc == NCD - 1:
                                    mm.then_inc(SEM["t1"], 1)
                        pr.inc("t1", 1)
                        pr.mark(("mm1grp", i, n), "t1")
                    pr.wait(E, SEM["s1"], ("tanh", i, NCS - 1))
                    for dc in range(NCD):
                        pr.wait(E, SEM[f"ldct{dc % 2}"], ("ldct", i, dc))
                        pr.wait(E, SEM["v1"], ("accf",) + w2k(i, dc - 2, NCD))
                        for sc in range(NCS):
                            if pr.emit:
                                mm = E.matmul(psum2[dc % 2][:],
                                              ctbuf[dc % 2][:, sc * 128:(sc + 1) * 128],
                                              st_bf[:, sc * T1:(sc + 1) * T1],
                                              start=(sc == 0), stop=(sc == NCS - 1))
                                if sc == NCS - 1:
                                    mm.then_inc(SEM["t1"], 1)
                        pr.inc("t1", 1)
                        pr.mark(("mm2grp", i, dc), "t1")
                        if dc >= 1:
                            _tr_group(E, i, dc - 1)
                    _tr_group(E, i, NCD - 1)
                barrier_end(E)

            def g1_gp(E):
                barrier_wait(E)
                for i in range(NT1):
                    pr.wait(E, SEM["v1"], ("stats", i))
                    if pr.emit:
                        E.partition_all_reduce(rows[:, 0:2, :].rearrange("p a t -> p (a t)"),
                                               rows[:, 0:2, :].rearrange("p a t -> p (a t)"),
                                               128, ReduceOp.add)
                        E.partition_all_reduce(rows[:, 2:4, :].rearrange("p a t -> p (a t)"),
                                               rows[:, 2:4, :].rearrange("p a t -> p (a t)"),
                                               128, ReduceOp.max
                                               ).then_inc(SEM["g1"], 4)
                    pr.inc("g1", 4)
                    pr.mark(("par", i), "g1")
                    for c in range(NCD):
                        pr.wait(E, SEM["v1"], ("xq", i, c))
                        if pr.emit:
                            E.dma_start(out=xq_d[c * 128:(c + 1) * 128,
                                                 i * T1:(i + 1) * T1],
                                        in_=sxqd[c % 2][:]
                                        ).then_inc(SEM[f"stxq{c % 2}"], 16)
                        pr.inc(f"stxq{c % 2}", 16)
                        pr.mark(("stxq", i, c), f"stxq{c % 2}")
                    if i >= 1:
                        _gp_acc(E, i - 1)
                _gp_acc(E, NT1 - 1)
                pr.wait(E, SEM["stxq0"], ("stxq", NT1 - 1, NCD - 2))
                pr.wait(E, SEM["stxq1"], ("stxq", NT1 - 1, NCD - 1))
                pr.wait(E, SEM["stacc"], ("stacc", NT1 - 1))
                barrier_end(E)

            def _gp_acc(E, i):
                pr.wait(E, SEM["s1"], ("sevac_all", i))
                for tc in range(4):
                    if pr.emit:
                        E.dma_start(out=acc_d[i * T1 + tc * 128:
                                              i * T1 + (tc + 1) * 128, :],
                                    in_=acc_tm[tc][:]).then_inc(SEM["stacc"], 16)
                    pr.inc("stacc", 16)
                pr.mark(("stacc", i), "stacc")

            def run_p1(block):
                if block is None:
                    g1_vec(NULL); g1_scal(NULL); g1_tens(NULL)
                    g1_gp(NULL); g1_sync(NULL)
                else:
                    block.vector(lambda e: g1_vec(e))
                    block.scalar(lambda e: g1_scal(e))
                    block.tensor(lambda e: g1_tens(e))
                    block.gpsimd(lambda e: g1_gp(e))
                    block.sync(lambda e: g1_sync(e))

            if 1 in phases:
                phase(run_p1, 5)

        with ExitStack() as p2:
            sxqh = p2.enter_context(nc.sbuf_tensor("sxqh", [128, NCD, TH], BF16))
            w1buf = [p2.enter_context(nc.sbuf_tensor(f"w1b{j}", [128, NCD * 128], BF16))
                     for j in range(2)]
            gtmp = [p2.enter_context(nc.sbuf_tensor(f"gtmp{j}", [128, TH], F32))
                    for j in range(2)]
            gbuf = [p2.enter_context(nc.sbuf_tensor(f"gbuf{j}", [128, TH], F32))
                    for j in range(2)]
            gabs = [p2.enter_context(nc.sbuf_tensor(f"gabs{j}", [128, TH], F32))
                    for j in range(2)]
            macc = p2.enter_context(nc.sbuf_tensor("macc", [128, TH], F32))
            c3row = p2.enter_context(nc.sbuf_tensor("c3row", [128, TH], F32))
            p2ps = [p2.enter_context(nc.psum_tensor(f"p2ps{j}", [128, TH], F32))
                    for j in range(2)]

            def g2_sync(E):
                barrier_wait(E)
                for h in range(NTH):
                    pr.wait(E, SEM["t2"], ("mmh", h * NCH - 1))
                    for hh in range(2):
                        if pr.emit:
                            src = (xq_d[:, h * TH:(h + 1) * TH]
                                   .rearrange("(c p) t -> p c t", p=128)
                                   [:, hh * 8:(hh + 1) * 8, :])
                            E.dma_start(out=sxqh[:, hh * 8:(hh + 1) * 8, :], in_=src
                                        ).then_inc(SEM["ldxq"], 16)
                        pr.inc("ldxq", 16)
                    pr.mark(("ldxq", h), "ldxq")
                    for hc in range(NCH):
                        k = h * NCH + hc
                        pr.wait(E, SEM["t2"], ("mmh", k - 2))
                        if pr.emit:
                            E.dma_start(out=w1buf[k % 2][:],
                                        in_=w1q_d[hc].rearrange("p k m -> p (k m)")
                                        ).then_inc(SEM[f"ldw1_{k % 2}"], 16)
                        pr.inc(f"ldw1_{k % 2}", 16)
                        pr.mark(("ldw1", k), f"ldw1_{k % 2}")
                barrier_end(E)

            def g2_tens(E):
                barrier_wait(E)
                for h in range(NTH):
                    pr.wait(E, SEM["ldxq"], ("ldxq", h))
                    for hc in range(NCH):
                        k = h * NCH + hc
                        pr.wait(E, SEM[f"ldw1_{k % 2}"], ("ldw1", k))
                        pr.wait(E, SEM["v2"], ("evacmul", k - 2))
                        for kc in range(NCD):
                            for tc in range(P2TC):
                                if pr.emit:
                                    mm = E.matmul(
                                        p2ps[k % 2][:, tc * 512:(tc + 1) * 512],
                                        w1buf[k % 2][:, kc * 128:(kc + 1) * 128],
                                        sxqh[:, kc, tc * 512:(tc + 1) * 512],
                                        start=(kc == 0), stop=(kc == NCD - 1),
                                        skip_group_check=True)
                                    if kc == NCD - 1 and tc == P2TC - 1:
                                        mm.then_inc(SEM["t2"], 1)
                        pr.inc("t2", 1)
                        pr.mark(("mmh", k), "t2")
                barrier_end(E)

            def _macc(E, k):
                pr.wait(E, SEM["s2"], ("absd", k))
                if pr.emit:
                    E.tensor_max(out=macc[:], in0=macc[:], in1=gabs[k % 2][:]
                                 ).then_inc(SEM["v2"], 1)
                pr.inc("v2", 1)
                pr.mark(("macc", k), "v2")

            def g2_vec(E):
                barrier_wait(E)
                for h in range(NTH):
                    if pr.emit:
                        E.memset(macc[:], 0.0).then_inc(SEM["v2"], 1)
                    pr.inc("v2", 1)
                    for hc in range(NCH):
                        k = h * NCH + hc
                        pr.wait(E, SEM["t2"], ("mmh", k))
                        pr.wait(E, SEM["s2"], ("gelu", k - 2))
                        for tc in range(P2TC):
                            if pr.emit:
                                mm = E.tensor_mul(
                                    out=gtmp[k % 2][:, tc * 512:(tc + 1) * 512],
                                    in0=p2ps[k % 2][:, tc * 512:(tc + 1) * 512],
                                    in1=c1_full[:, h * TH + tc * 512:
                                                h * TH + (tc + 1) * 512])
                                if tc == P2TC - 1:
                                    mm.then_inc(SEM["v2"], 1)
                        pr.inc("v2", 1)
                        pr.mark(("evacmul", k), "v2")
                        if hc >= 1:
                            _macc(E, k - 1)
                    _macc(E, h * NCH + NCH - 1)
                    pr.mark(("maccdone", h), "v2")
                    pr.wait(E, SEM["g2"], ("parh", h))
                    if pr.emit:
                        E.tensor_scalar_max(out=macc[:], in0=macc[:], scalar1=EPS)
                        E.reciprocal(out=c3row[:], in_=macc[:])
                        E.tensor_scalar_mul(out=s2bc[:, h * TH:(h + 1) * TH],
                                            in0=c3row[:], scalar1=127.0)
                        E.tensor_scalar(out=c3row[:], in0=macc[:], scalar1=mw2,
                                        scalar2=1.0 / 127.0, op0=OP.mult,
                                        op1=OP.mult).then_inc(SEM["v2"], 4)
                    pr.inc("v2", 4)
                    pr.mark(("c3row", h), "v2")
                barrier_end(E)

            def g2_scal(E):
                barrier_wait(E)
                for h in range(NTH):
                    for hc in range(NCH):
                        k = h * NCH + hc
                        pr.wait(E, SEM["v2"], ("evacmul", k))
                        pr.wait(E, SEM[f"stg{k % 2}"], ("stg", k - 2))
                        if pr.emit:
                            E.activation(out=gbuf[k % 2][:], in_=gtmp[k % 2][:],
                                         func=AF.Gelu).then_inc(SEM["s2"], 1)
                        pr.inc("s2", 1)
                        pr.mark(("gelu", k), "s2")
                        pr.wait(E, SEM["v2"], ("macc", k - 2))
                        if pr.emit:
                            E.activation(out=gabs[k % 2][:], in_=gbuf[k % 2][:],
                                         func=AF.Abs).then_inc(SEM["s2"], 1)
                        pr.inc("s2", 1)
                        pr.mark(("absd", k), "s2")
                barrier_end(E)

            def g2_gp(E):
                barrier_wait(E)
                for h in range(NTH):
                    for hc in range(NCH):
                        k = h * NCH + hc
                        pr.wait(E, SEM["s2"], ("gelu", k))
                        if pr.emit:
                            E.dma_start(out=g_d[h, hc], in_=gbuf[k % 2][:]
                                        ).then_inc(SEM[f"stg{k % 2}"], 16)
                        pr.inc(f"stg{k % 2}", 16)
                        pr.mark(("stg", k), f"stg{k % 2}")
                    pr.wait(E, SEM["v2"], ("maccdone", h))
                    if pr.emit:
                        E.partition_all_reduce(macc[:], macc[:], 128, ReduceOp.absmax
                                               ).then_inc(SEM["g2"], 1)
                    pr.inc("g2", 1)
                    pr.mark(("parh", h), "g2")
                    pr.wait(E, SEM["v2"], ("c3row", h))
                    if pr.emit:
                        E.dma_start(out=scl_d[h * TH:(h + 1) * TH],
                                    in_=c3row[0:1, :]).then_inc(SEM["stscl"], 16)
                    pr.inc("stscl", 16)
                pr.mark("stscl_all", "stscl")
                pr.wait(E, SEM["stscl"], "stscl_all")
                pr.wait(E, SEM["stg0"], ("stg", 2 * NCH - 2))
                pr.wait(E, SEM["stg1"], ("stg", 2 * NCH - 1))
                barrier_end(E)

            def run_p2(block):
                if block is None:
                    g2_vec(NULL); g2_scal(NULL); g2_tens(NULL)
                    g2_gp(NULL); g2_sync(NULL)
                else:
                    block.vector(lambda e: g2_vec(e))
                    block.scalar(lambda e: g2_scal(e))
                    block.tensor(lambda e: g2_tens(e))
                    block.gpsimd(lambda e: g2_gp(e))
                    block.sync(lambda e: g2_sync(e))

            if 2 in phases:
                phase(run_p2, 5)

        with ExitStack() as p3:
            hq_sb = p3.enter_context(nc.sbuf_tensor("hq_sb", [128, NCH * TG], BF16))
            w2buf = [p3.enter_context(nc.sbuf_tensor(f"w2b{j}", [128, 2, 1024], BF16))
                     for j in range(4)]
            gld = [p3.enter_context(nc.sbuf_tensor(f"gld{j}", [128, TG], F32))
                   for j in range(4)]
            accb = [p3.enter_context(nc.sbuf_tensor(f"accb{j}", [128, DM], F32))
                    for j in range(4)]
            outb = [p3.enter_context(nc.sbuf_tensor(f"outb{j}", [128, DM], F32))
                    for j in range(4)]
            q3f = p3.enter_context(nc.sbuf_tensor("q3f", [128, TG], F32))
            q3i = p3.enter_context(nc.sbuf_tensor("q3i", [128, TG], I32))
            p3ps = p3.enter_context(nc.psum_tensor("p3ps", [128, 4096], F32))

            def lin3(g, dh, hc):
                return (g * 2 + dh) * NCH + hc

            def unlin3(lin):
                g, r = divmod(lin, 2 * NCH)
                dh, hc = divmod(r, NCH)
                return (g, dh, hc)

            def g3_scal(E):
                barrier_wait(E)
                if pr.emit:
                    E.dma_start(out=c3col[:],
                                in_=scl_d.rearrange("(c p) -> p c", p=128),
                                allow_slow_non_contiguous=True
                                ).then_inc(SEM["ldc3"], 16)
                pr.inc("ldc3", 16)
                pr.mark("ldc3", "ldc3")
                for g in range(NG):
                    h, lt0, gt0 = g // 4, (g % 4) * TG, g * TG
                    for hc in range(NCH):
                        pr.wait(E, SEM["v3"],
                                ("hqv",) + ((g, hc - 4) if hc >= 4 else
                                            (g - 1, NCH + hc - 4)))
                        if pr.emit:
                            E.dma_start(out=gld[hc % 4][:],
                                        in_=g_d[h, hc][:, lt0:lt0 + TG]
                                        ).then_inc(SEM[f"ldg{hc % 4}"], 16)
                        pr.inc(f"ldg{hc % 4}", 16)
                        pr.mark(("ldg", g, hc), f"ldg{hc % 4}")
                    for tc in range(4):
                        pr.wait(E, SEM["v3"], ("outv", g - 1, tc))
                        if pr.emit:
                            E.dma_start(out=accb[tc][:],
                                        in_=acc_d[gt0 + tc * 128:
                                                  gt0 + (tc + 1) * 128, :]
                                        ).then_inc(SEM["ldacc"], 16)
                        pr.inc("ldacc", 16)
                        pr.mark(("ldacc", g, tc), "ldacc")
                barrier_end(E)

            def g3_sync(E):
                barrier_wait(E)
                for g in range(NG):
                    h, lt0, gt0 = g // 4, (g % 4) * TG, g * TG
                    for dh in range(2):
                        for hcp in range(NCH // 2):
                            li = (g * 2 + dh) * (NCH // 2) + hcp - 4
                            if li >= 0:
                                gg_, r_ = divmod(li, 2 * (NCH // 2))
                                dh_, hcp_ = divmod(r_, NCH // 2)
                                pr.wait(E, SEM["t3"], ("mm3", gg_, dh_, 2 * hcp_ + 1))
                            if pr.emit:
                                E.dma_start(
                                    out=w2buf[hcp % 4][:],
                                    in_=w2q_d[hcp * 256:(hcp + 1) * 256,
                                              dh * 1024:(dh + 1) * 1024]
                                    .rearrange("(c p) d -> p c d", p=128)
                                    ).then_inc(SEM[f"ldw2_{hcp % 4}"], 16)
                            pr.inc(f"ldw2_{hcp % 4}", 16)
                            pr.mark(("ldw2", g, dh, hcp), f"ldw2_{hcp % 4}")
                barrier_end(E)

            def _evac(E, g, dh):
                pr.wait(E, SEM["t3"], ("mm3", g, dh, NCH - 1))
                pr.wait(E, SEM["ldc3"], "ldc3")
                for tc in range(4):
                    pr.wait(E, SEM["ldacc"], ("ldacc", g, 3))
                    pr.wait(E, SEM["stout"], ("stout", g - 1))
                    for dcn in range(2):
                        sl = slice(dh * 1024 + dcn * 512,
                                   dh * 1024 + (dcn + 1) * 512)
                        if pr.emit:
                            E.scalar_tensor_tensor(
                                out=outb[tc][:, sl],
                                in0=p3ps[:, tc * 1024 + dcn * 512:
                                         tc * 1024 + (dcn + 1) * 512],
                                scalar=c3col[:, g * 4 + tc:g * 4 + tc + 1],
                                in1=accb[tc][:, sl], op0=OP.mult, op1=OP.add
                                ).then_inc(SEM["v3"], 1)
                        pr.inc("v3", 1)
                    if dh == 1:
                        pr.mark(("outv", g, tc), "v3")
                pr.mark(("evac", g, dh), "v3")

            def g3_vec(E):
                barrier_wait(E)
                for g in range(NG):
                    h, lt0 = g // 4, (g % 4) * TG
                    if g >= 1:
                        _evac(E, g - 1, 0)
                    for hc in range(NCH):
                        pr.wait(E, SEM[f"ldg{hc % 4}"], ("ldg", g, hc))
                        pr.wait(E, SEM["t3"], ("mm3", g - 1, 1, hc))
                        if pr.emit:
                            E.tensor_mul(out=q3f[:], in0=gld[hc % 4][:],
                                         in1=s2bc[:, h * TH + lt0:
                                                  h * TH + lt0 + TG])
                            E.tensor_copy(out=q3i[:], in_=q3f[:])
                            E.tensor_scalar(out=hq_sb[:, hc * TG:(hc + 1) * TG],
                                            in0=q3i[:], scalar1=0.0, scalar2=None,
                                            op0=OP.add).then_inc(SEM["v3"], 4)
                        pr.inc("v3", 4)
                        pr.mark(("hqv", g, hc), "v3")
                    if g >= 1:
                        _evac(E, g - 1, 1)
                _evac(E, NG - 1, 0)
                _evac(E, NG - 1, 1)
                barrier_end(E)

            def g3_tens(E):
                barrier_wait(E)
                for g in range(NG):
                    for dh in range(2):
                        for hc in range(NCH):
                            hcp = hc // 2
                            pr.wait(E, SEM[f"ldw2_{hcp % 4}"], ("ldw2", g, dh, hcp))
                            if dh == 0:
                                pr.wait(E, SEM["v3"], ("hqv", g, hc))
                                if hc == 0:
                                    pr.wait(E, SEM["v3"], ("evac", g - 1, 1))
                            elif hc == 0:
                                pr.wait(E, SEM["v3"], ("evac", g, 0))
                            for tc in range(4):
                                for dcn in range(2):
                                    if pr.emit:
                                        mm = E.matmul(
                                            p3ps[:, tc * 1024 + dcn * 512:
                                                 tc * 1024 + (dcn + 1) * 512],
                                            hq_sb[:, hc * TG + tc * 128:
                                                  hc * TG + (tc + 1) * 128],
                                            w2buf[(hc // 2) % 4][:, hc % 2,
                                                  dcn * 512:(dcn + 1) * 512],
                                            start=(hc == 0), stop=(hc == NCH - 1),
                                            skip_group_check=True)
                                        if tc == 3 and dcn == 1:
                                            mm.then_inc(SEM["t3"], 1)
                            pr.inc("t3", 1)
                            pr.mark(("mm3", g, dh, hc), "t3")
                barrier_end(E)

            def g3_gp(E):
                barrier_wait(E)
                for g in range(NG):
                    gt0 = g * TG
                    for tc in range(4):
                        pr.wait(E, SEM["v3"], ("outv", g, tc))
                        if pr.emit:
                            E.dma_start(out=out_tm[gt0 + tc * 128:
                                                   gt0 + (tc + 1) * 128, :],
                                        in_=outb[tc][:]).then_inc(SEM["stout"], 16)
                        pr.inc("stout", 16)
                    pr.mark(("stout", g), "stout")
                pr.wait(E, SEM["stout"], ("stout", NG - 1))
                barrier_end(E)

            def run_p3(block):
                if block is None:
                    g3_vec(NULL); g3_tens(NULL); g3_gp(NULL)
                    g3_scal(NULL); g3_sync(NULL)
                else:
                    block.vector(lambda e: g3_vec(e))
                    block.tensor(lambda e: g3_tens(e))
                    block.gpsimd(lambda e: g3_gp(e))
                    block.scalar(lambda e: g3_scal(e))
                    block.sync(lambda e: g3_sync(e))

            if 3 in phases:
                phase(run_p3, 5)

    mybir.codegen_inst_isa_subclasses(nc)
    return nc


_NC_CACHE = []


def kernel(L0_attn_x, L0_attn_B, L0_attn_C, L0_attn_D, L0_ffn_w1, L0_ffn_w2):
    x = np.asarray(L0_attn_x, dtype=np.float32)
    B = np.asarray(L0_attn_B, dtype=np.float32)
    C = np.asarray(L0_attn_C, dtype=np.float32)
    D = np.asarray(L0_attn_D, dtype=np.float32)
    w1 = np.asarray(L0_ffn_w1, dtype=np.float32)
    w2 = np.asarray(L0_ffn_w2, dtype=np.float32)

    btf = np.ascontiguousarray(B.T)
    ctf = np.ascontiguousarray(C.T)
    w1t = np.ascontiguousarray(w1.T)
    w2t = np.ascontiguousarray(w2.T)
    eyem = np.eye(128, dtype=np.float32)

    if not _NC_CACHE:
        _NC_CACHE.append(build_program())
    nc = _NC_CACHE[0]

    in_maps = [dict(x_fm=np.ascontiguousarray(x[i].T),
                    btf=btf, ctf=ctf, dvec=D, w1t=w1t, w2t=w2t, eye=eyem)
               for i in range(N_CORES)]
    res = run_bass_kernel_spmd(nc, in_maps, core_ids=list(range(N_CORES)))
    return np.stack([np.asarray(r["out_tm"]) for r in res.results], axis=0)
